# version 08
# baseline (speedup 1.0000x reference)
import numpy as np
import concourse.bass as bass
import concourse.mybir as mybir
from concourse.bass_utils import run_bass_kernel_spmd
from concourse.bass import AP

F32 = mybir.dt.float32
BF16 = mybir.dt.bfloat16
I32 = mybir.dt.int32
Alu = mybir.AluOpType
Act = mybir.ActivationFunctionType

NB = 2
L = 2048
D = 1024
DIN = 2048
NL = 4
NCH = 4
CH = 512
NKT = D // 128
NDT = DIN // 128
DTR = 64
GRP = 4
EPS = 1e-5
MAGIC = float(3 << 22)
NL_RUN = 4
SELF_SYNC = frozenset(["V"])
V_SYNC_REGIONS = frozenset(["prep", "c", "s2"])

PO_A = 0
PO_CW = PO_A + 128 * 256
PO_CB = PO_CW + 128 * 64
PO_DTB = PO_CB + 128 * 16
PO_DP = PO_DTB + 128 * 16
PO_NW = PO_DP + 128 * 16
PO_NB = PO_NW + 128 * 8
PO_XP = PO_NB + 128 * 8
PO_DTW = PO_XP + DIN * 96
PW = PO_DTW + DTR * DIN

O_LD0 = 0
O_LD1 = 2048
O_TMPA = 4096
O_TMPB = 6144
O_BIG0 = 8192
O_BIG1 = 12288
O_P = 16384
O_LNSQ = O_P + 20992
O_SROW = O_P + 25088
ARW = O_P + 30720


class Eng:

    def __init__(self, eng, sem, tag=""):
        self.e, self.sem, self.n = eng, sem, 0
        self.tag = tag
        self.region = "prep"

    @property
    def selfsync(self):
        if self.tag != "V":
            return self.tag in SELF_SYNC
        return self.region in V_SYNC_REGIONS

    def op(self, name, *a, waits=(), **kw):
        for s, v in waits:
            self.e.wait_ge(s, v)
        if self.selfsync and self.n > 0:
            self.e.wait_ge(self.sem, self.n)
        r = getattr(self.e, name)(*a, **kw)
        r.then_inc(self.sem, 1)
        self.n += 1
        return r

    def dma(self, out, in_, dsem, dcnt, waits=(), **kw):
        for s, v in waits:
            self.e.wait_ge(s, v)
        if self.selfsync and self.n > 0:
            self.e.wait_ge(self.sem, self.n)
        r = self.e.dma_start(out=out, in_=in_, **kw)
        r.then_inc(dsem, 16)
        dcnt[0] += 16
        return dcnt[0]

    def wait(self, sem, v):
        self.e.wait_ge(sem, v)

    def reset(self):
        self.n = 0


PHASE_MARKS = []


def _mark(nc, name):
    PHASE_MARKS.append((name, sum(1 for _ in nc.all_instructions())))


def build_program():
    nc = bass.Bass("TRN2", target_bir_lowering=False,
                   detect_race_conditions=False)

    t_hsT = nc.dram_tensor("hsT", [NB, D, L], F32, kind="ExternalInput").ap()
    t_cT = nc.dram_tensor("cT", [D, NB], F32, kind="ExternalInput").ap()
    t_wmod = nc.dram_tensor("wmod", [D, 6 * D], F32, kind="ExternalInput").ap()
    t_win = nc.dram_tensor("in_projT", [NL, D, 2 * DIN], F32, kind="ExternalInput").ap()
    t_wout = nc.dram_tensor("out_projT", [NL, DIN, D], F32, kind="ExternalInput").ap()
    t_lay = nc.dram_tensor("lay_params", [NL, PW], F32, kind="ExternalInput").ap()
    t_fw = nc.dram_tensor("fnorm_w8", [128, NKT], F32, kind="ExternalInput").ap()
    t_fb = nc.dram_tensor("fnorm_b8", [128, NKT], F32, kind="ExternalInput").ap()
    t_ffw = nc.dram_tensor("ffn_w", [D, D], F32, kind="ExternalInput").ap()
    t_ffb = nc.dram_tensor("ffn_b8", [128, NKT], F32, kind="ExternalInput").ap()

    t_out = nc.dram_tensor("outT", [NB, D, L], F32, kind="ExternalOutput").ap()

    t_winb = nc.dram_tensor("win_bf16", [NL, D, 2 * DIN], BF16, kind="Internal").ap()
    t_woutb = nc.dram_tensor("wout_bf16", [NL, DIN, D], BF16, kind="Internal").ap()
    t_R = nc.dram_tensor("R_s", [NB, NKT, 128, L], F32, kind="Internal").ap()
    t_xc = nc.dram_tensor("xc_s", [NB, NDT, 128, L], F32, kind="Internal").ap()
    t_z = nc.dram_tensor("z_s", [NB, NDT, 128, L], F32, kind="Internal").ap()
    t_dt = nc.dram_tensor("dt_s", [NB, NDT, 128, L], F32, kind="Internal").ap()
    t_g = nc.dram_tensor("g_s", [NB, NDT, 128, L], BF16, kind="Internal").ap()
    t_wsin = nc.dram_tensor("wsin", [D, 2 * DIN], BF16, kind="Internal").ap()
    t_wsout = nc.dram_tensor("wsout", [DIN, D], BF16, kind="Internal").ap()
    t_pstg = nc.dram_tensor("pstg", [PW], F32, kind="Internal").ap()

    import contextlib
    with contextlib.ExitStack() as _es:
        E = _es.enter_context
        AR = E(nc.sbuf_tensor("AR", [128, ARW], F32))
        ones1 = E(nc.sbuf_tensor("ones1", [128, 1], F32))
        ones1b = E(nc.sbuf_tensor("ones1b", [128, 1], BF16))
        cEPS = E(nc.sbuf_tensor("cEPS", [128, 1], F32))
        cNH = E(nc.sbuf_tensor("cNH", [128, 1], F32))
        onesr = E(nc.sbuf_tensor("onesr", [1, 128], F32))
        sel = E(nc.sbuf_tensor("sel", [16, 16, 128], BF16))
        modT = E(nc.sbuf_tensor("modT", [128, 6 * NKT, NB], F32))
        fw8 = E(nc.sbuf_tensor("fw8", [128, NKT], F32))
        fb8 = E(nc.sbuf_tensor("fb8", [128, NKT], F32))
        ffb8 = E(nc.sbuf_tensor("ffb8", [128, NKT], F32))
        idb = E(nc.sbuf_tensor("idb", [128, 128], BF16))
        idf = E(nc.sbuf_tensor("idf", [128, 128], F32))
        Abuf = E(nc.sbuf_tensor("Abuf", [128, 256], F32))
        cwb = E(nc.sbuf_tensor("cwb", [128, NDT * 4], F32))
        cbb = E(nc.sbuf_tensor("cbb", [128, NDT], F32))
        dtbb = E(nc.sbuf_tensor("dtbb", [128, NDT], F32))
        dpb = E(nc.sbuf_tensor("dpb", [128, NDT], F32))
        nw8 = E(nc.sbuf_tensor("nw8", [128, NKT], F32))
        nb8 = E(nc.sbuf_tensor("nb8", [128, NKT], F32))
        wstr = E(nc.sbuf_tensor("wstr", [128, 2, NDT, 128], BF16))
        q32 = E(nc.sbuf_tensor("q32", [128, 32], F32))
        q32b = E(nc.sbuf_tensor("q32b", [128, 32], F32))
        ps8 = E(nc.psum_tensor("ps8", [128, 8, 512], F32))
        sv = E(nc.semaphore("sv"))
        ss = E(nc.semaphore("ss"))
        st = E(nc.semaphore("st"))
        sg = E(nc.semaphore("sg"))
        sy = E(nc.semaphore("sy"))
        dw = E(nc.semaphore("dw"))
        dr = E(nc.semaphore("dr"))
        dsto = E(nc.semaphore("dsto"))
        dsg = E(nc.semaphore("dsg"))
        dst0 = E(nc.semaphore("dst0"))
        dst1 = E(nc.semaphore("dst1"))
        dst2 = E(nc.semaphore("dst2"))
        dst3 = E(nc.semaphore("dst3"))
        dwa = E(nc.semaphore("dwa"))
        dwb = E(nc.semaphore("dwb"))

        V = Eng(nc.vector, sv, "V")
        S = Eng(nc.scalar, ss, "S")
        T = Eng(nc.tensor, st, "T")
        G = Eng(nc.gpsimd, sg, "G")
        Y = Eng(nc.sync, sy, "Y")
        cw_ = [0]
        cr_ = [0]
        cs_ = [0]
        c0_ = [0]
        c1_ = [0]
        c2_ = [0]
        c3_ = [0]
        ca_ = [0]
        cb_2 = [0]

        ARB = AR.bitcast(BF16)

        def fv(off, n):
            return AR[:, off:off + n]

        def bv(off, nbf):
            return ARB[:, 2 * off:2 * off + nbf]

        ld0 = fv(O_LD0, L)
        ld1 = fv(O_LD1, L)
        tmpA = fv(O_TMPA, L)
        tmpB = fv(O_TMPB, L)
        big0 = fv(O_BIG0, 4096)
        big1 = fv(O_BIG1, 4096)
        big0k = big0.rearrange("p (k c) -> p k c", k=NKT)
        big1k = big1.rearrange("p (k c) -> p k c", k=NKT)
        dtin = AR[0:DTR, O_TMPB:O_TMPB + CH]
        xhat = bv(O_P, 2 * 16384).rearrange("p (b k l) -> p b k l", b=NB, k=NKT)
        xpre = fv(O_P + 16384, L + 3)
        cv = fv(O_P + 18436, L)
        xpT = fv(O_P, NDT * 96).rearrange("p (t m) -> p t m", t=NDT)
        dtT = AR[0:DTR, O_P + 1536:O_P + 1536 + DIN]
        BCrowb = ARB[0:16, 2 * O_BIG0:2 * O_BIG0 + 2 * NB * L].rearrange(
            "p (b v l) -> p b v l", b=NB, v=2)
        dt_g = fv(O_P, GRP * L).rearrange("p (g l) -> p g l", g=GRP)
        dtx_g = fv(O_P + 8192, GRP * L).rearrange("p (g l) -> p g l", g=GRP)
        y_g = fv(O_P + 16384, GRP * L).rearrange("p (g l) -> p g l", g=GRP)
        dAb = fv(O_P + 24576, L)
        ub = fv(O_P + 26624, L)
        hbb = fv(O_P + 28672, L)
        gst = bv(O_P, NDT * CH).rearrange("p (t c) -> p t c", t=NDT)
        woutres = bv(O_P + 4096, NDT * D).rearrange("p (k m) -> p k m", k=NDT)
        wdT = bv(O_P, NKT * D).rearrange("p (k m) -> p k m", k=NKT)
        qa = fv(O_P + 4096, D)
        qb = fv(O_P + 5120, D)
        qc = fv(O_P + 6144, D)
        qd = fv(O_P + 7168, D)
        qeb = bv(O_P + 8192, D)
        x1buf = fv(O_P + 8704, NKT * CH).rearrange("p (k c) -> p k c", k=NKT)
        h2T = fv(O_P + 12800, 4 * D).rearrange("p (t d) -> p t d", t=4)
        xdT = bv(O_P + 16896, NKT * CH).rearrange("p (k c) -> p k c", k=NKT)

        def bar():
            nc.all_engine_barrier()

        _mark(nc, 'prep0')
        V.op('memset', ones1[:], 1.0)
        V.op('memset', ones1b[:], 1.0)
        V.op('memset', cEPS[:], EPS)
        V.op('memset', cNH[:], -0.5)
        V.op('memset', onesr[:], 1.0)
        io128 = E(nc.sbuf_tensor("io128", [128, 128], I32))
        G.op('iota', io128[:], [[-1, 128]], base=0, channel_multiplier=1)
        V.op('tensor_scalar', out=idb[:], in0=io128[:], scalar1=0, scalar2=None,
             op0=Alu.is_equal, waits=[(sg, G.n)])
        V.op('tensor_scalar', out=idf[:], in0=io128[:], scalar1=0, scalar2=None,
             op0=Alu.is_equal)
        G.op('iota', io128[0:16, :], [[0, 128]], base=0, channel_multiplier=1,
             waits=[(sv, V.n)])
        for n in range(16):
            V.op('tensor_scalar', out=sel[:, n, :], in0=io128[0:16, :], scalar1=n,
                 scalar2=None, op0=Alu.is_equal, waits=[(sg, G.n)] if n == 0 else [])
        bar()

        _mark(nc, 'prep1')
        jobs = []
        for li0 in range(NL):
            for kt in range(NKT):
                jobs.append((t_win, t_winb, (li0 * D + kt * 128) * 2 * DIN, 2 * DIN))
        for li0 in range(NL):
            for kt in range(NDT):
                jobs.append((t_wout, t_woutb, (li0 * DIN + kt * 128) * D, D))
        bigs = [big0, big1]
        csem = [dw, dr]
        ccnt = [cw_, cr_]
        csto = [[0], [0]]
        for j, (src, dst, off, w) in enumerate(jobs):
            bb = j % 2
            dv = Y.dma(bigs[bb][:, 0:w], AP(src.tensor, off, [[w, 128], [1, w]]),
                       csem[bb], ccnt[bb],
                       waits=[(sv, V.n), ((dsto, dsg)[bb], csto[bb][0])] if j >= 2 else [])
            V.op('tensor_copy', out=bigs[bb].bitcast(BF16)[:, 0:w], in_=bigs[bb][:, 0:w],
                 waits=[(csem[bb], dv)])
            Y.dma(AP(dst.tensor, off, [[w, 128], [1, w]]), bigs[bb].bitcast(BF16)[:, 0:w],
                   (dsto, dsg)[bb], csto[bb], waits=[(sv, V.n)])
        G.wait(dsto, csto[0][0])
        G.wait(dsg, csto[1][0])
        cs_[0] = csto[0][0]
        bar()

        _mark(nc, 'prep2')
        dv = Y.dma(big0[:, 0:NKT * NB].rearrange("p (k b) -> p k b", k=NKT),
                   t_cT.rearrange("(k p) b -> p k b", p=128), dst0, c0_)
        S.op('activation', out=big0[:, 0:NKT * NB], in_=big0[:, 0:NKT * NB], func=Act.Silu,
             waits=[(dst0, c0_[0])])
        for jc in range(12):
            bank = jc % 2
            for kt in range(NKT):
                dv = Y.dma(tmpA[:, 0:512],
                           AP(t_wmod.tensor, kt * 128 * 6 * D + jc * 512,
                              [[6 * D, 128], [1, 512]]),
                           dr, cr_, waits=[(st, T.n)])
                T.op('matmul', out=ps8[0:NB, bank, :], lhsT=big0[:, 2 * kt:2 * kt + 2],
                     rhs=tmpA[:, 0:512], start=(kt == 0), stop=(kt == NKT - 1),
                     waits=[(dr, dv), (ss, S.n)])
            S.op('copy', out=AR[0:NB, O_P + jc * 512:O_P + (jc + 1) * 512],
                 in_=ps8[0:NB, bank, :], waits=[(st, T.n)])
        for blk in range(48):
            T.op('transpose', out=ps8[:, blk % 2, 0:NB],
                 in_=AR[0:NB, O_P + blk * 128:O_P + (blk + 1) * 128],
                 identity=idf[0:NB, 0:NB],
                 waits=[(ss, S.n), (sv, V.n)])
            V.op('tensor_copy', out=modT[:, blk, :], in_=ps8[:, blk % 2, 0:NB],
                 waits=[(st, T.n)])
        for v0 in (1, 4):
            V.op('tensor_scalar', out=modT[:, v0 * NKT:(v0 + 1) * NKT, :],
                 in0=modT[:, v0 * NKT:(v0 + 1) * NKT, :], scalar1=1.0, scalar2=None,
                 op0=Alu.add)
        for v0 in (2, 5):
            gsl = modT[:, v0 * NKT:(v0 + 1) * NKT, :]
            nn = NKT * NB
            V.op('tensor_scalar', out=big0[:, 0:nn], in0=gsl, scalar1=0.0, scalar2=None,
                 op0=Alu.is_ge)
            V.op('tensor_scalar', out=big0[:, 0:nn], in0=big0[:, 0:nn], scalar1=2.0,
                 scalar2=-1.0, op0=Alu.mult, op1=Alu.add)
            S.op('activation', out=big1[:, 0:nn], in_=gsl, func=Act.Abs, waits=[(sv, V.n)])
            V.op('tensor_scalar', out=big1[:, 0:nn], in0=big1[:, 0:nn], scalar1=0.25,
                 scalar2=None, op0=Alu.max, waits=[(ss, S.n)])
            V.op('tensor_tensor', out=gsl, in0=big0[:, 0:nn], in1=big1[:, 0:nn], op=Alu.mult)
        Y.dma(fw8[:], t_fw, dst0, c0_)
        Y.dma(fb8[:], t_fb, dst0, c0_)
        Y.dma(ffb8[:], t_ffb, dst0, c0_)
        Y.wait(dst0, c0_[0])
        bar()

        ln_sq_t = [0]

        def ln_chunk(xk, extra_wait=(), par=0):
            bs0, bs1 = (0, 1) if par == 0 else (4, 5)
            br0, br1 = (2, 3) if par == 0 else (6, 7)
            xkb = ARB[:, 2 * O_LNSQ:2 * O_LNSQ + NKT * CH].rearrange(
                "p (k c) -> p k c", k=NKT)
            sq = ARB[:, 2 * O_LNSQ + NKT * CH:2 * O_LNSQ + 2 * NKT * CH]
            sqk = sq.rearrange("p (k c) -> p k c", k=NKT)
            so = O_SROW + par * 1536
            mu = AR[0:1, so:so + CH]
            r_ = AR[0:1, so + CH:so + 2 * CH]
            tm = AR[0:1, so + 2 * CH:so + 3 * CH]
            S.op('copy', out=xkb.rearrange("p k c -> p (k c)"),
                 in_=xk.rearrange("p k c -> p (k c)"),
                 waits=list(extra_wait) + [(st, ln_sq_t[0])])
            S.op('activation', out=sq, in_=xk.rearrange("p k c -> p (k c)"),
                 func=Act.Square)
            for kt in range(NKT):
                T.op('matmul', out=ps8[0:1, bs0, :], lhsT=ones1b[:], rhs=xkb[:, kt, :],
                     start=(kt == 0), stop=(kt == NKT - 1),
                     waits=[(ss, S.n - 1)] if kt == 0 else [])
            for kt in range(NKT):
                T.op('matmul', out=ps8[0:1, bs1, :], lhsT=ones1b[:], rhs=sqk[:, kt, :],
                     start=(kt == 0), stop=(kt == NKT - 1),
                     waits=[(ss, S.n)] if kt == 0 else [])
            ln_sq_t[0] = T.n
            V.op('tensor_scalar', out=mu, in0=ps8[0:1, bs0, :], scalar1=1.0 / D, scalar2=None,
                 op0=Alu.mult, waits=[(st, T.n)])
            V.op('tensor_scalar', out=tm, in0=ps8[0:1, bs1, :], scalar1=1.0 / D, scalar2=None,
                 op0=Alu.mult)
            V.op('tensor_tensor', out=r_, in0=mu, in1=mu, op=Alu.mult)
            V.op('tensor_tensor', out=tm, in0=tm, in1=r_, op=Alu.subtract)
            S.op('activation', out=tm, in_=tm, func=Act.Ln, bias=cEPS[0:1, :],
                 waits=[(sv, V.n)])
            S.op('activation', out=r_, in_=tm, func=Act.Exp, scale=cNH[0:1, :])
            T.op('matmul', out=ps8[:, br0, :], lhsT=onesr[:], rhs=mu, start=True,
                 stop=True, waits=[(sv, V.n)])
            T.op('matmul', out=ps8[:, br1, :], lhsT=onesr[:], rhs=r_, start=True,
                 stop=True, waits=[(ss, S.n)])
            V.op('tensor_tensor', out=xk, in0=xk,
                 in1=ps8[:, br0, :].unsqueeze(1).broadcast_to([128, NKT, CH]),
                 op=Alu.subtract, waits=[(st, T.n)])
            V.op('tensor_tensor', out=xk, in0=xk,
                 in1=ps8[:, br1, :].unsqueeze(1).broadcast_to([128, NKT, CH]), op=Alu.mult)

        def r_chunk_ap(b, ch):
            return t_R[b].transpose([1, 0, 2])[:, :, ch * CH:(ch + 1) * CH]

        def hs_chunk_ap(b, ch):
            return t_hsT[b].rearrange("(k p) l -> p k l", p=128)[:, :, ch * CH:(ch + 1) * CH]

        _mark(nc, 'stage1')
        bigks = (big0k, big1k)
        st_sem = ((dst0, c0_), (dst1, c1_))
        vbuf = [0, 0]
        for idx, (b, ch) in enumerate([(b, ch) for b in range(NB) for ch in range(NCH)]):
            par = idx % 2
            xbk = bigks[par]
            psem, pcnt = st_sem[par]
            dv = Y.dma(xbk, hs_chunk_ap(b, ch), dr, cr_,
                       waits=[(sv, vbuf[par]), (psem, pcnt[0])])
            ln_chunk(xbk, extra_wait=[(dr, dv)], par=par)
            for kt in range(NKT):
                V.op('tensor_scalar', out=xbk[:, kt, :], in0=xbk[:, kt, :],
                     scalar1=modT[:, 1 * NKT + kt, b:b+1], scalar2=modT[:, 0 * NKT + kt, b:b+1],
                     op0=Alu.mult, op1=Alu.add)
            vbuf[par] = V.n
            Y.dma(r_chunk_ap(b, ch), xbk, psem, pcnt, waits=[(sv, V.n)])
        G.wait(dst0, c0_[0])
        G.wait(dst1, c1_[0])
        bar()

        _mark(nc, 'loop_head')
        loop_sems = (sv, ss, st, sg, sy, dw, dr, dsto, dsg, dst0, dst1, dst2, dst3, dwa, dwb)
        for sem_ in loop_sems:
            nc.sync.sem_clear(sem_)
        bar()
        with nc.Fori(0, NL_RUN) as li:
            for e in (V, S, T, G, Y):
                e.reset()
            for c in (cw_, cr_, cs_, c0_, c1_, c2_, c3_, ca_, cb_2):
                c[0] = 0
            ln_sq_t[0] = 0

            cg_ = [0]
            with nc.sync.register("lof") as lof:
                nc.sync.reg_mul(lof, li, PW)
                Y.dma(t_pstg[:], AP(t_lay.tensor, lof, [[1, PW]]), dsg, cg_)
                WQ = D * 2 * DIN // 4
                for j in range(4):
                    nc.sync.reg_mul(lof, li, D * 2 * DIN)
                    nc.sync.reg_add(lof, lof, j * WQ)
                    Y.dma(AP(t_wsin.tensor, j * WQ, [[1, WQ]]),
                          AP(t_winb.tensor, lof, [[1, WQ]]), dsg, cg_)
                WO = DIN * D // 2
                for j in range(2):
                    nc.sync.reg_mul(lof, li, DIN * D)
                    nc.sync.reg_add(lof, lof, j * WO)
                    Y.dma(AP(t_wsout.tensor, j * WO, [[1, WO]]),
                          AP(t_woutb.tensor, lof, [[1, WO]]), dsg, cg_)
            STAGE_DONE = cg_[0]
            Y.wait(dsg, STAGE_DONE)
            Y.dma(Abuf[:], AP(t_pstg.tensor, PO_A, [[256, 128], [1, 256]]), dw, cw_)
            Y.dma(cwb[:], AP(t_pstg.tensor, PO_CW, [[64, 128], [1, 64]]), dw, cw_)
            Y.dma(cbb[:], AP(t_pstg.tensor, PO_CB, [[16, 128], [1, 16]]), dw, cw_)
            Y.dma(dtbb[:], AP(t_pstg.tensor, PO_DTB, [[16, 128], [1, 16]]), dw, cw_)
            Y.dma(dpb[:], AP(t_pstg.tensor, PO_DP, [[16, 128], [1, 16]]), dw, cw_)
            Y.dma(nw8[:], AP(t_pstg.tensor, PO_NW, [[8, 128], [1, 8]]), dw, cw_)
            Y.dma(nb8[:], AP(t_pstg.tensor, PO_NB, [[8, 128], [1, 8]]), dw, cw_)
            S.op('activation', out=Abuf[:], in_=Abuf[:], func=Act.Exp, waits=[(dw, cw_[0])])
            V.op('tensor_scalar', out=Abuf[:], in0=Abuf[:], scalar1=-1.0, scalar2=None,
                 op0=Alu.mult, waits=[(ss, S.n)])
            bar()

            _mark(nc, 'a1')
            vbuf = [0, 0]
            for idx, (b, ch) in enumerate([(b, ch) for b in range(NB) for ch in range(NCH)]):
                par = idx % 2
                xbk = bigks[par]
                dv = Y.dma(xbk, r_chunk_ap(b, ch), dr, cr_,
                           waits=[(sv, vbuf[par])])
                ln_chunk(xbk, extra_wait=[(dr, dv)], par=par)
                for kt in range(NKT):
                    V.op('tensor_scalar', out=xhat[:, b, kt, ch * CH:(ch + 1) * CH],
                         in0=xbk[:, kt, :], scalar1=nw8[:, kt:kt + 1],
                         scalar2=nb8[:, kt:kt + 1], op0=Alu.mult, op1=Alu.add)
                vbuf[par] = V.n
            bar()

            V.region = "a2"
            _mark(nc, 'a2')
            V.op('memset', xpre[:, 0:3], 0.0)
            if True:
                last_store = [0, 0]
                t_after_mt = [0, 0]
                s_bank = [0] * 8
                for mt in range(2 * NDT):
                    wb = mt % 2
                    wsem, wcnt = ((dwa, ca_), (dwb, cb_2))[wb]
                    dvw = Y.dma(wstr[:, wb, 0:NKT, :],
                                AP(t_wsin.tensor, mt * 128,
                                   [[2 * DIN, 128], [128 * 2 * DIN, NKT], [1, 128]]),
                                wsem, wcnt, waits=[(st, t_after_mt[wb]), (dsg, STAGE_DONE)])
                    for kt in range(NKT):
                        for b in range(NB):
                            for ch in range(NCH):
                                w0 = []
                                if kt == 0:
                                    w0 = [(ss, s_bank[b * NCH + ch])]
                                    if b == 0 and ch == 0:
                                        w0.append((wsem, dvw))
                                T.op('matmul', out=ps8[:, b * NCH + ch, :],
                                     lhsT=wstr[:, wb, kt, :],
                                     rhs=xhat[:, b, kt, ch * CH:(ch + 1) * CH],
                                     start=(kt == 0), stop=(kt == NKT - 1), waits=w0)
                    t_after_mt[wb] = T.n
                    if mt < NDT:
                        for b in range(NB):
                            ob = (ld0, ld1)[b]
                            bsem, bcnt = ((dst0, c0_), (dst1, c1_))[b]
                            for ch in range(NCH):
                                S.op('copy', out=xpre[:, 3 + ch * CH:3 + (ch + 1) * CH],
                                     in_=ps8[:, b * NCH + ch, :],
                                     waits=[(st, T.n), (sv, V.n)] if ch == 0 else [])
                                s_bank[b * NCH + ch] = S.n
                            t4 = mt * 4
                            V.op('tensor_scalar', out=cv[:], in0=xpre[:, 0:L],
                                 scalar1=cwb[:, t4:t4 + 1], scalar2=None, op0=Alu.mult,
                                 waits=[(ss, S.n)])
                            for k in range(1, 4):
                                V.op('scalar_tensor_tensor', out=cv[:], in0=xpre[:, k:L + k],
                                     scalar=cwb[:, t4 + k:t4 + k + 1], in1=cv[:],
                                     op0=Alu.mult, op1=Alu.add)
                            S.op('activation', out=ob, in_=cv[:], func=Act.Silu,
                                 bias=cbb[:, mt:mt + 1],
                                 waits=[(sv, V.n), (bsem, bcnt[0])])
                            Y.dma(t_xc[b, mt], ob, bsem, bcnt, waits=[(ss, S.n)])
                    else:
                        for b in range(NB):
                            for ch in range(NCH):
                                S.op('copy', out=big1[:, ch * CH:(ch + 1) * CH],
                                     in_=ps8[:, b * NCH + ch, :],
                                     waits=([(st, T.n), (dst2, c2_[0])]
                                            if ch == 0 else []))
                                s_bank[b * NCH + ch] = S.n
                            Y.dma(t_z[b, mt - NDT], big1[:, 0:L], dst2, c2_,
                                  waits=[(ss, S.n)])
            G.wait(dst0, c0_[0])
            G.wait(dst1, c1_[0])
            G.wait(dst2, c2_[0])
            bar()

            V.region = "a3"
            _mark(nc, 'a3')
            dvx1 = Y.dma(xpT, AP(t_pstg.tensor, PO_XP,
                                 [[96, 128], [128 * 96, NDT], [1, 96]]), dw, cw_)
            dvx2 = Y.dma(dtT, AP(t_pstg.tensor, PO_DTW, [[DIN, DTR], [1, DIN]]), dw, cw_)
            xpTb = bv(O_P + 3584, NDT * 96).rearrange("p (t m) -> p t m", t=NDT)
            dtTb = ARB[0:DTR, 2 * (O_P + 4352):2 * (O_P + 4352) + DIN]
            V.op('tensor_copy', out=xpTb, in_=xpT, waits=[(dw, dvx2)])
            V.op('tensor_copy', out=dtTb, in_=dtT)
            dtinB = ARB[0:DTR, 2 * (O_TMPB + CH):2 * (O_TMPB + CH) + CH]
            lbB = (ARB[:, 2 * (O_LD0 + CH):2 * (O_LD0 + CH) + CH],
                   ARB[:, 2 * (O_LD1 + CH):2 * (O_LD1 + CH) + CH])
            for b in range(NB):
                for ch in range(NCH):
                    for kt in range(NDT):
                        lb = (ld0, ld1)[kt % 2]
                        dvx = Y.dma(lb[:, 0:CH], t_xc[b, kt, :, ch * CH:(ch + 1) * CH],
                                    dr, cr_, waits=[(st, T.n)])
                        S.op('copy', out=lbB[kt % 2], in_=lb[:, 0:CH],
                             waits=[(dr, dvx)])
                        T.op('matmul', out=ps8[0:DTR, 0, :], lhsT=xpTb[:, kt, 0:DTR],
                             rhs=lbB[kt % 2], start=(kt == 0), stop=(kt == NDT - 1),
                             waits=[(ss, S.n)] +
                                   ([(sv, V.n)] if kt == 0 else []))
                        T.op('matmul', out=ps8[0:16, 1, :], lhsT=xpTb[:, kt, DTR:DTR + 16],
                             rhs=lbB[kt % 2], start=(kt == 0), stop=(kt == NDT - 1))
                        T.op('matmul', out=ps8[0:16, 2, :], lhsT=xpTb[:, kt, DTR + 16:96],
                             rhs=lbB[kt % 2], start=(kt == 0), stop=(kt == NDT - 1))
                    S.op('copy', out=dtinB, in_=ps8[0:DTR, 0, :], waits=[(st, T.n)])
                    V.op('tensor_copy', out=BCrowb[:, b, 0, ch * CH:(ch + 1) * CH],
                         in_=ps8[0:16, 1, :], waits=[(st, T.n)])
                    V.op('tensor_copy', out=BCrowb[:, b, 1, ch * CH:(ch + 1) * CH],
                         in_=ps8[0:16, 2, :])
                    for mt in range(NDT):
                        hf = mt % 2
                        bk = 3 + hf
                        hsem, hcnt = ((dst0, c0_), (dst1, c1_))[hf]
                        T.op('matmul', out=ps8[:, bk, :],
                             lhsT=dtTb[:, mt * 128:(mt + 1) * 128], rhs=dtinB,
                             start=True, stop=True, waits=[(ss, S.n)])
                        S.op('copy', out=big1[:, hf * CH:(hf + 1) * CH],
                             in_=ps8[:, bk, :],
                             waits=[(st, T.n), (hsem, hcnt[0])])
                        Y.dma(t_dt[b, mt, :, ch * CH:(ch + 1) * CH],
                              big1[:, hf * CH:(hf + 1) * CH], hsem, hcnt,
                              waits=[(ss, S.n)])
            G.wait(dst0, c0_[0])
            G.wait(dst1, c1_[0])
            bar()

            V.region = "bprep"
            _mark(nc, 'b')
            BrepB = tmpA.bitcast(BF16)[:, 0:L]
            CrepB = tmpA.bitcast(BF16)[:, L:2 * L]
            for b in range(NB):
                for g0 in range(0, NDT, GRP):
                    V.region = "bprep"
                    xcls = []
                    for ti in range(GRP):
                        t = g0 + ti
                        Y.dma(dt_g[:, ti, :], t_dt[b, t], dr, cr_,
                              waits=[(sv, V.n), (ss, S.n), (sg, G.n), (dst2, c2_[0]),
                                     (dst3, c3_[0])] if ti == 0 else [])
                        xcl = (ld0, ld1, tmpB, tmpA)[ti]
                        xcls.append(xcl)
                        Y.dma(xcl, t_xc[b, t], dr, cr_)
                    GL = cr_[0]
                    for ti in range(GRP):
                        t = g0 + ti
                        xcl = xcls[ti]
                        V.op('tensor_scalar', out=dt_g[:, ti, :], in0=dt_g[:, ti, :],
                             scalar1=dtbb[:, t:t + 1], scalar2=None, op0=Alu.add,
                             waits=[(dr, GL)])
                        S.op('activation', out=dAb[:], in_=dt_g[:, ti, :], func=Act.Relu,
                             waits=[(sv, V.n)])
                        V.op('scalar_tensor_tensor', out=ub[:], in0=dAb[:], scalar=-2.0,
                             in1=dt_g[:, ti, :], op0=Alu.mult, op1=Alu.add,
                             waits=[(ss, S.n)])
                        S.op('activation', out=ub[:], in_=ub[:], func=Act.Exp,
                             waits=[(sv, V.n)])
                        S.op('activation', out=ub[:], in_=ub[:], func=Act.Ln, bias=ones1[:])
                        V.op('tensor_tensor', out=dt_g[:, ti, :], in0=dAb[:], in1=ub[:],
                             op=Alu.add, waits=[(ss, S.n)])
                        V.op('tensor_tensor', out=dtx_g[:, ti, :], in0=dt_g[:, ti, :],
                             in1=xcl, op=Alu.mult)
                    V.region = "bscan"
                    vread_rep = [V.n, V.n]
                    vread_dA = [V.n, V.n]
                    gread_rep = [G.n, G.n]
                    gdone = [G.n, G.n]
                    hg = (big1[:, 0:L], big1[:, L:2 * L])
                    gstep = 0
                    for n in range(16):
                        np_ = n % 2
                        repbuf = (tmpA, ld0)[np_]
                        Bp = repbuf.bitcast(BF16)[:, 0:L]
                        Cp = repbuf.bitcast(BF16)[:, L:2 * L]
                        for cc in range(4):
                            T.op('matmul', out=ps8[:, cc, :], lhsT=sel[:, n, :],
                                 rhs=BCrowb[:, b, 0, cc * CH:(cc + 1) * CH], start=True,
                                 stop=True,
                                 waits=[(sv, vread_rep[np_]), (sg, gread_rep[np_]),
                                        (ss, S.n)] if cc == 0 else [])
                            T.op('matmul', out=ps8[:, 4 + cc, :], lhsT=sel[:, n, :],
                                 rhs=BCrowb[:, b, 1, cc * CH:(cc + 1) * CH], start=True,
                                 stop=True)
                        for cc in range(4):
                            S.op('copy', out=Bp[:, cc * CH:(cc + 1) * CH],
                                 in_=ps8[:, cc, :],
                                 waits=[(st, T.n)] if cc == 0 else [])
                            S.op('copy', out=Cp[:, cc * CH:(cc + 1) * CH],
                                 in_=ps8[:, 4 + cc, :])
                        s_rep = S.n
                        for ti in range(GRP):
                            t = g0 + ti
                            tp = ti % 2
                            dab = (dAb, tmpB)[tp]
                            S.op('activation', out=dab[:], in_=dt_g[:, ti, :], func=Act.Exp,
                                 scale=Abuf[:, t * 16 + n:t * 16 + n + 1],
                                 waits=[(sv, vread_dA[tp])])
                            s_exp = S.n
                            V.op('tensor_tensor', out=ub[:], in0=dtx_g[:, ti, :], in1=Bp,
                                 op=Alu.mult, waits=[(ss, s_rep)])
                            if ti == 0:
                                V.op('tensor_tensor_scan', out=hbb[:], data0=dab[:],
                                     data1=ub[:], initial=0.0, op0=Alu.mult, op1=Alu.add,
                                     waits=[(ss, s_exp)])
                                vread_dA[tp] = V.n
                                if n == 0:
                                    V.op('tensor_tensor', out=y_g[:, ti, :], in0=hbb[:],
                                         in1=Cp, op=Alu.mult)
                                else:
                                    V.op('tensor_tensor', out=ub[:], in0=hbb[:], in1=Cp,
                                         op=Alu.mult)
                                    V.op('tensor_tensor', out=y_g[:, ti, :],
                                         in0=y_g[:, ti, :], in1=ub[:], op=Alu.add)
                                vread_rep[np_] = V.n
                            else:
                                hp = gstep % 2
                                gstep += 1
                                hb = hg[hp]
                                V.op('tensor_tensor_scan', out=hb, data0=dab[:],
                                     data1=ub[:], initial=0.0, op0=Alu.mult, op1=Alu.add,
                                     waits=[(ss, s_exp), (sg, gdone[hp])])
                                vread_dA[tp] = V.n
                                vread_rep[np_] = V.n
                                if n == 0:
                                    G.op('tensor_tensor', out=y_g[:, ti, :], in0=hb,
                                         in1=Cp, op=Alu.mult,
                                         waits=[(sv, V.n), (ss, s_rep)])
                                else:
                                    G.op('tensor_tensor', out=ld1, in0=hb, in1=Cp,
                                         op=Alu.mult, waits=[(sv, V.n), (ss, s_rep)])
                                    G.op('tensor_tensor', out=y_g[:, ti, :],
                                         in0=y_g[:, ti, :], in1=ld1, op=Alu.add)
                                gdone[hp] = G.n
                                gread_rep[np_] = G.n
                    V.region = "bgate"
                    for ti in range(GRP):
                        t = g0 + ti
                        dv1 = Y.dma(ld0, t_z[b, t], dr, cr_,
                                    waits=[(ss, S.n), (sv, V.n)] +
                                          ([(sg, G.n)] if ti == 0 else []))
                        dv2 = Y.dma(ld1, t_xc[b, t], dr, cr_)
                        gob = (dAb, hbb)[ti % 2]
                        gsem, gcnt = ((dst2, c2_), (dst3, c3_))[ti % 2]
                        S.op('activation', out=tmpB[:], in_=ld0, func=Act.Silu,
                             waits=[(dr, dv2), (sv, V.n)])
                        V.op('scalar_tensor_tensor', out=y_g[:, ti, :], in0=ld1,
                             scalar=dpb[:, t:t + 1], in1=y_g[:, ti, :], op0=Alu.mult,
                             op1=Alu.add, waits=[(dr, dv2), (ss, S.n)])
                        V.op('tensor_tensor', out=gob.bitcast(BF16)[:, 0:L],
                             in0=y_g[:, ti, :], in1=tmpB[:], op=Alu.mult,
                             waits=[(gsem, gcnt[0])])
                        Y.dma(t_g[b, t], gob.bitcast(BF16)[:, 0:L], gsem, gcnt,
                              waits=[(sv, V.n)])
            G.wait(dst2, c2_[0])
            G.wait(dst3, c3_[0])
            bar()

            V.region = "c"
            _mark(nc, 'c')
            if True:
                dvwo = Y.dma(woutres, AP(t_wsout.tensor, 0,
                                         [[D, 128], [128 * D, NDT], [1, D]]),
                             dwa, ca_, waits=[(dsg, STAGE_DONE), (st, T.n), (sv, V.n)])
                for b in range(NB):
                    for ch in range(NCH):
                        dvg = Y.dma(gst, t_g[b].transpose([1, 0, 2])[:, :, ch * CH:(ch + 1) * CH],
                                    dr, cr_, waits=[(st, T.n), (dsto, cs_[0])])
                        dvr = Y.dma(big0k, r_chunk_ap(b, ch), dr, cr_,
                                    waits=[(sv, V.n)])
                        dvg = dvr
                        v_mt = [0, 0]
                        for mt in range(NKT):
                            wb = mt % 2
                            for kt in range(NDT):
                                T.op('matmul', out=ps8[:, 6 + wb, :],
                                     lhsT=woutres[:, kt, mt * 128:(mt + 1) * 128],
                                     rhs=gst[:, kt, :], start=(kt == 0), stop=(kt == NDT - 1),
                                     waits=[(dwa, dvwo), (dr, dvg), (sv, v_mt[wb])]
                                     if kt == 0 else [])
                            V.op('tensor_tensor', out=big0k[:, mt, :], in0=big0k[:, mt, :],
                                 in1=ps8[:, 6 + wb, :], op=Alu.add,
                                 waits=[(st, T.n), (dr, dvr)])
                            v_mt[wb] = V.n
                        Y.dma(r_chunk_ap(b, ch), big0k, dsto, cs_, waits=[(sv, V.n)])
            G.wait(dsto, cs_[0])

            bar()
            for sem_ in loop_sems:
                nc.sync.sem_clear(sem_)
            bar()

        for e in (V, S, T, G, Y):
            e.reset()
        for c in (cw_, cr_, cs_, c0_, c1_, c2_, c3_, ca_, cb_2):
            c[0] = 0
        ln_sq_t[0] = 0

        V.region = "s2"
        _mark(nc, 'wquant')
        def quant_tile(x_ap, out_bf, width):
            nbk = width // 32
            xb = x_ap.rearrange("p (k c) -> p k c", c=32)
            V.op('tensor_reduce', out=q32[:, 0:nbk], in_=xb, axis=mybir.AxisListType.X,
                 op=Alu.max, apply_absolute_value=True)
            V.op('tensor_scalar', out=q32[:, 0:nbk], in0=q32[:, 0:nbk], scalar1=1e-35,
                 scalar2=None, op0=Alu.max)
            V.op('reciprocal', out=q32b[:, 0:nbk], in_=q32[:, 0:nbk])
            V.op('tensor_scalar', out=q32b[:, 0:nbk], in0=q32b[:, 0:nbk], scalar1=7.5,
                 scalar2=None, op0=Alu.mult)
            V.op('reciprocal', out=q32[:, 0:nbk], in_=q32b[:, 0:nbk])
            V.op('tensor_scalar', out=q32[:, 0:nbk], in0=q32[:, 0:nbk], scalar1=1e-30,
                 scalar2=None, op0=Alu.max)
            V.op('reciprocal', out=q32b[:, 0:nbk], in_=q32[:, 0:nbk])
            V.op('tensor_tensor', out=qa[:, 0:width].rearrange("p (k c) -> p k c", c=32),
                 in0=xb, in1=q32b[:, 0:nbk].unsqueeze(2).broadcast_to([128, nbk, 32]),
                 op=Alu.mult)
            V.op('tensor_tensor', out=qb[:, 0:width], in0=qa[:, 0:width],
                 in1=qa[:, 0:width], op=Alu.mult)
            V.op('tensor_scalar', out=qc[:, 0:width], in0=qb[:, 0:width], scalar1=4.0,
                 scalar2=None, op0=Alu.is_ge)
            V.op('tensor_scalar', out=qd[:, 0:width], in0=qb[:, 0:width], scalar1=16.0,
                 scalar2=None, op0=Alu.is_ge)
            V.op('tensor_scalar', out=qc[:, 0:width], in0=qc[:, 0:width], scalar1=0.125,
                 scalar2=0.125, op0=Alu.mult, op1=Alu.add)
            V.op('tensor_scalar', out=qd[:, 0:width], in0=qd[:, 0:width], scalar1=1.0,
                 scalar2=None, op0=Alu.add)
            V.op('tensor_tensor', out=qc[:, 0:width], in0=qc[:, 0:width],
                 in1=qd[:, 0:width], op=Alu.mult)
            V.op('reciprocal', out=qd[:, 0:width], in_=qc[:, 0:width])
            V.op('tensor_tensor', out=qb[:, 0:width], in0=qa[:, 0:width],
                 in1=qd[:, 0:width], op=Alu.mult)
            V.op('tensor_scalar', out=qb[:, 0:width], in0=qb[:, 0:width], scalar1=MAGIC,
                 scalar2=-MAGIC, op0=Alu.add, op1=Alu.add)
            V.op('tensor_tensor', out=qb[:, 0:width], in0=qb[:, 0:width],
                 in1=qc[:, 0:width], op=Alu.mult)
            V.op('tensor_tensor', out=out_bf.rearrange("p (k c) -> p k c", c=32),
                 in0=qb[:, 0:width].rearrange("p (k c) -> p k c", c=32),
                 in1=q32[:, 0:nbk].unsqueeze(2).broadcast_to([128, nbk, 32]), op=Alu.mult)

        for wt in range(NKT):
            dvq = Y.dma(big0[:, 0:D], t_ffw[wt * 128:(wt + 1) * 128, :], dr, cr_,
                        waits=[(sv, V.n), (st, T.n)])
            V.wait(dr, dvq)
            quant_tile(big0[:, 0:D], qeb, D)
            for kt in range(NKT):
                T.op('transpose', out=ps8.bitcast(BF16)[:, kt % 2, 0:128],
                     in_=qeb[:, kt * 128:(kt + 1) * 128], identity=idb[:],
                     waits=[(sv, V.n)])
                V.op('tensor_copy', out=wdT[:, kt, wt * 128:(wt + 1) * 128],
                     in_=ps8.bitcast(BF16)[:, kt % 2, 0:128], waits=[(st, T.n)])
        bar()

        _mark(nc, 'stage2')
        for b in range(NB):
            for ch in range(NCH):
                dv1 = Y.dma(big0k, r_chunk_ap(b, ch), dr, cr_,
                            waits=[(sv, V.n), (ss, S.n), (dsto, cs_[0])])
                dv2 = Y.dma(x1buf, hs_chunk_ap(b, ch), dr, cr_)
                ln_chunk(big0k, extra_wait=[(dr, dv2)])
                for kt in range(NKT):
                    V.op('tensor_scalar', out=big0k[:, kt, :], in0=big0k[:, kt, :],
                         scalar1=fw8[:, kt:kt + 1], scalar2=fb8[:, kt:kt + 1],
                         op0=Alu.mult, op1=Alu.add)
                for kt in range(NKT):
                    V.op('scalar_tensor_tensor', out=x1buf[:, kt, :], in0=big0k[:, kt, :],
                         scalar=modT[:, 2 * NKT + kt, b:b+1], in1=x1buf[:, kt, :],
                         op0=Alu.mult, op1=Alu.add,
                         waits=[(dr, dv2)] if kt == 0 else [])
                V.op('tensor_copy', out=big0[:, 0:NKT * CH],
                     in_=x1buf.rearrange("p k c -> p (k c)"))
                ln_chunk(big0k, extra_wait=[(sv, V.n)])
                for kt in range(NKT):
                    V.op('tensor_scalar', out=big0k[:, kt, :], in0=big0k[:, kt, :],
                         scalar1=modT[:, 4 * NKT + kt, b:b+1], scalar2=modT[:, 3 * NKT + kt, b:b+1],
                         op0=Alu.mult, op1=Alu.add)
                for tb in range(4):
                    for kt in range(NKT):
                        bk = 4 + ((tb * NKT + kt) % 2)
                        T.op('transpose', out=ps8[:, bk, 0:128],
                             in_=big0k[:, kt, tb * 128:(tb + 1) * 128], identity=idf[:],
                             waits=[(sv, V.n)])
                        V.op('tensor_copy', out=h2T[:, tb, kt * 128:(kt + 1) * 128],
                             in_=ps8[:, bk, 0:128], waits=[(st, T.n)])
                for tb in range(4):
                    quant_tile(h2T[:, tb, :], qeb, D)
                    for kt in range(NKT):
                        bk = 4 + ((tb * NKT + kt) % 2)
                        T.op('transpose', out=ps8.bitcast(BF16)[:, bk, 0:128],
                             in_=qeb[:, kt * 128:(kt + 1) * 128], identity=idb[:],
                             waits=[(sv, V.n)])
                        V.op('tensor_copy', out=xdT[:, kt, tb * 128:(tb + 1) * 128],
                             in_=ps8.bitcast(BF16)[:, bk, 0:128], waits=[(st, T.n)])
                for mt in range(NKT):
                    bk = mt % 2
                    for kt in range(NKT):
                        T.op('matmul', out=ps8[:, bk, :],
                             lhsT=wdT[:, kt, mt * 128:(mt + 1) * 128], rhs=xdT[:, kt, :],
                             start=(kt == 0), stop=(kt == NKT - 1),
                             waits=[(sv, V.n), (ss, S.n)] if kt == 0 else [])
                    S.op('activation', out=big0k[:, mt, :], in_=ps8[:, bk, :],
                         func=Act.Identity, bias=ffb8[:, mt:mt + 1], waits=[(st, T.n)])
                    V.op('scalar_tensor_tensor', out=big0k[:, mt, :], in0=big0k[:, mt, :],
                         scalar=modT[:, 5 * NKT + mt, b:b+1], in1=x1buf[:, mt, :],
                         op0=Alu.mult, op1=Alu.add, waits=[(ss, S.n)])
                Y.dma(t_out[b].rearrange("(k p) l -> p k l", p=128)
                      [:, :, ch * CH:(ch + 1) * CH], big0k, dsto, cs_,
                      waits=[(sv, V.n)])
        G.wait(dsto, cs_[0])
    _mark(nc, 'end')
    return nc


_PROG = None


def _get_prog():
    global _PROG
    if _PROG is None:
        _PROG = build_program()
    return _PROG


def _layout(inputs):
    f = np.ascontiguousarray
    hs = np.asarray(inputs["L0_attn_hidden_states"])
    c = np.asarray(inputs["L0_c"])

    def vecN(x):
        n = x.shape[-1] // 128
        return f(np.asarray(x, np.float32).reshape(x.shape[0], n, 128).transpose(0, 2, 1))

    def vec1(x):
        n = x.shape[-1] // 128
        return f(np.asarray(x, np.float32).reshape(n, 128).T)

    A_lay = (np.asarray(inputs["L0_attn_A_log"], np.float32)
             .reshape(NL, NDT, 128, 16).transpose(0, 2, 1, 3).reshape(NL, 128, 256))
    conv_w = (np.asarray(inputs["L0_attn_conv_w"], np.float32)
              .reshape(NL, NDT, 128, 4).transpose(0, 2, 1, 3).reshape(NL, 128, NDT * 4))
    lay = np.concatenate([
        A_lay.reshape(NL, -1),
        conv_w.reshape(NL, -1),
        vecN(inputs["L0_attn_conv_b"]).reshape(NL, -1),
        vecN(inputs["L0_attn_dt_proj_b"]).reshape(NL, -1),
        vecN(inputs["L0_attn_D_param"]).reshape(NL, -1),
        vecN(inputs["L0_attn_norm_w"]).reshape(NL, -1),
        vecN(inputs["L0_attn_norm_b"]).reshape(NL, -1),
        np.asarray(inputs["L0_attn_x_proj_w"], np.float32).transpose(0, 2, 1).reshape(NL, -1),
        np.asarray(inputs["L0_attn_dt_proj_w"], np.float32).transpose(0, 2, 1).reshape(NL, -1),
    ], axis=1)
    assert lay.shape[1] == PW, (lay.shape, PW)
    com = dict(
        wmod=f(np.asarray(inputs["L0_wmod"], np.float32)),
        in_projT=f(np.asarray(inputs["L0_attn_in_proj_w"], np.float32).transpose(0, 2, 1)),
        out_projT=f(np.asarray(inputs["L0_attn_out_proj_w"], np.float32).transpose(0, 2, 1)),
        lay_params=f(lay),
        fnorm_w8=vec1(inputs["L0_attn_fnorm_w"]),
        fnorm_b8=vec1(inputs["L0_attn_fnorm_b"]),
        ffn_w=f(np.asarray(inputs["L0_ffn_weight"], np.float32)),
        ffn_b8=vec1(inputs["L0_ffn_bias"]),
    )
    maps = []
    for core in range(8):
        b0 = core * NB
        m = dict(com)
        m["hsT"] = f(hs[b0:b0 + NB].transpose(0, 2, 1).astype(np.float32))
        m["cT"] = f(c[b0:b0 + NB].T.astype(np.float32))
        maps.append(m)
    return maps


def kernel(**inputs):
    nc = _get_prog()
    in_maps = _layout(inputs)
    res = run_bass_kernel_spmd(nc, in_maps, core_ids=list(range(8)))
    outs = [np.asarray(r["outT"]).transpose(0, 2, 1) for r in res.results]
    return np.ascontiguousarray(np.concatenate(outs, axis=0)).astype(np.float32)


# version 10
# speedup vs baseline: 1.2452x; 1.2452x over previous
import numpy as np
import concourse.bass as bass
import concourse.mybir as mybir
from concourse.bass_utils import run_bass_kernel_spmd
from concourse.bass import AP

F32 = mybir.dt.float32
BF16 = mybir.dt.bfloat16
I32 = mybir.dt.int32
Alu = mybir.AluOpType
Act = mybir.ActivationFunctionType

NB = 2
L = 2048
D = 1024
DIN = 2048
NL = 4
NCH = 4
CH = 512
NKT = D // 128
NDT = DIN // 128
DTR = 64
GRP = 4
EPS = 1e-5
MAGIC = float(3 << 22)
NL_RUN = 4
SELF_SYNC = frozenset(["V"])
V_SYNC_REGIONS = frozenset(["prep", "c", "s2"])

PO_A = 0
PO_CW = PO_A + 128 * 256
PO_CB = PO_CW + 128 * 64
PO_DTB = PO_CB + 128 * 16
PO_DP = PO_DTB + 128 * 16
PO_NW = PO_DP + 128 * 16
PO_NB = PO_NW + 128 * 8
PO_XP = PO_NB + 128 * 8
PO_DTW = PO_XP + DIN * 96
PW = PO_DTW + DTR * DIN

O_LD0 = 0
O_LD1 = 2048
O_TMPA = 4096
O_TMPB = 6144
O_BIG0 = 8192
O_BIG1 = 12288
O_P = 16384
O_LNSQ = O_P + 20992
O_SROW = O_P + 25088
ARW = O_P + 30720


class Eng:

    def __init__(self, eng, sem, tag=""):
        self.e, self.sem, self.n = eng, sem, 0
        self.tag = tag
        self.region = "prep"

    @property
    def selfsync(self):
        if self.tag != "V":
            return self.tag in SELF_SYNC
        return self.region in V_SYNC_REGIONS

    def op(self, name, *a, waits=(), **kw):
        for s, v in waits:
            self.e.wait_ge(s, v)
        if self.selfsync and self.n > 0:
            self.e.wait_ge(self.sem, self.n)
        r = getattr(self.e, name)(*a, **kw)
        r.then_inc(self.sem, 1)
        self.n += 1
        return r

    def dma(self, out, in_, dsem, dcnt, waits=(), **kw):
        for s, v in waits:
            self.e.wait_ge(s, v)
        if self.selfsync and self.n > 0:
            self.e.wait_ge(self.sem, self.n)
        r = self.e.dma_start(out=out, in_=in_, **kw)
        r.then_inc(dsem, 16)
        dcnt[0] += 16
        return dcnt[0]

    def wait(self, sem, v):
        self.e.wait_ge(sem, v)

    def reset(self):
        self.n = 0


PHASE_MARKS = []


def _mark(nc, name):
    PHASE_MARKS.append((name, sum(1 for _ in nc.all_instructions())))


def build_program():
    nc = bass.Bass("TRN2", target_bir_lowering=False,
                   detect_race_conditions=False)

    t_hsT = nc.dram_tensor("hsT", [NB, D, L], F32, kind="ExternalInput").ap()
    t_cT = nc.dram_tensor("cT", [D, NB], F32, kind="ExternalInput").ap()
    t_wmod = nc.dram_tensor("wmod", [D, 6 * D], F32, kind="ExternalInput").ap()
    t_win = nc.dram_tensor("in_projT", [NL, D, 2 * DIN], F32, kind="ExternalInput").ap()
    t_wout = nc.dram_tensor("out_projT", [NL, DIN, D], F32, kind="ExternalInput").ap()
    t_lay = nc.dram_tensor("lay_params", [NL, PW], F32, kind="ExternalInput").ap()
    t_fw = nc.dram_tensor("fnorm_w8", [128, NKT], F32, kind="ExternalInput").ap()
    t_fb = nc.dram_tensor("fnorm_b8", [128, NKT], F32, kind="ExternalInput").ap()
    t_ffw = nc.dram_tensor("ffn_w", [D, D], F32, kind="ExternalInput").ap()
    t_ffb = nc.dram_tensor("ffn_b8", [128, NKT], F32, kind="ExternalInput").ap()

    t_out = nc.dram_tensor("outT", [NB, D, L], F32, kind="ExternalOutput").ap()

    t_winb = nc.dram_tensor("win_bf16", [NL, D, 2 * DIN], BF16, kind="Internal").ap()
    t_woutb = nc.dram_tensor("wout_bf16", [NL, DIN, D], BF16, kind="Internal").ap()
    t_R = nc.dram_tensor("R_s", [NB, NKT, 128, L], F32, kind="Internal").ap()
    t_xc = nc.dram_tensor("xc_s", [NB, NDT, 128, L], F32, kind="Internal").ap()
    t_z = nc.dram_tensor("z_s", [NB, NDT, 128, L], F32, kind="Internal").ap()
    t_dt = nc.dram_tensor("dt_s", [NB, NDT, 128, L], F32, kind="Internal").ap()
    t_g = nc.dram_tensor("g_s", [NB, NDT, 128, L], BF16, kind="Internal").ap()
    t_wsin = nc.dram_tensor("wsin", [D, 2 * DIN], BF16, kind="Internal").ap()
    t_wsout = nc.dram_tensor("wsout", [DIN, D], BF16, kind="Internal").ap()
    t_pstg = nc.dram_tensor("pstg", [PW], F32, kind="Internal").ap()

    import contextlib
    with contextlib.ExitStack() as _es:
        E = _es.enter_context
        AR = E(nc.sbuf_tensor("AR", [128, ARW], F32))
        ones1 = E(nc.sbuf_tensor("ones1", [128, 1], F32))
        ones1b = E(nc.sbuf_tensor("ones1b", [128, 1], BF16))
        cEPS = E(nc.sbuf_tensor("cEPS", [128, 1], F32))
        cNH = E(nc.sbuf_tensor("cNH", [128, 1], F32))
        onesr = E(nc.sbuf_tensor("onesr", [1, 128], F32))
        sel = E(nc.sbuf_tensor("sel", [16, 16, 128], BF16))
        modT = E(nc.sbuf_tensor("modT", [128, 6 * NKT, NB], F32))
        fw8 = E(nc.sbuf_tensor("fw8", [128, NKT], F32))
        fb8 = E(nc.sbuf_tensor("fb8", [128, NKT], F32))
        ffb8 = E(nc.sbuf_tensor("ffb8", [128, NKT], F32))
        idb = E(nc.sbuf_tensor("idb", [128, 128], BF16))
        idf = E(nc.sbuf_tensor("idf", [128, 128], F32))
        Abuf = E(nc.sbuf_tensor("Abuf", [128, 256], F32))
        cwb = E(nc.sbuf_tensor("cwb", [128, NDT * 4], F32))
        cbb = E(nc.sbuf_tensor("cbb", [128, NDT], F32))
        dtbb = E(nc.sbuf_tensor("dtbb", [128, NDT], F32))
        dpb = E(nc.sbuf_tensor("dpb", [128, NDT], F32))
        nw8 = E(nc.sbuf_tensor("nw8", [128, NKT], F32))
        nb8 = E(nc.sbuf_tensor("nb8", [128, NKT], F32))
        wstr = E(nc.sbuf_tensor("wstr", [128, 2, NDT, 128], BF16))
        q32 = E(nc.sbuf_tensor("q32", [128, 32], F32))
        q32b = E(nc.sbuf_tensor("q32b", [128, 32], F32))
        ps8 = E(nc.psum_tensor("ps8", [128, 8, 512], F32))
        sv = E(nc.semaphore("sv"))
        ss = E(nc.semaphore("ss"))
        st = E(nc.semaphore("st"))
        sg = E(nc.semaphore("sg"))
        sy = E(nc.semaphore("sy"))
        dw = E(nc.semaphore("dw"))
        dr = E(nc.semaphore("dr"))
        dsto = E(nc.semaphore("dsto"))
        dsg = E(nc.semaphore("dsg"))
        dst0 = E(nc.semaphore("dst0"))
        dst1 = E(nc.semaphore("dst1"))
        dst2 = E(nc.semaphore("dst2"))
        dst3 = E(nc.semaphore("dst3"))
        dwa = E(nc.semaphore("dwa"))
        dwb = E(nc.semaphore("dwb"))

        V = Eng(nc.vector, sv, "V")
        S = Eng(nc.scalar, ss, "S")
        T = Eng(nc.tensor, st, "T")
        G = Eng(nc.gpsimd, sg, "G")
        Y = Eng(nc.sync, sy, "Y")
        cw_ = [0]
        cr_ = [0]
        cs_ = [0]
        c0_ = [0]
        c1_ = [0]
        c2_ = [0]
        c3_ = [0]
        ca_ = [0]
        cb_2 = [0]

        ARB = AR.bitcast(BF16)

        def fv(off, n):
            return AR[:, off:off + n]

        def bv(off, nbf):
            return ARB[:, 2 * off:2 * off + nbf]

        ld0 = fv(O_LD0, L)
        ld1 = fv(O_LD1, L)
        tmpA = fv(O_TMPA, L)
        tmpB = fv(O_TMPB, L)
        big0 = fv(O_BIG0, 4096)
        big1 = fv(O_BIG1, 4096)
        big0k = big0.rearrange("p (k c) -> p k c", k=NKT)
        big1k = big1.rearrange("p (k c) -> p k c", k=NKT)
        dtin = AR[0:DTR, O_TMPB:O_TMPB + CH]
        xhat = bv(O_P, 2 * 16384).rearrange("p (b k l) -> p b k l", b=NB, k=NKT)
        xpre = fv(O_P + 16384, L + 3)
        cv = fv(O_P + 18436, L)
        xpT = fv(O_P, NDT * 96).rearrange("p (t m) -> p t m", t=NDT)
        dtT = AR[0:DTR, O_P + 1536:O_P + 1536 + DIN]
        BCrowb = ARB[0:16, 2 * O_BIG0:2 * O_BIG0 + 2 * NB * L].rearrange(
            "p (b v l) -> p b v l", b=NB, v=2)
        dt_g = fv(O_P, GRP * L).rearrange("p (g l) -> p g l", g=GRP)
        dtx_g = fv(O_P + 8192, GRP * L).rearrange("p (g l) -> p g l", g=GRP)
        y_g = fv(O_P + 16384, GRP * L).rearrange("p (g l) -> p g l", g=GRP)
        dAb = fv(O_P + 24576, L)
        ub = fv(O_P + 26624, L)
        hbb = fv(O_P + 28672, L)
        gst = bv(O_P, NDT * CH).rearrange("p (t c) -> p t c", t=NDT)
        woutres = bv(O_P + 4096, NDT * D).rearrange("p (k m) -> p k m", k=NDT)
        wdT = bv(O_P, NKT * D).rearrange("p (k m) -> p k m", k=NKT)
        qa = fv(O_P + 4096, D)
        qb = fv(O_P + 5120, D)
        qc = fv(O_P + 6144, D)
        qd = fv(O_P + 7168, D)
        qeb = bv(O_P + 8192, D)
        x1buf = fv(O_P + 8704, NKT * CH).rearrange("p (k c) -> p k c", k=NKT)
        h2T = fv(O_P + 12800, 4 * D).rearrange("p (t d) -> p t d", t=4)
        xdT = bv(O_P + 16896, NKT * CH).rearrange("p (k c) -> p k c", k=NKT)

        def bar():
            nc.all_engine_barrier()

        _mark(nc, 'prep0')
        V.op('memset', ones1[:], 1.0)
        V.op('memset', ones1b[:], 1.0)
        V.op('memset', cEPS[:], EPS)
        V.op('memset', cNH[:], -0.5)
        V.op('memset', onesr[:], 1.0)
        io128 = E(nc.sbuf_tensor("io128", [128, 128], I32))
        G.op('iota', io128[:], [[-1, 128]], base=0, channel_multiplier=1)
        V.op('tensor_scalar', out=idb[:], in0=io128[:], scalar1=0, scalar2=None,
             op0=Alu.is_equal, waits=[(sg, G.n)])
        V.op('tensor_scalar', out=idf[:], in0=io128[:], scalar1=0, scalar2=None,
             op0=Alu.is_equal)
        G.op('iota', io128[0:16, :], [[0, 128]], base=0, channel_multiplier=1,
             waits=[(sv, V.n)])
        for n in range(16):
            V.op('tensor_scalar', out=sel[:, n, :], in0=io128[0:16, :], scalar1=n,
                 scalar2=None, op0=Alu.is_equal, waits=[(sg, G.n)] if n == 0 else [])
        bar()

        _mark(nc, 'prep1')
        jobs = []
        for li0 in range(NL):
            for kt in range(NKT):
                jobs.append((t_win, t_winb, (li0 * D + kt * 128) * 2 * DIN, 2 * DIN))
        for li0 in range(NL):
            for kt in range(NDT):
                jobs.append((t_wout, t_woutb, (li0 * DIN + kt * 128) * D, D))
        bigs = [big0, big1]
        csem = [dw, dr]
        ccnt = [cw_, cr_]
        csto = [[0], [0]]
        for j, (src, dst, off, w) in enumerate(jobs):
            bb = j % 2
            dv = Y.dma(bigs[bb][:, 0:w], AP(src.tensor, off, [[w, 128], [1, w]]),
                       csem[bb], ccnt[bb],
                       waits=[(sv, V.n), ((dsto, dsg)[bb], csto[bb][0])] if j >= 2 else [])
            V.op('tensor_copy', out=bigs[bb].bitcast(BF16)[:, 0:w], in_=bigs[bb][:, 0:w],
                 waits=[(csem[bb], dv)])
            Y.dma(AP(dst.tensor, off, [[w, 128], [1, w]]), bigs[bb].bitcast(BF16)[:, 0:w],
                   (dsto, dsg)[bb], csto[bb], waits=[(sv, V.n)])
        G.wait(dsto, csto[0][0])
        G.wait(dsg, csto[1][0])
        cs_[0] = csto[0][0]
        bar()

        _mark(nc, 'prep2')
        dv = Y.dma(big0[:, 0:NKT * NB].rearrange("p (k b) -> p k b", k=NKT),
                   t_cT.rearrange("(k p) b -> p k b", p=128), dst0, c0_)
        S.op('activation', out=big0[:, 0:NKT * NB], in_=big0[:, 0:NKT * NB], func=Act.Silu,
             waits=[(dst0, c0_[0])])
        for jc in range(12):
            bank = jc % 2
            for kt in range(NKT):
                dv = Y.dma(tmpA[:, 0:512],
                           AP(t_wmod.tensor, kt * 128 * 6 * D + jc * 512,
                              [[6 * D, 128], [1, 512]]),
                           dr, cr_, waits=[(st, T.n)])
                T.op('matmul', out=ps8[0:NB, bank, :], lhsT=big0[:, 2 * kt:2 * kt + 2],
                     rhs=tmpA[:, 0:512], start=(kt == 0), stop=(kt == NKT - 1),
                     waits=[(dr, dv), (ss, S.n)])
            S.op('copy', out=AR[0:NB, O_P + jc * 512:O_P + (jc + 1) * 512],
                 in_=ps8[0:NB, bank, :], waits=[(st, T.n)])
        for blk in range(48):
            T.op('transpose', out=ps8[:, blk % 2, 0:NB],
                 in_=AR[0:NB, O_P + blk * 128:O_P + (blk + 1) * 128],
                 identity=idf[0:NB, 0:NB],
                 waits=[(ss, S.n), (sv, V.n)])
            V.op('tensor_copy', out=modT[:, blk, :], in_=ps8[:, blk % 2, 0:NB],
                 waits=[(st, T.n)])
        for v0 in (1, 4):
            V.op('tensor_scalar', out=modT[:, v0 * NKT:(v0 + 1) * NKT, :],
                 in0=modT[:, v0 * NKT:(v0 + 1) * NKT, :], scalar1=1.0, scalar2=None,
                 op0=Alu.add)
        for v0 in (2, 5):
            gsl = modT[:, v0 * NKT:(v0 + 1) * NKT, :]
            nn = NKT * NB
            V.op('tensor_scalar', out=big0[:, 0:nn], in0=gsl, scalar1=0.0, scalar2=None,
                 op0=Alu.is_ge)
            V.op('tensor_scalar', out=big0[:, 0:nn], in0=big0[:, 0:nn], scalar1=2.0,
                 scalar2=-1.0, op0=Alu.mult, op1=Alu.add)
            S.op('activation', out=big1[:, 0:nn], in_=gsl, func=Act.Abs, waits=[(sv, V.n)])
            V.op('tensor_scalar', out=big1[:, 0:nn], in0=big1[:, 0:nn], scalar1=0.25,
                 scalar2=None, op0=Alu.max, waits=[(ss, S.n)])
            V.op('tensor_tensor', out=gsl, in0=big0[:, 0:nn], in1=big1[:, 0:nn], op=Alu.mult)
        Y.dma(fw8[:], t_fw, dst0, c0_)
        Y.dma(fb8[:], t_fb, dst0, c0_)
        Y.dma(ffb8[:], t_ffb, dst0, c0_)
        Y.wait(dst0, c0_[0])
        bar()

        ln_sq_t = [0]

        def ln_chunk(xk, extra_wait=(), par=0):
            bs0, bs1 = (0, 1) if par == 0 else (4, 5)
            br0, br1 = (2, 3) if par == 0 else (6, 7)
            xkb = ARB[:, 2 * O_LNSQ:2 * O_LNSQ + NKT * CH].rearrange(
                "p (k c) -> p k c", k=NKT)
            sq = ARB[:, 2 * O_LNSQ + NKT * CH:2 * O_LNSQ + 2 * NKT * CH]
            sqk = sq.rearrange("p (k c) -> p k c", k=NKT)
            so = O_SROW + par * 1536
            mu = AR[0:1, so:so + CH]
            r_ = AR[0:1, so + CH:so + 2 * CH]
            tm = AR[0:1, so + 2 * CH:so + 3 * CH]
            S.op('copy', out=xkb.rearrange("p k c -> p (k c)"),
                 in_=xk.rearrange("p k c -> p (k c)"),
                 waits=list(extra_wait) + [(st, ln_sq_t[0])])
            S.op('activation', out=sq, in_=xk.rearrange("p k c -> p (k c)"),
                 func=Act.Square)
            for kt in range(NKT):
                T.op('matmul', out=ps8[0:1, bs0, :], lhsT=ones1b[:], rhs=xkb[:, kt, :],
                     start=(kt == 0), stop=(kt == NKT - 1),
                     waits=[(ss, S.n - 1)] if kt == 0 else [])
            for kt in range(NKT):
                T.op('matmul', out=ps8[0:1, bs1, :], lhsT=ones1b[:], rhs=sqk[:, kt, :],
                     start=(kt == 0), stop=(kt == NKT - 1),
                     waits=[(ss, S.n)] if kt == 0 else [])
            ln_sq_t[0] = T.n
            V.op('tensor_scalar', out=mu, in0=ps8[0:1, bs0, :], scalar1=1.0 / D, scalar2=None,
                 op0=Alu.mult, waits=[(st, T.n)])
            V.op('tensor_scalar', out=tm, in0=ps8[0:1, bs1, :], scalar1=1.0 / D, scalar2=None,
                 op0=Alu.mult)
            V.op('tensor_tensor', out=r_, in0=mu, in1=mu, op=Alu.mult)
            V.op('tensor_tensor', out=tm, in0=tm, in1=r_, op=Alu.subtract)
            S.op('activation', out=tm, in_=tm, func=Act.Ln, bias=cEPS[0:1, :],
                 waits=[(sv, V.n)])
            S.op('activation', out=r_, in_=tm, func=Act.Exp, scale=cNH[0:1, :])
            T.op('matmul', out=ps8[:, br0, :], lhsT=onesr[:], rhs=mu, start=True,
                 stop=True, waits=[(sv, V.n)])
            T.op('matmul', out=ps8[:, br1, :], lhsT=onesr[:], rhs=r_, start=True,
                 stop=True, waits=[(ss, S.n)])
            V.op('tensor_tensor', out=xk, in0=xk,
                 in1=ps8[:, br0, :].unsqueeze(1).broadcast_to([128, NKT, CH]),
                 op=Alu.subtract, waits=[(st, T.n)])
            V.op('tensor_tensor', out=xk, in0=xk,
                 in1=ps8[:, br1, :].unsqueeze(1).broadcast_to([128, NKT, CH]), op=Alu.mult)

        def r_chunk_ap(b, ch):
            return t_R[b].transpose([1, 0, 2])[:, :, ch * CH:(ch + 1) * CH]

        def hs_chunk_ap(b, ch):
            return t_hsT[b].rearrange("(k p) l -> p k l", p=128)[:, :, ch * CH:(ch + 1) * CH]

        _mark(nc, 'stage1')
        bigks = (big0k, big1k)
        st_sem = ((dst0, c0_), (dst1, c1_))
        vbuf = [0, 0]
        for idx, (b, ch) in enumerate([(b, ch) for b in range(NB) for ch in range(NCH)]):
            par = idx % 2
            xbk = bigks[par]
            psem, pcnt = st_sem[par]
            dv = Y.dma(xbk, hs_chunk_ap(b, ch), dr, cr_,
                       waits=[(sv, vbuf[par]), (psem, pcnt[0])])
            ln_chunk(xbk, extra_wait=[(dr, dv)], par=par)
            for kt in range(NKT):
                V.op('tensor_scalar', out=xbk[:, kt, :], in0=xbk[:, kt, :],
                     scalar1=modT[:, 1 * NKT + kt, b:b+1], scalar2=modT[:, 0 * NKT + kt, b:b+1],
                     op0=Alu.mult, op1=Alu.add)
            vbuf[par] = V.n
            Y.dma(r_chunk_ap(b, ch), xbk, psem, pcnt, waits=[(sv, V.n)])
        G.wait(dst0, c0_[0])
        G.wait(dst1, c1_[0])
        bar()

        _mark(nc, 'loop_head')
        loop_sems = (sv, ss, st, sg, sy, dw, dr, dsto, dsg, dst0, dst1, dst2, dst3, dwa, dwb)
        for sem_ in loop_sems:
            nc.sync.sem_clear(sem_)
        bar()
        with nc.Fori(0, NL_RUN) as li:
            for e in (V, S, T, G, Y):
                e.reset()
            for c in (cw_, cr_, cs_, c0_, c1_, c2_, c3_, ca_, cb_2):
                c[0] = 0
            ln_sq_t[0] = 0

            cg_ = [0]
            with nc.sync.register("lof") as lof:
                nc.sync.reg_mul(lof, li, PW)
                Y.dma(t_pstg[:], AP(t_lay.tensor, lof, [[1, PW]]), dsg, cg_)
                WQ = D * 2 * DIN // 4
                for j in range(4):
                    nc.sync.reg_mul(lof, li, D * 2 * DIN)
                    nc.sync.reg_add(lof, lof, j * WQ)
                    Y.dma(AP(t_wsin.tensor, j * WQ, [[1, WQ]]),
                          AP(t_winb.tensor, lof, [[1, WQ]]), dsg, cg_)
                WO = DIN * D // 2
                for j in range(2):
                    nc.sync.reg_mul(lof, li, DIN * D)
                    nc.sync.reg_add(lof, lof, j * WO)
                    Y.dma(AP(t_wsout.tensor, j * WO, [[1, WO]]),
                          AP(t_woutb.tensor, lof, [[1, WO]]), dsg, cg_)
            STAGE_DONE = cg_[0]
            Y.wait(dsg, STAGE_DONE)
            Y.dma(Abuf[:], AP(t_pstg.tensor, PO_A, [[256, 128], [1, 256]]), dw, cw_)
            Y.dma(cwb[:], AP(t_pstg.tensor, PO_CW, [[64, 128], [1, 64]]), dw, cw_)
            Y.dma(cbb[:], AP(t_pstg.tensor, PO_CB, [[16, 128], [1, 16]]), dw, cw_)
            Y.dma(dtbb[:], AP(t_pstg.tensor, PO_DTB, [[16, 128], [1, 16]]), dw, cw_)
            Y.dma(dpb[:], AP(t_pstg.tensor, PO_DP, [[16, 128], [1, 16]]), dw, cw_)
            Y.dma(nw8[:], AP(t_pstg.tensor, PO_NW, [[8, 128], [1, 8]]), dw, cw_)
            Y.dma(nb8[:], AP(t_pstg.tensor, PO_NB, [[8, 128], [1, 8]]), dw, cw_)
            S.op('activation', out=Abuf[:], in_=Abuf[:], func=Act.Exp, waits=[(dw, cw_[0])])
            V.op('tensor_scalar', out=Abuf[:], in0=Abuf[:], scalar1=-1.0, scalar2=None,
                 op0=Alu.mult, waits=[(ss, S.n)])
            bar()

            _mark(nc, 'a1')
            vbuf = [0, 0]
            for idx, (b, ch) in enumerate([(b, ch) for b in range(NB) for ch in range(NCH)]):
                par = idx % 2
                xbk = bigks[par]
                dv = Y.dma(xbk, r_chunk_ap(b, ch), dr, cr_,
                           waits=[(sv, vbuf[par])])
                ln_chunk(xbk, extra_wait=[(dr, dv)], par=par)
                for kt in range(NKT):
                    V.op('tensor_scalar', out=xhat[:, b, kt, ch * CH:(ch + 1) * CH],
                         in0=xbk[:, kt, :], scalar1=nw8[:, kt:kt + 1],
                         scalar2=nb8[:, kt:kt + 1], op0=Alu.mult, op1=Alu.add)
                vbuf[par] = V.n
            bar()

            V.region = "a2"
            _mark(nc, 'a2')
            V.op('memset', xpre[:, 0:3], 0.0)
            if True:
                last_store = [0, 0]
                t_after_mt = [0, 0]
                s_bank = [0] * 8

                def wsin_ap(m):
                    return AP(t_wsin.tensor, m * 128,
                              [[2 * DIN, 128], [128 * 2 * DIN, NKT], [1, 128]])

                dvw_slot = [0, 0]
                for m0 in range(2):
                    wsem, wcnt = ((dwa, ca_), (dwb, cb_2))[m0]
                    dvw_slot[m0] = Y.dma(wstr[:, m0, 0:NKT, :], wsin_ap(m0),
                                         wsem, wcnt, waits=[(dsg, STAGE_DONE)])
                for mt in range(2 * NDT):
                    wb = mt % 2
                    wsem, wcnt = ((dwa, ca_), (dwb, cb_2))[wb]
                    dvw = dvw_slot[wb]
                    for kt in range(NKT):
                        for b in range(NB):
                            for ch in range(NCH):
                                w0 = []
                                if kt == 0:
                                    w0 = [(ss, s_bank[b * NCH + ch])]
                                    if b == 0 and ch == 0:
                                        w0.append((wsem, dvw))
                                T.op('matmul', out=ps8[:, b * NCH + ch, :],
                                     lhsT=wstr[:, wb, kt, :],
                                     rhs=xhat[:, b, kt, ch * CH:(ch + 1) * CH],
                                     start=(kt == 0), stop=(kt == NKT - 1), waits=w0)
                    t_after_mt[wb] = T.n
                    if mt < NDT:
                        for b in range(NB):
                            ob = (ld0, ld1)[b]
                            bsem, bcnt = ((dst0, c0_), (dst1, c1_))[b]
                            for ch in range(NCH):
                                S.op('copy', out=xpre[:, 3 + ch * CH:3 + (ch + 1) * CH],
                                     in_=ps8[:, b * NCH + ch, :],
                                     waits=[(st, T.n), (sv, V.n)] if ch == 0 else [])
                                s_bank[b * NCH + ch] = S.n
                            t4 = mt * 4
                            V.op('tensor_scalar', out=cv[:], in0=xpre[:, 0:L],
                                 scalar1=cwb[:, t4:t4 + 1], scalar2=None, op0=Alu.mult,
                                 waits=[(ss, S.n)])
                            for k in range(1, 4):
                                V.op('scalar_tensor_tensor', out=cv[:], in0=xpre[:, k:L + k],
                                     scalar=cwb[:, t4 + k:t4 + k + 1], in1=cv[:],
                                     op0=Alu.mult, op1=Alu.add)
                            S.op('activation', out=ob, in_=cv[:], func=Act.Silu,
                                 bias=cbb[:, mt:mt + 1],
                                 waits=[(sv, V.n), (bsem, bcnt[0])])
                            S.dma(t_xc[b, mt], ob, bsem, bcnt, waits=[(ss, S.n)])
                    else:
                        for b in range(NB):
                            for ch in range(NCH):
                                S.op('copy', out=big1[:, ch * CH:(ch + 1) * CH],
                                     in_=ps8[:, b * NCH + ch, :],
                                     waits=([(st, T.n), (dst2, c2_[0])]
                                            if ch == 0 else []))
                                s_bank[b * NCH + ch] = S.n
                            S.dma(t_z[b, mt - NDT], big1[:, 0:L], dst2, c2_,
                                  waits=[(ss, S.n)])
                    if mt + 2 < 2 * NDT:
                        dvw_slot[wb] = S.dma(wstr[:, wb, 0:NKT, :], wsin_ap(mt + 2),
                                             wsem, wcnt)
            G.wait(dst0, c0_[0])
            G.wait(dst1, c1_[0])
            G.wait(dst2, c2_[0])
            bar()

            V.region = "a3"
            _mark(nc, 'a3')
            dvx1 = Y.dma(xpT, AP(t_pstg.tensor, PO_XP,
                                 [[96, 128], [128 * 96, NDT], [1, 96]]), dw, cw_)
            dvx2 = Y.dma(dtT, AP(t_pstg.tensor, PO_DTW, [[DIN, DTR], [1, DIN]]), dw, cw_)
            xpTb = bv(O_P + 3584, NDT * 96).rearrange("p (t m) -> p t m", t=NDT)
            dtTb = ARB[0:DTR, 2 * (O_P + 4352):2 * (O_P + 4352) + DIN]
            V.op('tensor_copy', out=xpTb, in_=xpT, waits=[(dw, dvx2)])
            V.op('tensor_copy', out=dtTb, in_=dtT)
            dtinB = ARB[0:DTR, 2 * (O_TMPB + CH):2 * (O_TMPB + CH) + CH]
            lbF = (ld0[:, 0:CH], ld1[:, 0:CH], tmpA[:, 0:CH], tmpA[:, CH:2 * CH])
            lbB4 = (ARB[:, 2 * (O_LD0 + CH):2 * (O_LD0 + CH) + CH],
                    ARB[:, 2 * (O_LD1 + CH):2 * (O_LD1 + CH) + CH],
                    ARB[:, 2 * (O_TMPA + 2 * CH):2 * (O_TMPA + 2 * CH) + CH],
                    ARB[:, 2 * (O_TMPA + 2 * CH) + CH:2 * (O_TMPA + 2 * CH) + 2 * CH])
            chunks = [(b, ch) for b in range(NB) for ch in range(NCH)]
            NQ = len(chunks) * NDT

            def xc_ap(q):
                qb, qch = chunks[q // NDT]
                return t_xc[qb, q % NDT, :, qch * CH:(qch + 1) * CH]

            dvxq = {}
            t_slot = [0, 0, 0, 0]
            for q in range(4):
                dvxq[q] = Y.dma(lbF[q], xc_ap(q), dr, cr_)
            dt_bank_s = [0, 0, 0, 0]
            dt_sems = ((dst0, c0_), (dst1, c1_), (dst2, c2_), (dst3, c3_))
            for ci, (b, ch) in enumerate(chunks):
                for kt in range(NDT):
                    q = ci * NDT + kt
                    sl = q % 4
                    S.op('copy', out=lbB4[sl], in_=lbF[sl], waits=[(dr, dvxq[q])])
                    T.op('matmul', out=ps8[0:DTR, 0, :], lhsT=xpTb[:, kt, 0:DTR],
                         rhs=lbB4[sl], start=(kt == 0), stop=(kt == NDT - 1),
                         waits=[(ss, S.n)] +
                               ([(sv, V.n)] if kt == 0 else []))
                    T.op('matmul', out=ps8[0:16, 1, :], lhsT=xpTb[:, kt, DTR:DTR + 16],
                         rhs=lbB4[sl], start=(kt == 0), stop=(kt == NDT - 1))
                    T.op('matmul', out=ps8[0:16, 2, :], lhsT=xpTb[:, kt, DTR + 16:96],
                         rhs=lbB4[sl], start=(kt == 0), stop=(kt == NDT - 1))
                    t_slot[sl] = T.n
                    if q + 4 < NQ:
                        dvxq[q + 4] = Y.dma(lbF[sl], xc_ap(q + 4), dr, cr_,
                                            waits=[(st, t_slot[sl])])
                S.op('copy', out=dtinB, in_=ps8[0:DTR, 0, :], waits=[(st, T.n)])
                s_dtin = S.n
                V.op('tensor_copy', out=BCrowb[:, b, 0, ch * CH:(ch + 1) * CH],
                     in_=ps8[0:16, 1, :], waits=[(st, T.n)])
                V.op('tensor_copy', out=BCrowb[:, b, 1, ch * CH:(ch + 1) * CH],
                     in_=ps8[0:16, 2, :])
                for mt in range(NDT):
                    jf = mt % 4
                    bk = 3 + jf
                    hsem, hcnt = dt_sems[jf]
                    T.op('matmul', out=ps8[:, bk, :],
                         lhsT=dtTb[:, mt * 128:(mt + 1) * 128], rhs=dtinB,
                         start=True, stop=True,
                         waits=[(ss, max(s_dtin, dt_bank_s[jf]))])
                    S.op('copy', out=big1[:, jf * CH:(jf + 1) * CH],
                         in_=ps8[:, bk, :],
                         waits=[(st, T.n), (hsem, hcnt[0])])
                    dt_bank_s[jf] = S.n
                    S.dma(t_dt[b, mt, :, ch * CH:(ch + 1) * CH],
                          big1[:, jf * CH:(jf + 1) * CH], hsem, hcnt,
                          waits=[(ss, S.n)])
            G.wait(dst0, c0_[0])
            G.wait(dst1, c1_[0])
            G.wait(dst2, c2_[0])
            G.wait(dst3, c3_[0])
            bar()

            V.region = "bprep"
            _mark(nc, 'b')
            BrepB = tmpA.bitcast(BF16)[:, 0:L]
            CrepB = tmpA.bitcast(BF16)[:, L:2 * L]
            for b in range(NB):
                for g0 in range(0, NDT, GRP):
                    V.region = "bprep"
                    xcls = []
                    for ti in range(GRP):
                        t = g0 + ti
                        Y.dma(dt_g[:, ti, :], t_dt[b, t], dr, cr_,
                              waits=[(sv, V.n), (ss, S.n), (dst2, c2_[0]),
                                     (dst3, c3_[0])] if ti == 0 else [])
                        xcl = (ld0, ld1, tmpB, tmpA)[ti]
                        xcls.append(xcl)
                        Y.dma(xcl, t_xc[b, t], dr, cr_)
                    GL = cr_[0]
                    for ti in range(GRP):
                        t = g0 + ti
                        xcl = xcls[ti]
                        V.op('tensor_scalar', out=dt_g[:, ti, :], in0=dt_g[:, ti, :],
                             scalar1=dtbb[:, t:t + 1], scalar2=None, op0=Alu.add,
                             waits=[(dr, GL)])
                        S.op('activation', out=dAb[:], in_=dt_g[:, ti, :], func=Act.Relu,
                             waits=[(sv, V.n)])
                        V.op('scalar_tensor_tensor', out=ub[:], in0=dAb[:], scalar=-2.0,
                             in1=dt_g[:, ti, :], op0=Alu.mult, op1=Alu.add,
                             waits=[(ss, S.n)])
                        S.op('activation', out=ub[:], in_=ub[:], func=Act.Exp,
                             waits=[(sv, V.n)])
                        S.op('activation', out=ub[:], in_=ub[:], func=Act.Ln, bias=ones1[:])
                        V.op('tensor_tensor', out=dt_g[:, ti, :], in0=dAb[:], in1=ub[:],
                             op=Alu.add, waits=[(ss, S.n)])
                        V.op('tensor_tensor', out=dtx_g[:, ti, :], in0=dt_g[:, ti, :],
                             in1=xcl, op=Alu.mult)
                    V.region = "bscan"
                    vread_rep = [V.n, V.n]
                    vread_dA = [V.n, V.n]
                    for n in range(16):
                        np_ = n % 2
                        repbuf = (tmpA, ld0)[np_]
                        Bp = repbuf.bitcast(BF16)[:, 0:L]
                        Cp = repbuf.bitcast(BF16)[:, L:2 * L]
                        for cc in range(4):
                            T.op('matmul', out=ps8[:, cc, :], lhsT=sel[:, n, :],
                                 rhs=BCrowb[:, b, 0, cc * CH:(cc + 1) * CH], start=True,
                                 stop=True,
                                 waits=[(sv, vread_rep[np_]), (ss, S.n)] if cc == 0 else [])
                            T.op('matmul', out=ps8[:, 4 + cc, :], lhsT=sel[:, n, :],
                                 rhs=BCrowb[:, b, 1, cc * CH:(cc + 1) * CH], start=True,
                                 stop=True)
                        for cc in range(4):
                            S.op('copy', out=Bp[:, cc * CH:(cc + 1) * CH],
                                 in_=ps8[:, cc, :],
                                 waits=[(st, T.n)] if cc == 0 else [])
                            S.op('copy', out=Cp[:, cc * CH:(cc + 1) * CH],
                                 in_=ps8[:, 4 + cc, :])
                        s_rep = S.n
                        for ti in range(GRP):
                            t = g0 + ti
                            tp = ti % 2
                            dab = (dAb, tmpB)[tp]
                            S.op('activation', out=dab[:], in_=dt_g[:, ti, :], func=Act.Exp,
                                 scale=Abuf[:, t * 16 + n:t * 16 + n + 1],
                                 waits=[(sv, vread_dA[tp])])
                            s_exp = S.n
                            V.op('tensor_tensor', out=ub[:], in0=dtx_g[:, ti, :], in1=Bp,
                                 op=Alu.mult, waits=[(ss, s_rep)])
                            V.op('tensor_tensor_scan', out=hbb[:], data0=dab[:], data1=ub[:],
                                 initial=0.0, op0=Alu.mult, op1=Alu.add,
                                 waits=[(ss, s_exp)])
                            vread_dA[tp] = V.n
                            if n == 0:
                                V.op('tensor_tensor', out=y_g[:, ti, :], in0=hbb[:],
                                     in1=Cp, op=Alu.mult)
                            else:
                                V.op('tensor_tensor', out=ub[:], in0=hbb[:], in1=Cp,
                                     op=Alu.mult)
                                V.op('tensor_tensor', out=y_g[:, ti, :], in0=y_g[:, ti, :],
                                     in1=ub[:], op=Alu.add)
                            vread_rep[np_] = V.n
                    V.region = "bgate"
                    for ti in range(GRP):
                        t = g0 + ti
                        dv1 = Y.dma(ld0, t_z[b, t], dr, cr_, waits=[(ss, S.n), (sv, V.n)])
                        dv2 = Y.dma(ld1, t_xc[b, t], dr, cr_)
                        gob = (dAb, hbb)[ti % 2]
                        gsem, gcnt = ((dst2, c2_), (dst3, c3_))[ti % 2]
                        S.op('activation', out=tmpB[:], in_=ld0, func=Act.Silu,
                             waits=[(dr, dv2), (sv, V.n)])
                        V.op('scalar_tensor_tensor', out=y_g[:, ti, :], in0=ld1,
                             scalar=dpb[:, t:t + 1], in1=y_g[:, ti, :], op0=Alu.mult,
                             op1=Alu.add, waits=[(dr, dv2), (ss, S.n)])
                        V.op('tensor_tensor', out=gob.bitcast(BF16)[:, 0:L],
                             in0=y_g[:, ti, :], in1=tmpB[:], op=Alu.mult,
                             waits=[(gsem, gcnt[0])])
                        Y.dma(t_g[b, t], gob.bitcast(BF16)[:, 0:L], gsem, gcnt,
                              waits=[(sv, V.n)])
            G.wait(dst2, c2_[0])
            G.wait(dst3, c3_[0])
            bar()

            V.region = "c"
            _mark(nc, 'c')
            if True:
                dvwo = Y.dma(woutres, AP(t_wsout.tensor, 0,
                                         [[D, 128], [128 * D, NDT], [1, D]]),
                             dwa, ca_, waits=[(dsg, STAGE_DONE), (st, T.n), (sv, V.n)])
                for b in range(NB):
                    for ch in range(NCH):
                        dvg = Y.dma(gst, t_g[b].transpose([1, 0, 2])[:, :, ch * CH:(ch + 1) * CH],
                                    dr, cr_, waits=[(st, T.n), (dsto, cs_[0])])
                        dvr = Y.dma(big0k, r_chunk_ap(b, ch), dr, cr_,
                                    waits=[(sv, V.n)])
                        dvg = dvr
                        v_mt = [0, 0]
                        for mt in range(NKT):
                            wb = mt % 2
                            for kt in range(NDT):
                                T.op('matmul', out=ps8[:, 6 + wb, :],
                                     lhsT=woutres[:, kt, mt * 128:(mt + 1) * 128],
                                     rhs=gst[:, kt, :], start=(kt == 0), stop=(kt == NDT - 1),
                                     waits=[(dwa, dvwo), (dr, dvg), (sv, v_mt[wb])]
                                     if kt == 0 else [])
                            V.op('tensor_tensor', out=big0k[:, mt, :], in0=big0k[:, mt, :],
                                 in1=ps8[:, 6 + wb, :], op=Alu.add,
                                 waits=[(st, T.n), (dr, dvr)])
                            v_mt[wb] = V.n
                        Y.dma(r_chunk_ap(b, ch), big0k, dsto, cs_, waits=[(sv, V.n)])
            G.wait(dsto, cs_[0])

            bar()
            for sem_ in loop_sems:
                nc.sync.sem_clear(sem_)
            bar()

        for e in (V, S, T, G, Y):
            e.reset()
        for c in (cw_, cr_, cs_, c0_, c1_, c2_, c3_, ca_, cb_2):
            c[0] = 0
        ln_sq_t[0] = 0

        V.region = "s2"
        _mark(nc, 'wquant')
        def quant_tile(x_ap, out_bf, width):
            nbk = width // 32
            xb = x_ap.rearrange("p (k c) -> p k c", c=32)
            V.op('tensor_reduce', out=q32[:, 0:nbk], in_=xb, axis=mybir.AxisListType.X,
                 op=Alu.max, apply_absolute_value=True)
            V.op('tensor_scalar', out=q32[:, 0:nbk], in0=q32[:, 0:nbk], scalar1=1e-35,
                 scalar2=None, op0=Alu.max)
            V.op('reciprocal', out=q32b[:, 0:nbk], in_=q32[:, 0:nbk])
            V.op('tensor_scalar', out=q32b[:, 0:nbk], in0=q32b[:, 0:nbk], scalar1=7.5,
                 scalar2=None, op0=Alu.mult)
            V.op('reciprocal', out=q32[:, 0:nbk], in_=q32b[:, 0:nbk])
            V.op('tensor_scalar', out=q32[:, 0:nbk], in0=q32[:, 0:nbk], scalar1=1e-30,
                 scalar2=None, op0=Alu.max)
            V.op('reciprocal', out=q32b[:, 0:nbk], in_=q32[:, 0:nbk])
            V.op('tensor_tensor', out=qa[:, 0:width].rearrange("p (k c) -> p k c", c=32),
                 in0=xb, in1=q32b[:, 0:nbk].unsqueeze(2).broadcast_to([128, nbk, 32]),
                 op=Alu.mult)
            V.op('tensor_tensor', out=qb[:, 0:width], in0=qa[:, 0:width],
                 in1=qa[:, 0:width], op=Alu.mult)
            V.op('tensor_scalar', out=qc[:, 0:width], in0=qb[:, 0:width], scalar1=4.0,
                 scalar2=None, op0=Alu.is_ge)
            V.op('tensor_scalar', out=qd[:, 0:width], in0=qb[:, 0:width], scalar1=16.0,
                 scalar2=None, op0=Alu.is_ge)
            V.op('tensor_scalar', out=qc[:, 0:width], in0=qc[:, 0:width], scalar1=0.125,
                 scalar2=0.125, op0=Alu.mult, op1=Alu.add)
            V.op('tensor_scalar', out=qd[:, 0:width], in0=qd[:, 0:width], scalar1=1.0,
                 scalar2=None, op0=Alu.add)
            V.op('tensor_tensor', out=qc[:, 0:width], in0=qc[:, 0:width],
                 in1=qd[:, 0:width], op=Alu.mult)
            V.op('reciprocal', out=qd[:, 0:width], in_=qc[:, 0:width])
            V.op('tensor_tensor', out=qb[:, 0:width], in0=qa[:, 0:width],
                 in1=qd[:, 0:width], op=Alu.mult)
            V.op('tensor_scalar', out=qb[:, 0:width], in0=qb[:, 0:width], scalar1=MAGIC,
                 scalar2=-MAGIC, op0=Alu.add, op1=Alu.add)
            V.op('tensor_tensor', out=qb[:, 0:width], in0=qb[:, 0:width],
                 in1=qc[:, 0:width], op=Alu.mult)
            V.op('tensor_tensor', out=out_bf.rearrange("p (k c) -> p k c", c=32),
                 in0=qb[:, 0:width].rearrange("p (k c) -> p k c", c=32),
                 in1=q32[:, 0:nbk].unsqueeze(2).broadcast_to([128, nbk, 32]), op=Alu.mult)

        for wt in range(NKT):
            dvq = Y.dma(big0[:, 0:D], t_ffw[wt * 128:(wt + 1) * 128, :], dr, cr_,
                        waits=[(sv, V.n), (st, T.n)])
            V.wait(dr, dvq)
            quant_tile(big0[:, 0:D], qeb, D)
            for kt in range(NKT):
                T.op('transpose', out=ps8.bitcast(BF16)[:, kt % 2, 0:128],
                     in_=qeb[:, kt * 128:(kt + 1) * 128], identity=idb[:],
                     waits=[(sv, V.n)])
                V.op('tensor_copy', out=wdT[:, kt, wt * 128:(wt + 1) * 128],
                     in_=ps8.bitcast(BF16)[:, kt % 2, 0:128], waits=[(st, T.n)])
        bar()

        _mark(nc, 'stage2')
        for b in range(NB):
            for ch in range(NCH):
                dv1 = Y.dma(big0k, r_chunk_ap(b, ch), dr, cr_,
                            waits=[(sv, V.n), (ss, S.n), (dsto, cs_[0])])
                dv2 = Y.dma(x1buf, hs_chunk_ap(b, ch), dr, cr_)
                ln_chunk(big0k, extra_wait=[(dr, dv2)])
                for kt in range(NKT):
                    V.op('tensor_scalar', out=big0k[:, kt, :], in0=big0k[:, kt, :],
                         scalar1=fw8[:, kt:kt + 1], scalar2=fb8[:, kt:kt + 1],
                         op0=Alu.mult, op1=Alu.add)
                for kt in range(NKT):
                    V.op('scalar_tensor_tensor', out=x1buf[:, kt, :], in0=big0k[:, kt, :],
                         scalar=modT[:, 2 * NKT + kt, b:b+1], in1=x1buf[:, kt, :],
                         op0=Alu.mult, op1=Alu.add,
                         waits=[(dr, dv2)] if kt == 0 else [])
                V.op('tensor_copy', out=big0[:, 0:NKT * CH],
                     in_=x1buf.rearrange("p k c -> p (k c)"))
                ln_chunk(big0k, extra_wait=[(sv, V.n)])
                for kt in range(NKT):
                    V.op('tensor_scalar', out=big0k[:, kt, :], in0=big0k[:, kt, :],
                         scalar1=modT[:, 4 * NKT + kt, b:b+1], scalar2=modT[:, 3 * NKT + kt, b:b+1],
                         op0=Alu.mult, op1=Alu.add)
                for tb in range(4):
                    for kt in range(NKT):
                        bk = 4 + ((tb * NKT + kt) % 2)
                        T.op('transpose', out=ps8[:, bk, 0:128],
                             in_=big0k[:, kt, tb * 128:(tb + 1) * 128], identity=idf[:],
                             waits=[(sv, V.n)])
                        V.op('tensor_copy', out=h2T[:, tb, kt * 128:(kt + 1) * 128],
                             in_=ps8[:, bk, 0:128], waits=[(st, T.n)])
                for tb in range(4):
                    quant_tile(h2T[:, tb, :], qeb, D)
                    for kt in range(NKT):
                        bk = 4 + ((tb * NKT + kt) % 2)
                        T.op('transpose', out=ps8.bitcast(BF16)[:, bk, 0:128],
                             in_=qeb[:, kt * 128:(kt + 1) * 128], identity=idb[:],
                             waits=[(sv, V.n)])
                        V.op('tensor_copy', out=xdT[:, kt, tb * 128:(tb + 1) * 128],
                             in_=ps8.bitcast(BF16)[:, bk, 0:128], waits=[(st, T.n)])
                for mt in range(NKT):
                    bk = mt % 2
                    for kt in range(NKT):
                        T.op('matmul', out=ps8[:, bk, :],
                             lhsT=wdT[:, kt, mt * 128:(mt + 1) * 128], rhs=xdT[:, kt, :],
                             start=(kt == 0), stop=(kt == NKT - 1),
                             waits=[(sv, V.n), (ss, S.n)] if kt == 0 else [])
                    S.op('activation', out=big0k[:, mt, :], in_=ps8[:, bk, :],
                         func=Act.Identity, bias=ffb8[:, mt:mt + 1], waits=[(st, T.n)])
                    V.op('scalar_tensor_tensor', out=big0k[:, mt, :], in0=big0k[:, mt, :],
                         scalar=modT[:, 5 * NKT + mt, b:b+1], in1=x1buf[:, mt, :],
                         op0=Alu.mult, op1=Alu.add, waits=[(ss, S.n)])
                Y.dma(t_out[b].rearrange("(k p) l -> p k l", p=128)
                      [:, :, ch * CH:(ch + 1) * CH], big0k, dsto, cs_,
                      waits=[(sv, V.n)])
        G.wait(dsto, cs_[0])
    _mark(nc, 'end')
    return nc


_PROG = None


def _get_prog():
    global _PROG
    if _PROG is None:
        _PROG = build_program()
    return _PROG


def _layout(inputs):
    f = np.ascontiguousarray
    hs = np.asarray(inputs["L0_attn_hidden_states"])
    c = np.asarray(inputs["L0_c"])

    def vecN(x):
        n = x.shape[-1] // 128
        return f(np.asarray(x, np.float32).reshape(x.shape[0], n, 128).transpose(0, 2, 1))

    def vec1(x):
        n = x.shape[-1] // 128
        return f(np.asarray(x, np.float32).reshape(n, 128).T)

    A_lay = (np.asarray(inputs["L0_attn_A_log"], np.float32)
             .reshape(NL, NDT, 128, 16).transpose(0, 2, 1, 3).reshape(NL, 128, 256))
    conv_w = (np.asarray(inputs["L0_attn_conv_w"], np.float32)
              .reshape(NL, NDT, 128, 4).transpose(0, 2, 1, 3).reshape(NL, 128, NDT * 4))
    lay = np.concatenate([
        A_lay.reshape(NL, -1),
        conv_w.reshape(NL, -1),
        vecN(inputs["L0_attn_conv_b"]).reshape(NL, -1),
        vecN(inputs["L0_attn_dt_proj_b"]).reshape(NL, -1),
        vecN(inputs["L0_attn_D_param"]).reshape(NL, -1),
        vecN(inputs["L0_attn_norm_w"]).reshape(NL, -1),
        vecN(inputs["L0_attn_norm_b"]).reshape(NL, -1),
        np.asarray(inputs["L0_attn_x_proj_w"], np.float32).transpose(0, 2, 1).reshape(NL, -1),
        np.asarray(inputs["L0_attn_dt_proj_w"], np.float32).transpose(0, 2, 1).reshape(NL, -1),
    ], axis=1)
    assert lay.shape[1] == PW, (lay.shape, PW)
    com = dict(
        wmod=f(np.asarray(inputs["L0_wmod"], np.float32)),
        in_projT=f(np.asarray(inputs["L0_attn_in_proj_w"], np.float32).transpose(0, 2, 1)),
        out_projT=f(np.asarray(inputs["L0_attn_out_proj_w"], np.float32).transpose(0, 2, 1)),
        lay_params=f(lay),
        fnorm_w8=vec1(inputs["L0_attn_fnorm_w"]),
        fnorm_b8=vec1(inputs["L0_attn_fnorm_b"]),
        ffn_w=f(np.asarray(inputs["L0_ffn_weight"], np.float32)),
        ffn_b8=vec1(inputs["L0_ffn_bias"]),
    )
    maps = []
    for core in range(8):
        b0 = core * NB
        m = dict(com)
        m["hsT"] = f(hs[b0:b0 + NB].transpose(0, 2, 1).astype(np.float32))
        m["cT"] = f(c[b0:b0 + NB].T.astype(np.float32))
        maps.append(m)
    return maps


def kernel(**inputs):
    nc = _get_prog()
    in_maps = _layout(inputs)
    res = run_bass_kernel_spmd(nc, in_maps, core_ids=list(range(8)))
    outs = [np.asarray(r["outT"]).transpose(0, 2, 1) for r in res.results]
    return np.ascontiguousarray(np.concatenate(outs, axis=0)).astype(np.float32)


# version 11
# speedup vs baseline: 1.2548x; 1.0077x over previous
import numpy as np
import concourse.bass as bass
import concourse.mybir as mybir
from concourse.bass_utils import run_bass_kernel_spmd
from concourse.bass import AP

F32 = mybir.dt.float32
BF16 = mybir.dt.bfloat16
I32 = mybir.dt.int32
Alu = mybir.AluOpType
Act = mybir.ActivationFunctionType

NB = 2
L = 2048
D = 1024
DIN = 2048
NL = 4
NCH = 4
CH = 512
NKT = D // 128
NDT = DIN // 128
DTR = 64
GRP = 4
EPS = 1e-5
MAGIC = float(3 << 22)
NL_RUN = 4
SELF_SYNC = frozenset(["V"])
V_SYNC_REGIONS = frozenset(["prep", "c", "s2"])

PO_A = 0
PO_CW = PO_A + 128 * 256
PO_CB = PO_CW + 128 * 64
PO_DTB = PO_CB + 128 * 16
PO_DP = PO_DTB + 128 * 16
PO_NW = PO_DP + 128 * 16
PO_NB = PO_NW + 128 * 8
PO_XP = PO_NB + 128 * 8
PO_DTW = PO_XP + DIN * 96
PW = PO_DTW + DTR * DIN

O_LD0 = 0
O_LD1 = 2048
O_TMPA = 4096
O_TMPB = 6144
O_BIG0 = 8192
O_BIG1 = 12288
O_P = 16384
O_LNSQ = O_P + 20992
O_SROW = O_P + 25088
ARW = O_P + 30720


class Eng:

    def __init__(self, eng, sem, tag=""):
        self.e, self.sem, self.n = eng, sem, 0
        self.tag = tag
        self.region = "prep"

    @property
    def selfsync(self):
        if self.tag != "V":
            return self.tag in SELF_SYNC
        return self.region in V_SYNC_REGIONS

    def op(self, name, *a, waits=(), **kw):
        for s, v in waits:
            self.e.wait_ge(s, v)
        if self.selfsync and self.n > 0:
            self.e.wait_ge(self.sem, self.n)
        r = getattr(self.e, name)(*a, **kw)
        r.then_inc(self.sem, 1)
        self.n += 1
        return r

    def dma(self, out, in_, dsem, dcnt, waits=(), **kw):
        for s, v in waits:
            self.e.wait_ge(s, v)
        if self.selfsync and self.n > 0:
            self.e.wait_ge(self.sem, self.n)
        r = self.e.dma_start(out=out, in_=in_, **kw)
        r.then_inc(dsem, 16)
        dcnt[0] += 16
        return dcnt[0]

    def wait(self, sem, v):
        self.e.wait_ge(sem, v)

    def reset(self):
        self.n = 0


PHASE_MARKS = []


def _mark(nc, name):
    PHASE_MARKS.append((name, sum(1 for _ in nc.all_instructions())))


def build_program():
    nc = bass.Bass("TRN2", target_bir_lowering=False,
                   detect_race_conditions=False)

    t_hsT = nc.dram_tensor("hsT", [NB, D, L], F32, kind="ExternalInput").ap()
    t_cT = nc.dram_tensor("cT", [D, NB], F32, kind="ExternalInput").ap()
    t_wmod = nc.dram_tensor("wmod", [D, 6 * D], F32, kind="ExternalInput").ap()
    t_win = nc.dram_tensor("in_projT", [NL, D, 2 * DIN], F32, kind="ExternalInput").ap()
    t_wout = nc.dram_tensor("out_projT", [NL, DIN, D], F32, kind="ExternalInput").ap()
    t_lay = nc.dram_tensor("lay_params", [NL, PW], F32, kind="ExternalInput").ap()
    t_fw = nc.dram_tensor("fnorm_w8", [128, NKT], F32, kind="ExternalInput").ap()
    t_fb = nc.dram_tensor("fnorm_b8", [128, NKT], F32, kind="ExternalInput").ap()
    t_ffw = nc.dram_tensor("ffn_w", [D, D], F32, kind="ExternalInput").ap()
    t_ffb = nc.dram_tensor("ffn_b8", [128, NKT], F32, kind="ExternalInput").ap()

    t_out = nc.dram_tensor("outT", [NB, D, L], F32, kind="ExternalOutput").ap()

    t_winb = nc.dram_tensor("win_bf16", [NL, D, 2 * DIN], BF16, kind="Internal").ap()
    t_woutb = nc.dram_tensor("wout_bf16", [NL, DIN, D], BF16, kind="Internal").ap()
    t_R = nc.dram_tensor("R_s", [NB, NKT, 128, L], F32, kind="Internal").ap()
    t_xc = nc.dram_tensor("xc_s", [NB, NDT, 128, L], F32, kind="Internal").ap()
    t_z = nc.dram_tensor("z_s", [NB, NDT, 128, L], F32, kind="Internal").ap()
    t_dt = nc.dram_tensor("dt_s", [NB, NDT, 128, L], F32, kind="Internal").ap()
    t_g = nc.dram_tensor("g_s", [NB, NDT, 128, L], BF16, kind="Internal").ap()
    t_wsin = nc.dram_tensor("wsin", [D, 2 * DIN], BF16, kind="Internal").ap()
    t_wsout = nc.dram_tensor("wsout", [DIN, D], BF16, kind="Internal").ap()
    t_pstg = nc.dram_tensor("pstg", [PW], F32, kind="Internal").ap()

    import contextlib
    with contextlib.ExitStack() as _es:
        E = _es.enter_context
        AR = E(nc.sbuf_tensor("AR", [128, ARW], F32))
        ones1 = E(nc.sbuf_tensor("ones1", [128, 1], F32))
        ones1b = E(nc.sbuf_tensor("ones1b", [128, 1], BF16))
        cEPS = E(nc.sbuf_tensor("cEPS", [128, 1], F32))
        cNH = E(nc.sbuf_tensor("cNH", [128, 1], F32))
        onesr = E(nc.sbuf_tensor("onesr", [1, 128], F32))
        sel = E(nc.sbuf_tensor("sel", [16, 16, 128], BF16))
        modT = E(nc.sbuf_tensor("modT", [128, 6 * NKT, NB], F32))
        fw8 = E(nc.sbuf_tensor("fw8", [128, NKT], F32))
        fb8 = E(nc.sbuf_tensor("fb8", [128, NKT], F32))
        ffb8 = E(nc.sbuf_tensor("ffb8", [128, NKT], F32))
        idb = E(nc.sbuf_tensor("idb", [128, 128], BF16))
        idf = E(nc.sbuf_tensor("idf", [128, 128], F32))
        Abuf = E(nc.sbuf_tensor("Abuf", [128, 256], F32))
        cwb = E(nc.sbuf_tensor("cwb", [128, NDT * 4], F32))
        cbb = E(nc.sbuf_tensor("cbb", [128, NDT], F32))
        dtbb = E(nc.sbuf_tensor("dtbb", [128, NDT], F32))
        dpb = E(nc.sbuf_tensor("dpb", [128, NDT], F32))
        nw8 = E(nc.sbuf_tensor("nw8", [128, NKT], F32))
        nb8 = E(nc.sbuf_tensor("nb8", [128, NKT], F32))
        wstr = E(nc.sbuf_tensor("wstr", [128, 2, NDT, 128], BF16))
        q32 = E(nc.sbuf_tensor("q32", [128, 32], F32))
        q32b = E(nc.sbuf_tensor("q32b", [128, 32], F32))
        ps8 = E(nc.psum_tensor("ps8", [128, 8, 512], F32))
        sv = E(nc.semaphore("sv"))
        ss = E(nc.semaphore("ss"))
        st = E(nc.semaphore("st"))
        sg = E(nc.semaphore("sg"))
        sy = E(nc.semaphore("sy"))
        dw = E(nc.semaphore("dw"))
        dr = E(nc.semaphore("dr"))
        dsto = E(nc.semaphore("dsto"))
        dsg = E(nc.semaphore("dsg"))
        dst0 = E(nc.semaphore("dst0"))
        dst1 = E(nc.semaphore("dst1"))
        dst2 = E(nc.semaphore("dst2"))
        dst3 = E(nc.semaphore("dst3"))
        dwa = E(nc.semaphore("dwa"))
        dwb = E(nc.semaphore("dwb"))

        V = Eng(nc.vector, sv, "V")
        S = Eng(nc.scalar, ss, "S")
        T = Eng(nc.tensor, st, "T")
        G = Eng(nc.gpsimd, sg, "G")
        Y = Eng(nc.sync, sy, "Y")
        cw_ = [0]
        cr_ = [0]
        cs_ = [0]
        c0_ = [0]
        c1_ = [0]
        c2_ = [0]
        c3_ = [0]
        ca_ = [0]
        cb_2 = [0]

        ARB = AR.bitcast(BF16)

        def fv(off, n):
            return AR[:, off:off + n]

        def bv(off, nbf):
            return ARB[:, 2 * off:2 * off + nbf]

        ld0 = fv(O_LD0, L)
        ld1 = fv(O_LD1, L)
        tmpA = fv(O_TMPA, L)
        tmpB = fv(O_TMPB, L)
        big0 = fv(O_BIG0, 4096)
        big1 = fv(O_BIG1, 4096)
        big0k = big0.rearrange("p (k c) -> p k c", k=NKT)
        big1k = big1.rearrange("p (k c) -> p k c", k=NKT)
        dtin = AR[0:DTR, O_TMPB:O_TMPB + CH]
        xhat = bv(O_P, 2 * 16384).rearrange("p (b k l) -> p b k l", b=NB, k=NKT)
        xpre = fv(O_P + 16384, L + 3)
        cv = fv(O_P + 18436, L)
        xpT = fv(O_P, NDT * 96).rearrange("p (t m) -> p t m", t=NDT)
        dtT = AR[0:DTR, O_P + 1536:O_P + 1536 + DIN]
        BCrowb = ARB[0:16, 2 * O_BIG0:2 * O_BIG0 + 2 * NB * L].rearrange(
            "p (b v l) -> p b v l", b=NB, v=2)
        dt_g = fv(O_P, GRP * L).rearrange("p (g l) -> p g l", g=GRP)
        dtx_g = fv(O_P + 8192, GRP * L).rearrange("p (g l) -> p g l", g=GRP)
        y_g = fv(O_P + 16384, GRP * L).rearrange("p (g l) -> p g l", g=GRP)
        dAb = fv(O_P + 24576, L)
        ub = fv(O_P + 26624, L)
        hbb = fv(O_P + 28672, L)
        gst = bv(O_P, NDT * CH).rearrange("p (t c) -> p t c", t=NDT)
        woutres = bv(O_P + 4096, NDT * D).rearrange("p (k m) -> p k m", k=NDT)
        wdT = bv(O_P, NKT * D).rearrange("p (k m) -> p k m", k=NKT)
        qa = fv(O_P + 4096, D)
        qb = fv(O_P + 5120, D)
        qc = fv(O_P + 6144, D)
        qd = fv(O_P + 7168, D)
        qeb = bv(O_P + 8192, D)
        x1buf = fv(O_P + 8704, NKT * CH).rearrange("p (k c) -> p k c", k=NKT)
        h2T = fv(O_P + 12800, 4 * D).rearrange("p (t d) -> p t d", t=4)
        xdT = bv(O_P + 16896, NKT * CH).rearrange("p (k c) -> p k c", k=NKT)

        def bar():
            nc.all_engine_barrier()

        _mark(nc, 'prep0')
        V.op('memset', ones1[:], 1.0)
        V.op('memset', ones1b[:], 1.0)
        V.op('memset', cEPS[:], EPS)
        V.op('memset', cNH[:], -0.5)
        V.op('memset', onesr[:], 1.0)
        io128 = E(nc.sbuf_tensor("io128", [128, 128], I32))
        G.op('iota', io128[:], [[-1, 128]], base=0, channel_multiplier=1)
        V.op('tensor_scalar', out=idb[:], in0=io128[:], scalar1=0, scalar2=None,
             op0=Alu.is_equal, waits=[(sg, G.n)])
        V.op('tensor_scalar', out=idf[:], in0=io128[:], scalar1=0, scalar2=None,
             op0=Alu.is_equal)
        G.op('iota', io128[0:16, :], [[0, 128]], base=0, channel_multiplier=1,
             waits=[(sv, V.n)])
        for n in range(16):
            V.op('tensor_scalar', out=sel[:, n, :], in0=io128[0:16, :], scalar1=n,
                 scalar2=None, op0=Alu.is_equal, waits=[(sg, G.n)] if n == 0 else [])
        bar()

        _mark(nc, 'prep1')
        jobs = []
        for li0 in range(NL):
            for kt in range(NKT):
                jobs.append((t_win, t_winb, (li0 * D + kt * 128) * 2 * DIN, 2 * DIN))
        for li0 in range(NL):
            for kt in range(NDT):
                jobs.append((t_wout, t_woutb, (li0 * DIN + kt * 128) * D, D))
        bigs = [big0, big1]
        csem = [dw, dr]
        ccnt = [cw_, cr_]
        csto = [[0], [0]]
        for j, (src, dst, off, w) in enumerate(jobs):
            bb = j % 2
            dv = Y.dma(bigs[bb][:, 0:w], AP(src.tensor, off, [[w, 128], [1, w]]),
                       csem[bb], ccnt[bb],
                       waits=[(sv, V.n), ((dsto, dsg)[bb], csto[bb][0])] if j >= 2 else [])
            V.op('tensor_copy', out=bigs[bb].bitcast(BF16)[:, 0:w], in_=bigs[bb][:, 0:w],
                 waits=[(csem[bb], dv)])
            Y.dma(AP(dst.tensor, off, [[w, 128], [1, w]]), bigs[bb].bitcast(BF16)[:, 0:w],
                   (dsto, dsg)[bb], csto[bb], waits=[(sv, V.n)])
        G.wait(dsto, csto[0][0])
        G.wait(dsg, csto[1][0])
        cs_[0] = csto[0][0]
        bar()

        _mark(nc, 'prep2')
        dv = Y.dma(big0[:, 0:NKT * NB].rearrange("p (k b) -> p k b", k=NKT),
                   t_cT.rearrange("(k p) b -> p k b", p=128), dst0, c0_)
        S.op('activation', out=big0[:, 0:NKT * NB], in_=big0[:, 0:NKT * NB], func=Act.Silu,
             waits=[(dst0, c0_[0])])
        for jc in range(12):
            bank = jc % 2
            for kt in range(NKT):
                dv = Y.dma(tmpA[:, 0:512],
                           AP(t_wmod.tensor, kt * 128 * 6 * D + jc * 512,
                              [[6 * D, 128], [1, 512]]),
                           dr, cr_, waits=[(st, T.n)])
                T.op('matmul', out=ps8[0:NB, bank, :], lhsT=big0[:, 2 * kt:2 * kt + 2],
                     rhs=tmpA[:, 0:512], start=(kt == 0), stop=(kt == NKT - 1),
                     waits=[(dr, dv), (ss, S.n)])
            S.op('copy', out=AR[0:NB, O_P + jc * 512:O_P + (jc + 1) * 512],
                 in_=ps8[0:NB, bank, :], waits=[(st, T.n)])
        for blk in range(48):
            T.op('transpose', out=ps8[:, blk % 2, 0:NB],
                 in_=AR[0:NB, O_P + blk * 128:O_P + (blk + 1) * 128],
                 identity=idf[0:NB, 0:NB],
                 waits=[(ss, S.n), (sv, V.n)])
            V.op('tensor_copy', out=modT[:, blk, :], in_=ps8[:, blk % 2, 0:NB],
                 waits=[(st, T.n)])
        for v0 in (1, 4):
            V.op('tensor_scalar', out=modT[:, v0 * NKT:(v0 + 1) * NKT, :],
                 in0=modT[:, v0 * NKT:(v0 + 1) * NKT, :], scalar1=1.0, scalar2=None,
                 op0=Alu.add)
        for v0 in (2, 5):
            gsl = modT[:, v0 * NKT:(v0 + 1) * NKT, :]
            nn = NKT * NB
            V.op('tensor_scalar', out=big0[:, 0:nn], in0=gsl, scalar1=0.0, scalar2=None,
                 op0=Alu.is_ge)
            V.op('tensor_scalar', out=big0[:, 0:nn], in0=big0[:, 0:nn], scalar1=2.0,
                 scalar2=-1.0, op0=Alu.mult, op1=Alu.add)
            S.op('activation', out=big1[:, 0:nn], in_=gsl, func=Act.Abs, waits=[(sv, V.n)])
            V.op('tensor_scalar', out=big1[:, 0:nn], in0=big1[:, 0:nn], scalar1=0.25,
                 scalar2=None, op0=Alu.max, waits=[(ss, S.n)])
            V.op('tensor_tensor', out=gsl, in0=big0[:, 0:nn], in1=big1[:, 0:nn], op=Alu.mult)
        Y.dma(fw8[:], t_fw, dst0, c0_)
        Y.dma(fb8[:], t_fb, dst0, c0_)
        Y.dma(ffb8[:], t_ffb, dst0, c0_)
        Y.wait(dst0, c0_[0])
        bar()

        ln_sq_t = [0]

        def ln_chunk(xk, extra_wait=(), par=0):
            bs0, bs1 = (0, 1) if par == 0 else (4, 5)
            br0, br1 = (2, 3) if par == 0 else (6, 7)
            xkb = ARB[:, 2 * O_LNSQ:2 * O_LNSQ + NKT * CH].rearrange(
                "p (k c) -> p k c", k=NKT)
            sq = ARB[:, 2 * O_LNSQ + NKT * CH:2 * O_LNSQ + 2 * NKT * CH]
            sqk = sq.rearrange("p (k c) -> p k c", k=NKT)
            so = O_SROW + par * 1536
            mu = AR[0:1, so:so + CH]
            r_ = AR[0:1, so + CH:so + 2 * CH]
            tm = AR[0:1, so + 2 * CH:so + 3 * CH]
            S.op('copy', out=xkb.rearrange("p k c -> p (k c)"),
                 in_=xk.rearrange("p k c -> p (k c)"),
                 waits=list(extra_wait) + [(st, ln_sq_t[0])])
            S.op('activation', out=sq, in_=xk.rearrange("p k c -> p (k c)"),
                 func=Act.Square)
            for kt in range(NKT):
                T.op('matmul', out=ps8[0:1, bs0, :], lhsT=ones1b[:], rhs=xkb[:, kt, :],
                     start=(kt == 0), stop=(kt == NKT - 1),
                     waits=[(ss, S.n - 1)] if kt == 0 else [])
            for kt in range(NKT):
                T.op('matmul', out=ps8[0:1, bs1, :], lhsT=ones1b[:], rhs=sqk[:, kt, :],
                     start=(kt == 0), stop=(kt == NKT - 1),
                     waits=[(ss, S.n)] if kt == 0 else [])
            ln_sq_t[0] = T.n
            V.op('tensor_scalar', out=mu, in0=ps8[0:1, bs0, :], scalar1=1.0 / D, scalar2=None,
                 op0=Alu.mult, waits=[(st, T.n)])
            V.op('tensor_scalar', out=tm, in0=ps8[0:1, bs1, :], scalar1=1.0 / D, scalar2=None,
                 op0=Alu.mult)
            V.op('tensor_tensor', out=r_, in0=mu, in1=mu, op=Alu.mult)
            V.op('tensor_tensor', out=tm, in0=tm, in1=r_, op=Alu.subtract)
            S.op('activation', out=tm, in_=tm, func=Act.Ln, bias=cEPS[0:1, :],
                 waits=[(sv, V.n)])
            S.op('activation', out=r_, in_=tm, func=Act.Exp, scale=cNH[0:1, :])
            T.op('matmul', out=ps8[:, br0, :], lhsT=onesr[:], rhs=mu, start=True,
                 stop=True, waits=[(sv, V.n)])
            T.op('matmul', out=ps8[:, br1, :], lhsT=onesr[:], rhs=r_, start=True,
                 stop=True, waits=[(ss, S.n)])
            V.op('tensor_tensor', out=xk, in0=xk,
                 in1=ps8[:, br0, :].unsqueeze(1).broadcast_to([128, NKT, CH]),
                 op=Alu.subtract, waits=[(st, T.n)])
            V.op('tensor_tensor', out=xk, in0=xk,
                 in1=ps8[:, br1, :].unsqueeze(1).broadcast_to([128, NKT, CH]), op=Alu.mult)

        def r_chunk_ap(b, ch):
            return t_R[b].transpose([1, 0, 2])[:, :, ch * CH:(ch + 1) * CH]

        def hs_chunk_ap(b, ch):
            return t_hsT[b].rearrange("(k p) l -> p k l", p=128)[:, :, ch * CH:(ch + 1) * CH]

        _mark(nc, 'stage1')
        bigks = (big0k, big1k)
        st_sem = ((dst0, c0_), (dst1, c1_))
        vbuf = [0, 0]
        for idx, (b, ch) in enumerate([(b, ch) for b in range(NB) for ch in range(NCH)]):
            par = idx % 2
            xbk = bigks[par]
            psem, pcnt = st_sem[par]
            dv = Y.dma(xbk, hs_chunk_ap(b, ch), dr, cr_,
                       waits=[(sv, vbuf[par]), (psem, pcnt[0])])
            ln_chunk(xbk, extra_wait=[(dr, dv)], par=par)
            for kt in range(NKT):
                V.op('tensor_scalar', out=xbk[:, kt, :], in0=xbk[:, kt, :],
                     scalar1=modT[:, 1 * NKT + kt, b:b+1], scalar2=modT[:, 0 * NKT + kt, b:b+1],
                     op0=Alu.mult, op1=Alu.add)
            vbuf[par] = V.n
            Y.dma(r_chunk_ap(b, ch), xbk, psem, pcnt, waits=[(sv, V.n)])
        G.wait(dst0, c0_[0])
        G.wait(dst1, c1_[0])
        bar()

        _mark(nc, 'loop_head')
        loop_sems = (sv, ss, st, sg, sy, dw, dr, dsto, dsg, dst0, dst1, dst2, dst3, dwa, dwb)
        for sem_ in loop_sems:
            nc.sync.sem_clear(sem_)
        bar()
        with nc.Fori(0, NL_RUN) as li:
            for e in (V, S, T, G, Y):
                e.reset()
            for c in (cw_, cr_, cs_, c0_, c1_, c2_, c3_, ca_, cb_2):
                c[0] = 0
            ln_sq_t[0] = 0

            cg_ = [0]
            with nc.sync.register("lof") as lof:
                nc.sync.reg_mul(lof, li, PW)
                Y.dma(t_pstg[:], AP(t_lay.tensor, lof, [[1, PW]]), dsg, cg_)
                WQ = D * 2 * DIN // 4
                for j in range(4):
                    nc.sync.reg_mul(lof, li, D * 2 * DIN)
                    nc.sync.reg_add(lof, lof, j * WQ)
                    Y.dma(AP(t_wsin.tensor, j * WQ, [[1, WQ]]),
                          AP(t_winb.tensor, lof, [[1, WQ]]), dsg, cg_)
                WO = DIN * D // 2
                for j in range(2):
                    nc.sync.reg_mul(lof, li, DIN * D)
                    nc.sync.reg_add(lof, lof, j * WO)
                    Y.dma(AP(t_wsout.tensor, j * WO, [[1, WO]]),
                          AP(t_woutb.tensor, lof, [[1, WO]]), dsg, cg_)
            STAGE_DONE = cg_[0]
            Y.wait(dsg, STAGE_DONE)
            Y.dma(Abuf[:], AP(t_pstg.tensor, PO_A, [[256, 128], [1, 256]]), dw, cw_)
            Y.dma(cwb[:], AP(t_pstg.tensor, PO_CW, [[64, 128], [1, 64]]), dw, cw_)
            Y.dma(cbb[:], AP(t_pstg.tensor, PO_CB, [[16, 128], [1, 16]]), dw, cw_)
            Y.dma(dtbb[:], AP(t_pstg.tensor, PO_DTB, [[16, 128], [1, 16]]), dw, cw_)
            Y.dma(dpb[:], AP(t_pstg.tensor, PO_DP, [[16, 128], [1, 16]]), dw, cw_)
            Y.dma(nw8[:], AP(t_pstg.tensor, PO_NW, [[8, 128], [1, 8]]), dw, cw_)
            Y.dma(nb8[:], AP(t_pstg.tensor, PO_NB, [[8, 128], [1, 8]]), dw, cw_)
            S.op('activation', out=Abuf[:], in_=Abuf[:], func=Act.Exp, waits=[(dw, cw_[0])])
            V.op('tensor_scalar', out=Abuf[:], in0=Abuf[:], scalar1=-1.0, scalar2=None,
                 op0=Alu.mult, waits=[(ss, S.n)])
            bar()

            _mark(nc, 'a1')
            vbuf = [0, 0]
            for idx, (b, ch) in enumerate([(b, ch) for b in range(NB) for ch in range(NCH)]):
                par = idx % 2
                xbk = bigks[par]
                dv = Y.dma(xbk, r_chunk_ap(b, ch), dr, cr_,
                           waits=[(sv, vbuf[par])])
                ln_chunk(xbk, extra_wait=[(dr, dv)], par=par)
                for kt in range(NKT):
                    V.op('tensor_scalar', out=xhat[:, b, kt, ch * CH:(ch + 1) * CH],
                         in0=xbk[:, kt, :], scalar1=nw8[:, kt:kt + 1],
                         scalar2=nb8[:, kt:kt + 1], op0=Alu.mult, op1=Alu.add)
                vbuf[par] = V.n
            bar()

            V.region = "a2"
            _mark(nc, 'a2')
            V.op('memset', xpre[:, 0:3], 0.0)
            if True:
                last_store = [0, 0]
                t_after_mt = [0, 0]
                s_bank = [0] * 8

                def wsin_ap(m):
                    return AP(t_wsin.tensor, m * 128,
                              [[2 * DIN, 128], [128 * 2 * DIN, NKT], [1, 128]])

                dvw_slot = [0, 0]
                for m0 in range(2):
                    wsem, wcnt = ((dwa, ca_), (dwb, cb_2))[m0]
                    dvw_slot[m0] = Y.dma(wstr[:, m0, 0:NKT, :], wsin_ap(m0),
                                         wsem, wcnt, waits=[(dsg, STAGE_DONE)])
                for mt in range(2 * NDT):
                    wb = mt % 2
                    wsem, wcnt = ((dwa, ca_), (dwb, cb_2))[wb]
                    dvw = dvw_slot[wb]
                    for kt in range(NKT):
                        for b in range(NB):
                            for ch in range(NCH):
                                w0 = []
                                if kt == 0:
                                    w0 = [(ss, s_bank[b * NCH + ch])]
                                    if b == 0 and ch == 0:
                                        w0.append((wsem, dvw))
                                T.op('matmul', out=ps8[:, b * NCH + ch, :],
                                     lhsT=wstr[:, wb, kt, :],
                                     rhs=xhat[:, b, kt, ch * CH:(ch + 1) * CH],
                                     start=(kt == 0), stop=(kt == NKT - 1), waits=w0)
                    t_after_mt[wb] = T.n
                    if mt < NDT:
                        for b in range(NB):
                            ob = (ld0, ld1)[b]
                            bsem, bcnt = ((dst0, c0_), (dst1, c1_))[b]
                            for ch in range(NCH):
                                S.op('copy', out=xpre[:, 3 + ch * CH:3 + (ch + 1) * CH],
                                     in_=ps8[:, b * NCH + ch, :],
                                     waits=[(st, T.n), (sv, V.n)] if ch == 0 else [])
                                s_bank[b * NCH + ch] = S.n
                            t4 = mt * 4
                            V.op('tensor_scalar', out=cv[:], in0=xpre[:, 0:L],
                                 scalar1=cwb[:, t4:t4 + 1], scalar2=None, op0=Alu.mult,
                                 waits=[(ss, S.n)])
                            for k in range(1, 4):
                                V.op('scalar_tensor_tensor', out=cv[:], in0=xpre[:, k:L + k],
                                     scalar=cwb[:, t4 + k:t4 + k + 1], in1=cv[:],
                                     op0=Alu.mult, op1=Alu.add)
                            S.op('activation', out=ob, in_=cv[:], func=Act.Silu,
                                 bias=cbb[:, mt:mt + 1],
                                 waits=[(sv, V.n), (bsem, bcnt[0])])
                            S.dma(t_xc[b, mt], ob, bsem, bcnt, waits=[(ss, S.n)])
                    else:
                        for b in range(NB):
                            for ch in range(NCH):
                                S.op('copy', out=big1[:, ch * CH:(ch + 1) * CH],
                                     in_=ps8[:, b * NCH + ch, :],
                                     waits=([(st, T.n), (dst2, c2_[0])]
                                            if ch == 0 else []))
                                s_bank[b * NCH + ch] = S.n
                            S.dma(t_z[b, mt - NDT], big1[:, 0:L], dst2, c2_,
                                  waits=[(ss, S.n)])
                    if mt + 2 < 2 * NDT:
                        dvw_slot[wb] = S.dma(wstr[:, wb, 0:NKT, :], wsin_ap(mt + 2),
                                             wsem, wcnt)
            G.wait(dst0, c0_[0])
            G.wait(dst1, c1_[0])
            G.wait(dst2, c2_[0])
            bar()

            V.region = "a3"
            _mark(nc, 'a3')
            dvx1 = Y.dma(xpT, AP(t_pstg.tensor, PO_XP,
                                 [[96, 128], [128 * 96, NDT], [1, 96]]), dw, cw_)
            dvx2 = Y.dma(dtT, AP(t_pstg.tensor, PO_DTW, [[DIN, DTR], [1, DIN]]), dw, cw_)
            xpTb = bv(O_P + 3584, NDT * 96).rearrange("p (t m) -> p t m", t=NDT)
            dtTb = ARB[0:DTR, 2 * (O_P + 4352):2 * (O_P + 4352) + DIN]
            V.op('tensor_copy', out=xpTb, in_=xpT, waits=[(dw, dvx2)])
            V.op('tensor_copy', out=dtTb, in_=dtT)
            dtinB = ARB[0:DTR, 2 * (O_TMPB + CH):2 * (O_TMPB + CH) + CH]
            lbF = (ld0[:, 0:CH], ld1[:, 0:CH], tmpA[:, 0:CH], tmpA[:, CH:2 * CH])
            lbB4 = (ARB[:, 2 * (O_LD0 + CH):2 * (O_LD0 + CH) + CH],
                    ARB[:, 2 * (O_LD1 + CH):2 * (O_LD1 + CH) + CH],
                    ARB[:, 2 * (O_TMPA + 2 * CH):2 * (O_TMPA + 2 * CH) + CH],
                    ARB[:, 2 * (O_TMPA + 2 * CH) + CH:2 * (O_TMPA + 2 * CH) + 2 * CH])
            chunks = [(b, ch) for b in range(NB) for ch in range(NCH)]
            NQ = len(chunks) * NDT

            def xc_ap(q):
                qb, qch = chunks[q // NDT]
                return t_xc[qb, q % NDT, :, qch * CH:(qch + 1) * CH]

            dvxq = {}
            t_slot = [0, 0, 0, 0]
            for q in range(4):
                dvxq[q] = Y.dma(lbF[q], xc_ap(q), dr, cr_)
            dt_bank_s = [0, 0, 0, 0]
            dt_sems = ((dst0, c0_), (dst1, c1_), (dst2, c2_), (dst3, c3_))
            for ci, (b, ch) in enumerate(chunks):
                for kt in range(NDT):
                    q = ci * NDT + kt
                    sl = q % 4
                    S.op('copy', out=lbB4[sl], in_=lbF[sl], waits=[(dr, dvxq[q])])
                    T.op('matmul', out=ps8[0:DTR, 0, :], lhsT=xpTb[:, kt, 0:DTR],
                         rhs=lbB4[sl], start=(kt == 0), stop=(kt == NDT - 1),
                         waits=[(ss, S.n)] +
                               ([(sv, V.n)] if kt == 0 else []))
                    T.op('matmul', out=ps8[0:16, 1, :], lhsT=xpTb[:, kt, DTR:DTR + 16],
                         rhs=lbB4[sl], start=(kt == 0), stop=(kt == NDT - 1))
                    T.op('matmul', out=ps8[0:16, 2, :], lhsT=xpTb[:, kt, DTR + 16:96],
                         rhs=lbB4[sl], start=(kt == 0), stop=(kt == NDT - 1))
                    t_slot[sl] = T.n
                    if q + 4 < NQ:
                        dvxq[q + 4] = Y.dma(lbF[sl], xc_ap(q + 4), dr, cr_,
                                            waits=[(st, t_slot[sl])])
                S.op('copy', out=dtinB, in_=ps8[0:DTR, 0, :], waits=[(st, T.n)])
                s_dtin = S.n
                V.op('tensor_copy', out=BCrowb[:, b, 0, ch * CH:(ch + 1) * CH],
                     in_=ps8[0:16, 1, :], waits=[(st, T.n)])
                V.op('tensor_copy', out=BCrowb[:, b, 1, ch * CH:(ch + 1) * CH],
                     in_=ps8[0:16, 2, :])
                for mt in range(NDT):
                    jf = mt % 4
                    bk = 3 + jf
                    hsem, hcnt = dt_sems[jf]
                    T.op('matmul', out=ps8[:, bk, :],
                         lhsT=dtTb[:, mt * 128:(mt + 1) * 128], rhs=dtinB,
                         start=True, stop=True,
                         waits=[(ss, max(s_dtin, dt_bank_s[jf]))])
                    S.op('copy', out=big1[:, jf * CH:(jf + 1) * CH],
                         in_=ps8[:, bk, :],
                         waits=[(st, T.n), (hsem, hcnt[0])])
                    dt_bank_s[jf] = S.n
                    S.dma(t_dt[b, mt, :, ch * CH:(ch + 1) * CH],
                          big1[:, jf * CH:(jf + 1) * CH], hsem, hcnt,
                          waits=[(ss, S.n)])
            G.wait(dst0, c0_[0])
            G.wait(dst1, c1_[0])
            G.wait(dst2, c2_[0])
            G.wait(dst3, c3_[0])
            bar()

            V.region = "bprep"
            _mark(nc, 'b')
            BrepB = tmpA.bitcast(BF16)[:, 0:L]
            CrepB = tmpA.bitcast(BF16)[:, L:2 * L]
            for b in range(NB):
                for g0 in range(0, NDT, GRP):
                    V.region = "bprep"
                    xcls = []
                    for ti in range(GRP):
                        t = g0 + ti
                        Y.dma(dt_g[:, ti, :], t_dt[b, t], dr, cr_,
                              waits=[(sv, V.n), (ss, S.n), (dst2, c2_[0]),
                                     (dst3, c3_[0])] if ti == 0 else [])
                        xcl = (ld0, ld1, tmpB, tmpA)[ti]
                        xcls.append(xcl)
                        Y.dma(xcl, t_xc[b, t], dr, cr_)
                    GL = cr_[0]
                    for ti in range(GRP):
                        t = g0 + ti
                        xcl = xcls[ti]
                        V.op('tensor_scalar', out=dt_g[:, ti, :], in0=dt_g[:, ti, :],
                             scalar1=dtbb[:, t:t + 1], scalar2=None, op0=Alu.add,
                             waits=[(dr, GL)])
                        S.op('activation', out=dAb[:], in_=dt_g[:, ti, :], func=Act.Relu,
                             waits=[(sv, V.n)])
                        V.op('scalar_tensor_tensor', out=ub[:], in0=dAb[:], scalar=-2.0,
                             in1=dt_g[:, ti, :], op0=Alu.mult, op1=Alu.add,
                             waits=[(ss, S.n)])
                        S.op('activation', out=ub[:], in_=ub[:], func=Act.Exp,
                             waits=[(sv, V.n)])
                        S.op('activation', out=ub[:], in_=ub[:], func=Act.Ln, bias=ones1[:])
                        V.op('tensor_tensor', out=dt_g[:, ti, :], in0=dAb[:], in1=ub[:],
                             op=Alu.add, waits=[(ss, S.n)])
                        V.op('tensor_tensor', out=dtx_g[:, ti, :], in0=dt_g[:, ti, :],
                             in1=xcl, op=Alu.mult)
                    V.region = "bscan"
                    vread_rep = [V.n, V.n]
                    vread_dA = [V.n, V.n]
                    for n in range(16):
                        np_ = n % 2
                        repbuf = (tmpA, ld0)[np_]
                        Bp = repbuf.bitcast(BF16)[:, 0:L]
                        Cp = repbuf.bitcast(BF16)[:, L:2 * L]
                        for cc in range(4):
                            T.op('matmul', out=ps8[:, cc, :], lhsT=sel[:, n, :],
                                 rhs=BCrowb[:, b, 0, cc * CH:(cc + 1) * CH], start=True,
                                 stop=True,
                                 waits=[(sv, vread_rep[np_]), (ss, S.n)] if cc == 0 else [])
                            T.op('matmul', out=ps8[:, 4 + cc, :], lhsT=sel[:, n, :],
                                 rhs=BCrowb[:, b, 1, cc * CH:(cc + 1) * CH], start=True,
                                 stop=True)
                        for cc in range(4):
                            S.op('copy', out=Bp[:, cc * CH:(cc + 1) * CH],
                                 in_=ps8[:, cc, :],
                                 waits=[(st, T.n)] if cc == 0 else [])
                            S.op('copy', out=Cp[:, cc * CH:(cc + 1) * CH],
                                 in_=ps8[:, 4 + cc, :])
                        s_rep = S.n
                        for ti in range(GRP):
                            t = g0 + ti
                            tp = ti % 2
                            dab = (dAb, tmpB)[tp]
                            S.op('activation', out=dab[:], in_=dt_g[:, ti, :], func=Act.Exp,
                                 scale=Abuf[:, t * 16 + n:t * 16 + n + 1],
                                 waits=[(sv, vread_dA[tp])])
                            s_exp = S.n
                            V.op('tensor_tensor', out=ub[:], in0=dtx_g[:, ti, :], in1=Bp,
                                 op=Alu.mult, waits=[(ss, s_rep)])
                            V.op('tensor_tensor_scan', out=hbb[:], data0=dab[:], data1=ub[:],
                                 initial=0.0, op0=Alu.mult, op1=Alu.add,
                                 waits=[(ss, s_exp)])
                            vread_dA[tp] = V.n
                            if n == 0:
                                V.op('tensor_tensor', out=y_g[:, ti, :], in0=hbb[:],
                                     in1=Cp, op=Alu.mult)
                            else:
                                V.op('tensor_tensor', out=ub[:], in0=hbb[:], in1=Cp,
                                     op=Alu.mult)
                                V.op('tensor_tensor', out=y_g[:, ti, :], in0=y_g[:, ti, :],
                                     in1=ub[:], op=Alu.add)
                            vread_rep[np_] = V.n
                    V.region = "bgate"
                    for ti in range(GRP):
                        t = g0 + ti
                        dv1 = Y.dma(ld0, t_z[b, t], dr, cr_, waits=[(ss, S.n), (sv, V.n)])
                        dv2 = Y.dma(ld1, t_xc[b, t], dr, cr_)
                        gob = (dAb, hbb)[ti % 2]
                        gsem, gcnt = ((dst2, c2_), (dst3, c3_))[ti % 2]
                        S.op('activation', out=tmpB[:], in_=ld0, func=Act.Silu,
                             waits=[(dr, dv2), (sv, V.n)])
                        V.op('scalar_tensor_tensor', out=y_g[:, ti, :], in0=ld1,
                             scalar=dpb[:, t:t + 1], in1=y_g[:, ti, :], op0=Alu.mult,
                             op1=Alu.add, waits=[(dr, dv2), (ss, S.n)])
                        V.op('tensor_tensor', out=gob.bitcast(BF16)[:, 0:L],
                             in0=y_g[:, ti, :], in1=tmpB[:], op=Alu.mult,
                             waits=[(gsem, gcnt[0])])
                        Y.dma(t_g[b, t], gob.bitcast(BF16)[:, 0:L], gsem, gcnt,
                              waits=[(sv, V.n)])
            G.wait(dst2, c2_[0])
            G.wait(dst3, c3_[0])
            bar()

            V.region = "c"
            _mark(nc, 'c')
            if True:
                dvwo = Y.dma(woutres, AP(t_wsout.tensor, 0,
                                         [[D, 128], [128 * D, NDT], [1, D]]),
                             dwa, ca_, waits=[(dsg, STAGE_DONE), (st, T.n), (sv, V.n)])
                for b in range(NB):
                    for ch in range(NCH):
                        dvg = Y.dma(gst, t_g[b].transpose([1, 0, 2])[:, :, ch * CH:(ch + 1) * CH],
                                    dr, cr_, waits=[(st, T.n), (dsto, cs_[0])])
                        dvr = Y.dma(big0k, r_chunk_ap(b, ch), dr, cr_,
                                    waits=[(sv, V.n)])
                        dvg = dvr
                        v_mt = [0, 0]
                        for mt in range(NKT):
                            wb = mt % 2
                            for kt in range(NDT):
                                T.op('matmul', out=ps8[:, 6 + wb, :],
                                     lhsT=woutres[:, kt, mt * 128:(mt + 1) * 128],
                                     rhs=gst[:, kt, :], start=(kt == 0), stop=(kt == NDT - 1),
                                     waits=[(dwa, dvwo), (dr, dvg), (sv, v_mt[wb])]
                                     if kt == 0 else [])
                            V.op('tensor_tensor', out=big0k[:, mt, :], in0=big0k[:, mt, :],
                                 in1=ps8[:, 6 + wb, :], op=Alu.add,
                                 waits=[(st, T.n), (dr, dvr)])
                            v_mt[wb] = V.n
                        Y.dma(r_chunk_ap(b, ch), big0k, dsto, cs_, waits=[(sv, V.n)])
            G.wait(dsto, cs_[0])

            bar()
            for sem_ in loop_sems:
                nc.sync.sem_clear(sem_)
            bar()

        for e in (V, S, T, G, Y):
            e.reset()
        for c in (cw_, cr_, cs_, c0_, c1_, c2_, c3_, ca_, cb_2):
            c[0] = 0
        ln_sq_t[0] = 0

        V.region = "s2"
        _mark(nc, 'wquant')
        def quant_tile(x_ap, out_bf, width, extra_wait=()):
            nbk = width // 32
            xb = x_ap.rearrange("p (k c) -> p k c", c=32)
            V.op('tensor_reduce', out=q32[:, 0:nbk], in_=xb, axis=mybir.AxisListType.X,
                 op=Alu.max, apply_absolute_value=True, waits=list(extra_wait))
            V.op('tensor_scalar', out=q32[:, 0:nbk], in0=q32[:, 0:nbk], scalar1=1e-35,
                 scalar2=None, op0=Alu.max)
            V.op('reciprocal', out=q32b[:, 0:nbk], in_=q32[:, 0:nbk])
            V.op('tensor_scalar', out=q32b[:, 0:nbk], in0=q32b[:, 0:nbk], scalar1=7.5,
                 scalar2=None, op0=Alu.mult)
            V.op('reciprocal', out=q32[:, 0:nbk], in_=q32b[:, 0:nbk])
            V.op('tensor_scalar', out=q32[:, 0:nbk], in0=q32[:, 0:nbk], scalar1=1e-30,
                 scalar2=None, op0=Alu.max)
            V.op('reciprocal', out=q32b[:, 0:nbk], in_=q32[:, 0:nbk])
            V.op('tensor_tensor', out=qa[:, 0:width].rearrange("p (k c) -> p k c", c=32),
                 in0=xb, in1=q32b[:, 0:nbk].unsqueeze(2).broadcast_to([128, nbk, 32]),
                 op=Alu.mult)
            S.op('activation', out=qb[:, 0:width], in_=qa[:, 0:width],
                 func=Act.Square, waits=[(sv, V.n)])
            V.op('tensor_scalar', out=qc[:, 0:width], in0=qb[:, 0:width], scalar1=4.0,
                 scalar2=None, op0=Alu.is_ge, waits=[(ss, S.n)])
            V.op('tensor_scalar', out=qd[:, 0:width], in0=qb[:, 0:width], scalar1=16.0,
                 scalar2=None, op0=Alu.is_ge)
            V.op('tensor_scalar', out=qb[:, 0:width], in0=qc[:, 0:width], scalar1=0.125,
                 scalar2=0.125, op0=Alu.mult, op1=Alu.add)
            V.op('scalar_tensor_tensor', out=qb[:, 0:width], in0=qd[:, 0:width],
                 scalar=0.25, in1=qb[:, 0:width], op0=Alu.mult, op1=Alu.add)
            V.op('tensor_scalar', out=qc[:, 0:width], in0=qc[:, 0:width], scalar1=-4.0,
                 scalar2=8.0, op0=Alu.mult, op1=Alu.add)
            V.op('scalar_tensor_tensor', out=qd[:, 0:width], in0=qd[:, 0:width],
                 scalar=-2.0, in1=qc[:, 0:width], op0=Alu.mult, op1=Alu.add)
            V.op('tensor_tensor', out=qc[:, 0:width], in0=qa[:, 0:width],
                 in1=qd[:, 0:width], op=Alu.mult)
            V.op('tensor_scalar', out=qc[:, 0:width], in0=qc[:, 0:width], scalar1=MAGIC,
                 scalar2=-MAGIC, op0=Alu.add, op1=Alu.add)
            V.op('tensor_tensor', out=qb[:, 0:width], in0=qc[:, 0:width],
                 in1=qb[:, 0:width], op=Alu.mult)
            V.op('tensor_tensor', out=out_bf.rearrange("p (k c) -> p k c", c=32),
                 in0=qb[:, 0:width].rearrange("p (k c) -> p k c", c=32),
                 in1=q32[:, 0:nbk].unsqueeze(2).broadcast_to([128, nbk, 32]), op=Alu.mult)

        for wt in range(NKT):
            dvq = Y.dma(big0[:, 0:D], t_ffw[wt * 128:(wt + 1) * 128, :], dr, cr_,
                        waits=[(sv, V.n), (st, T.n)])
            V.wait(dr, dvq)
            quant_tile(big0[:, 0:D], qeb, D)
            for kt in range(NKT):
                T.op('transpose', out=ps8.bitcast(BF16)[:, kt % 2, 0:128],
                     in_=qeb[:, kt * 128:(kt + 1) * 128], identity=idb[:],
                     waits=[(sv, V.n), (ss, S.n)])
                S.op('copy', out=wdT[:, kt, wt * 128:(wt + 1) * 128],
                     in_=ps8.bitcast(BF16)[:, kt % 2, 0:128], waits=[(st, T.n)])
        bar()

        _mark(nc, 'stage2')
        for b in range(NB):
            for ch in range(NCH):
                dv1 = Y.dma(big0k, r_chunk_ap(b, ch), dr, cr_,
                            waits=[(sv, V.n), (ss, S.n), (dsto, cs_[0])])
                dv2 = Y.dma(x1buf, hs_chunk_ap(b, ch), dr, cr_)
                ln_chunk(big0k, extra_wait=[(dr, dv2)])
                for kt in range(NKT):
                    V.op('tensor_scalar', out=big0k[:, kt, :], in0=big0k[:, kt, :],
                         scalar1=fw8[:, kt:kt + 1], scalar2=fb8[:, kt:kt + 1],
                         op0=Alu.mult, op1=Alu.add)
                for kt in range(NKT):
                    V.op('scalar_tensor_tensor', out=x1buf[:, kt, :], in0=big0k[:, kt, :],
                         scalar=modT[:, 2 * NKT + kt, b:b+1], in1=x1buf[:, kt, :],
                         op0=Alu.mult, op1=Alu.add,
                         waits=[(dr, dv2)] if kt == 0 else [])
                V.op('tensor_copy', out=big0[:, 0:NKT * CH],
                     in_=x1buf.rearrange("p k c -> p (k c)"))
                ln_chunk(big0k, extra_wait=[(sv, V.n)])
                for kt in range(NKT):
                    V.op('tensor_scalar', out=big0k[:, kt, :], in0=big0k[:, kt, :],
                         scalar1=modT[:, 4 * NKT + kt, b:b+1], scalar2=modT[:, 3 * NKT + kt, b:b+1],
                         op0=Alu.mult, op1=Alu.add)
                for tb in range(4):
                    for kt in range(NKT):
                        bk = 4 + ((tb * NKT + kt) % 2)
                        T.op('transpose', out=ps8[:, bk, 0:128],
                             in_=big0k[:, kt, tb * 128:(tb + 1) * 128], identity=idf[:],
                             waits=[(sv, V.n), (ss, S.n)])
                        S.op('copy', out=h2T[:, tb, kt * 128:(kt + 1) * 128],
                             in_=ps8[:, bk, 0:128], waits=[(st, T.n)])
                for tb in range(4):
                    quant_tile(h2T[:, tb, :], qeb, D, extra_wait=[(ss, S.n)])
                    for kt in range(NKT):
                        bk = 4 + ((tb * NKT + kt) % 2)
                        T.op('transpose', out=ps8.bitcast(BF16)[:, bk, 0:128],
                             in_=qeb[:, kt * 128:(kt + 1) * 128], identity=idb[:],
                             waits=[(sv, V.n), (ss, S.n)])
                        S.op('copy', out=xdT[:, kt, tb * 128:(tb + 1) * 128],
                             in_=ps8.bitcast(BF16)[:, bk, 0:128], waits=[(st, T.n)])
                for mt in range(NKT):
                    bk = mt % 2
                    for kt in range(NKT):
                        T.op('matmul', out=ps8[:, bk, :],
                             lhsT=wdT[:, kt, mt * 128:(mt + 1) * 128], rhs=xdT[:, kt, :],
                             start=(kt == 0), stop=(kt == NKT - 1),
                             waits=[(sv, V.n), (ss, S.n)] if kt == 0 else [])
                    S.op('activation', out=big0k[:, mt, :], in_=ps8[:, bk, :],
                         func=Act.Identity, bias=ffb8[:, mt:mt + 1], waits=[(st, T.n)])
                    V.op('scalar_tensor_tensor', out=big0k[:, mt, :], in0=big0k[:, mt, :],
                         scalar=modT[:, 5 * NKT + mt, b:b+1], in1=x1buf[:, mt, :],
                         op0=Alu.mult, op1=Alu.add, waits=[(ss, S.n)])
                Y.dma(t_out[b].rearrange("(k p) l -> p k l", p=128)
                      [:, :, ch * CH:(ch + 1) * CH], big0k, dsto, cs_,
                      waits=[(sv, V.n)])
        G.wait(dsto, cs_[0])
    _mark(nc, 'end')
    return nc


_PROG = None


def _get_prog():
    global _PROG
    if _PROG is None:
        _PROG = build_program()
    return _PROG


def _layout(inputs):
    f = np.ascontiguousarray
    hs = np.asarray(inputs["L0_attn_hidden_states"])
    c = np.asarray(inputs["L0_c"])

    def vecN(x):
        n = x.shape[-1] // 128
        return f(np.asarray(x, np.float32).reshape(x.shape[0], n, 128).transpose(0, 2, 1))

    def vec1(x):
        n = x.shape[-1] // 128
        return f(np.asarray(x, np.float32).reshape(n, 128).T)

    A_lay = (np.asarray(inputs["L0_attn_A_log"], np.float32)
             .reshape(NL, NDT, 128, 16).transpose(0, 2, 1, 3).reshape(NL, 128, 256))
    conv_w = (np.asarray(inputs["L0_attn_conv_w"], np.float32)
              .reshape(NL, NDT, 128, 4).transpose(0, 2, 1, 3).reshape(NL, 128, NDT * 4))
    lay = np.concatenate([
        A_lay.reshape(NL, -1),
        conv_w.reshape(NL, -1),
        vecN(inputs["L0_attn_conv_b"]).reshape(NL, -1),
        vecN(inputs["L0_attn_dt_proj_b"]).reshape(NL, -1),
        vecN(inputs["L0_attn_D_param"]).reshape(NL, -1),
        vecN(inputs["L0_attn_norm_w"]).reshape(NL, -1),
        vecN(inputs["L0_attn_norm_b"]).reshape(NL, -1),
        np.asarray(inputs["L0_attn_x_proj_w"], np.float32).transpose(0, 2, 1).reshape(NL, -1),
        np.asarray(inputs["L0_attn_dt_proj_w"], np.float32).transpose(0, 2, 1).reshape(NL, -1),
    ], axis=1)
    assert lay.shape[1] == PW, (lay.shape, PW)
    com = dict(
        wmod=f(np.asarray(inputs["L0_wmod"], np.float32)),
        in_projT=f(np.asarray(inputs["L0_attn_in_proj_w"], np.float32).transpose(0, 2, 1)),
        out_projT=f(np.asarray(inputs["L0_attn_out_proj_w"], np.float32).transpose(0, 2, 1)),
        lay_params=f(lay),
        fnorm_w8=vec1(inputs["L0_attn_fnorm_w"]),
        fnorm_b8=vec1(inputs["L0_attn_fnorm_b"]),
        ffn_w=f(np.asarray(inputs["L0_ffn_weight"], np.float32)),
        ffn_b8=vec1(inputs["L0_ffn_bias"]),
    )
    maps = []
    for core in range(8):
        b0 = core * NB
        m = dict(com)
        m["hsT"] = f(hs[b0:b0 + NB].transpose(0, 2, 1).astype(np.float32))
        m["cT"] = f(c[b0:b0 + NB].T.astype(np.float32))
        maps.append(m)
    return maps


def kernel(**inputs):
    nc = _get_prog()
    in_maps = _layout(inputs)
    res = run_bass_kernel_spmd(nc, in_maps, core_ids=list(range(8)))
    outs = [np.asarray(r["outT"]).transpose(0, 2, 1) for r in res.results]
    return np.ascontiguousarray(np.concatenate(outs, axis=0)).astype(np.float32)


# version 12
# speedup vs baseline: 1.2587x; 1.0031x over previous
import numpy as np
import concourse.bass as bass
import concourse.mybir as mybir
from concourse.bass_utils import run_bass_kernel_spmd
from concourse.bass import AP

F32 = mybir.dt.float32
BF16 = mybir.dt.bfloat16
I32 = mybir.dt.int32
Alu = mybir.AluOpType
Act = mybir.ActivationFunctionType

NB = 2
L = 2048
D = 1024
DIN = 2048
NL = 4
NCH = 4
CH = 512
NKT = D // 128
NDT = DIN // 128
DTR = 64
GRP = 4
EPS = 1e-5
MAGIC = float(3 << 22)
NL_RUN = 4
SELF_SYNC = frozenset(["V"])
V_SYNC_REGIONS = frozenset(["prep", "c", "s2"])

PO_A = 0
PO_CW = PO_A + 128 * 256
PO_CB = PO_CW + 128 * 64
PO_DTB = PO_CB + 128 * 16
PO_DP = PO_DTB + 128 * 16
PO_NW = PO_DP + 128 * 16
PO_NB = PO_NW + 128 * 8
PO_XP = PO_NB + 128 * 8
PO_DTW = PO_XP + DIN * 96
PW = PO_DTW + DTR * DIN

O_LD0 = 0
O_LD1 = 2048
O_TMPA = 4096
O_TMPB = 6144
O_BIG0 = 8192
O_BIG1 = 12288
O_P = 16384
O_LNSQ = O_P + 20992
O_SROW = O_P + 25088
ARW = O_P + 30720


class Eng:

    def __init__(self, eng, sem, tag=""):
        self.e, self.sem, self.n = eng, sem, 0
        self.tag = tag
        self.region = "prep"

    @property
    def selfsync(self):
        if self.tag != "V":
            return self.tag in SELF_SYNC
        return self.region in V_SYNC_REGIONS

    def op(self, name, *a, waits=(), **kw):
        for s, v in waits:
            self.e.wait_ge(s, v)
        if self.selfsync and self.n > 0:
            self.e.wait_ge(self.sem, self.n)
        r = getattr(self.e, name)(*a, **kw)
        r.then_inc(self.sem, 1)
        self.n += 1
        return r

    def dma(self, out, in_, dsem, dcnt, waits=(), **kw):
        for s, v in waits:
            self.e.wait_ge(s, v)
        if self.selfsync and self.n > 0:
            self.e.wait_ge(self.sem, self.n)
        r = self.e.dma_start(out=out, in_=in_, **kw)
        r.then_inc(dsem, 16)
        dcnt[0] += 16
        return dcnt[0]

    def wait(self, sem, v):
        self.e.wait_ge(sem, v)

    def reset(self):
        self.n = 0


PHASE_MARKS = []


def _mark(nc, name):
    PHASE_MARKS.append((name, sum(1 for _ in nc.all_instructions())))


def build_program():
    nc = bass.Bass("TRN2", target_bir_lowering=False,
                   detect_race_conditions=False)

    t_hsT = nc.dram_tensor("hsT", [NB, D, L], F32, kind="ExternalInput").ap()
    t_cT = nc.dram_tensor("cT", [D, NB], F32, kind="ExternalInput").ap()
    t_wmod = nc.dram_tensor("wmod", [D, 6 * D], F32, kind="ExternalInput").ap()
    t_win = nc.dram_tensor("in_projT", [NL, D, 2 * DIN], F32, kind="ExternalInput").ap()
    t_wout = nc.dram_tensor("out_projT", [NL, DIN, D], F32, kind="ExternalInput").ap()
    t_lay = nc.dram_tensor("lay_params", [NL, PW], F32, kind="ExternalInput").ap()
    t_fw = nc.dram_tensor("fnorm_w8", [128, NKT], F32, kind="ExternalInput").ap()
    t_fb = nc.dram_tensor("fnorm_b8", [128, NKT], F32, kind="ExternalInput").ap()
    t_ffw = nc.dram_tensor("ffn_w", [D, D], F32, kind="ExternalInput").ap()
    t_ffb = nc.dram_tensor("ffn_b8", [128, NKT], F32, kind="ExternalInput").ap()

    t_out = nc.dram_tensor("outT", [NB, D, L], F32, kind="ExternalOutput").ap()

    t_winb = nc.dram_tensor("win_bf16", [NL, D, 2 * DIN], BF16, kind="Internal").ap()
    t_woutb = nc.dram_tensor("wout_bf16", [NL, DIN, D], BF16, kind="Internal").ap()
    t_R = nc.dram_tensor("R_s", [NB, NKT, 128, L], F32, kind="Internal").ap()
    t_xc = nc.dram_tensor("xc_s", [NB, NDT, 128, L], F32, kind="Internal").ap()
    t_z = nc.dram_tensor("z_s", [NB, NDT, 128, L], F32, kind="Internal").ap()
    t_dt = nc.dram_tensor("dt_s", [NB, NDT, 128, L], F32, kind="Internal").ap()
    t_g = nc.dram_tensor("g_s", [NB, NDT, 128, L], BF16, kind="Internal").ap()
    t_wsin = nc.dram_tensor("wsin", [D, 2 * DIN], BF16, kind="Internal").ap()
    t_wsout = nc.dram_tensor("wsout", [DIN, D], BF16, kind="Internal").ap()
    t_pstg = nc.dram_tensor("pstg", [PW], F32, kind="Internal").ap()

    import contextlib
    with contextlib.ExitStack() as _es:
        E = _es.enter_context
        AR = E(nc.sbuf_tensor("AR", [128, ARW], F32))
        ones1 = E(nc.sbuf_tensor("ones1", [128, 1], F32))
        ones1b = E(nc.sbuf_tensor("ones1b", [128, 1], BF16))
        cEPS = E(nc.sbuf_tensor("cEPS", [128, 1], F32))
        cNH = E(nc.sbuf_tensor("cNH", [128, 1], F32))
        onesr = E(nc.sbuf_tensor("onesr", [1, 128], F32))
        sel = E(nc.sbuf_tensor("sel", [16, 16, 128], BF16))
        modT = E(nc.sbuf_tensor("modT", [128, 6 * NKT, NB], F32))
        fw8 = E(nc.sbuf_tensor("fw8", [128, NKT], F32))
        fb8 = E(nc.sbuf_tensor("fb8", [128, NKT], F32))
        ffb8 = E(nc.sbuf_tensor("ffb8", [128, NKT], F32))
        idb = E(nc.sbuf_tensor("idb", [128, 128], BF16))
        idf = E(nc.sbuf_tensor("idf", [128, 128], F32))
        Abuf = E(nc.sbuf_tensor("Abuf", [128, 256], F32))
        cwb = E(nc.sbuf_tensor("cwb", [128, NDT * 4], F32))
        cbb = E(nc.sbuf_tensor("cbb", [128, NDT], F32))
        dtbb = E(nc.sbuf_tensor("dtbb", [128, NDT], F32))
        dpb = E(nc.sbuf_tensor("dpb", [128, NDT], F32))
        nw8 = E(nc.sbuf_tensor("nw8", [128, NKT], F32))
        nb8 = E(nc.sbuf_tensor("nb8", [128, NKT], F32))
        wstr = E(nc.sbuf_tensor("wstr", [128, 2, NDT, 128], BF16))
        q32 = E(nc.sbuf_tensor("q32", [128, 32], F32))
        q32b = E(nc.sbuf_tensor("q32b", [128, 32], F32))
        ps8 = E(nc.psum_tensor("ps8", [128, 8, 512], F32))
        sv = E(nc.semaphore("sv"))
        ss = E(nc.semaphore("ss"))
        st = E(nc.semaphore("st"))
        sg = E(nc.semaphore("sg"))
        sy = E(nc.semaphore("sy"))
        dw = E(nc.semaphore("dw"))
        dr = E(nc.semaphore("dr"))
        dsto = E(nc.semaphore("dsto"))
        dsg = E(nc.semaphore("dsg"))
        dst0 = E(nc.semaphore("dst0"))
        dst1 = E(nc.semaphore("dst1"))
        dst2 = E(nc.semaphore("dst2"))
        dst3 = E(nc.semaphore("dst3"))
        dwa = E(nc.semaphore("dwa"))
        dwb = E(nc.semaphore("dwb"))

        V = Eng(nc.vector, sv, "V")
        S = Eng(nc.scalar, ss, "S")
        T = Eng(nc.tensor, st, "T")
        G = Eng(nc.gpsimd, sg, "G")
        Y = Eng(nc.sync, sy, "Y")
        cw_ = [0]
        cr_ = [0]
        cs_ = [0]
        c0_ = [0]
        c1_ = [0]
        c2_ = [0]
        c3_ = [0]
        ca_ = [0]
        cb_2 = [0]

        ARB = AR.bitcast(BF16)

        def fv(off, n):
            return AR[:, off:off + n]

        def bv(off, nbf):
            return ARB[:, 2 * off:2 * off + nbf]

        ld0 = fv(O_LD0, L)
        ld1 = fv(O_LD1, L)
        tmpA = fv(O_TMPA, L)
        tmpB = fv(O_TMPB, L)
        big0 = fv(O_BIG0, 4096)
        big1 = fv(O_BIG1, 4096)
        big0k = big0.rearrange("p (k c) -> p k c", k=NKT)
        big1k = big1.rearrange("p (k c) -> p k c", k=NKT)
        dtin = AR[0:DTR, O_TMPB:O_TMPB + CH]
        xhat = bv(O_P, 2 * 16384).rearrange("p (b k l) -> p b k l", b=NB, k=NKT)
        xpre = fv(O_P + 16384, L + 3)
        cv = fv(O_P + 18436, L)
        xpT = fv(O_P, NDT * 96).rearrange("p (t m) -> p t m", t=NDT)
        dtT = AR[0:DTR, O_P + 1536:O_P + 1536 + DIN]
        BCrowb = ARB[0:16, 2 * O_BIG0:2 * O_BIG0 + 2 * NB * L].rearrange(
            "p (b v l) -> p b v l", b=NB, v=2)
        dt_g = fv(O_P, GRP * L).rearrange("p (g l) -> p g l", g=GRP)
        dtx_g = fv(O_P + 8192, GRP * L).rearrange("p (g l) -> p g l", g=GRP)
        y_g = fv(O_P + 16384, GRP * L).rearrange("p (g l) -> p g l", g=GRP)
        dAb = fv(O_P + 24576, L)
        ub = fv(O_P + 26624, L)
        hbb = fv(O_P + 28672, L)
        gst = bv(O_P, NDT * CH).rearrange("p (t c) -> p t c", t=NDT)
        woutres = bv(O_P + 4096, NDT * D).rearrange("p (k m) -> p k m", k=NDT)
        wdT = bv(O_P, NKT * D).rearrange("p (k m) -> p k m", k=NKT)
        qa = fv(O_P + 4096, D)
        qb = fv(O_P + 5120, D)
        qc = fv(O_P + 6144, D)
        qd = fv(O_P + 7168, D)
        qeb = bv(O_P + 8192, D)
        x1buf = fv(O_P + 8704, NKT * CH).rearrange("p (k c) -> p k c", k=NKT)
        h2T = fv(O_P + 12800, 4 * D).rearrange("p (t d) -> p t d", t=4)
        xdT = bv(O_P + 16896, NKT * CH).rearrange("p (k c) -> p k c", k=NKT)

        def bar():
            nc.all_engine_barrier()

        _mark(nc, 'prep0')
        V.op('memset', ones1[:], 1.0)
        V.op('memset', ones1b[:], 1.0)
        V.op('memset', cEPS[:], EPS)
        V.op('memset', cNH[:], -0.5)
        V.op('memset', onesr[:], 1.0)
        io128 = E(nc.sbuf_tensor("io128", [128, 128], I32))
        G.op('iota', io128[:], [[-1, 128]], base=0, channel_multiplier=1)
        V.op('tensor_scalar', out=idb[:], in0=io128[:], scalar1=0, scalar2=None,
             op0=Alu.is_equal, waits=[(sg, G.n)])
        V.op('tensor_scalar', out=idf[:], in0=io128[:], scalar1=0, scalar2=None,
             op0=Alu.is_equal)
        G.op('iota', io128[0:16, :], [[0, 128]], base=0, channel_multiplier=1,
             waits=[(sv, V.n)])
        for n in range(16):
            V.op('tensor_scalar', out=sel[:, n, :], in0=io128[0:16, :], scalar1=n,
                 scalar2=None, op0=Alu.is_equal, waits=[(sg, G.n)] if n == 0 else [])
        bar()

        _mark(nc, 'prep1')
        jobs = []
        for li0 in range(NL):
            for kt in range(NKT):
                jobs.append((t_win, t_winb, (li0 * D + kt * 128) * 2 * DIN, 2 * DIN))
        for li0 in range(NL):
            for kt in range(NDT):
                jobs.append((t_wout, t_woutb, (li0 * DIN + kt * 128) * D, D))
        bigs = [big0, big1]
        csem = [dw, dr]
        ccnt = [cw_, cr_]
        csto = [[0], [0]]
        for j, (src, dst, off, w) in enumerate(jobs):
            bb = j % 2
            dv = Y.dma(bigs[bb][:, 0:w], AP(src.tensor, off, [[w, 128], [1, w]]),
                       csem[bb], ccnt[bb],
                       waits=[(sv, V.n), ((dsto, dsg)[bb], csto[bb][0])] if j >= 2 else [])
            V.op('tensor_copy', out=bigs[bb].bitcast(BF16)[:, 0:w], in_=bigs[bb][:, 0:w],
                 waits=[(csem[bb], dv)])
            Y.dma(AP(dst.tensor, off, [[w, 128], [1, w]]), bigs[bb].bitcast(BF16)[:, 0:w],
                   (dsto, dsg)[bb], csto[bb], waits=[(sv, V.n)])
        G.wait(dsto, csto[0][0])
        G.wait(dsg, csto[1][0])
        cs_[0] = csto[0][0]
        bar()

        _mark(nc, 'prep2')
        dv = Y.dma(big0[:, 0:NKT * NB].rearrange("p (k b) -> p k b", k=NKT),
                   t_cT.rearrange("(k p) b -> p k b", p=128), dst0, c0_)
        S.op('activation', out=big0[:, 0:NKT * NB], in_=big0[:, 0:NKT * NB], func=Act.Silu,
             waits=[(dst0, c0_[0])])
        for jc in range(12):
            bank = jc % 2
            for kt in range(NKT):
                dv = Y.dma(tmpA[:, 0:512],
                           AP(t_wmod.tensor, kt * 128 * 6 * D + jc * 512,
                              [[6 * D, 128], [1, 512]]),
                           dr, cr_, waits=[(st, T.n)])
                T.op('matmul', out=ps8[0:NB, bank, :], lhsT=big0[:, 2 * kt:2 * kt + 2],
                     rhs=tmpA[:, 0:512], start=(kt == 0), stop=(kt == NKT - 1),
                     waits=[(dr, dv), (ss, S.n)])
            S.op('copy', out=AR[0:NB, O_P + jc * 512:O_P + (jc + 1) * 512],
                 in_=ps8[0:NB, bank, :], waits=[(st, T.n)])
        for blk in range(48):
            T.op('transpose', out=ps8[:, blk % 2, 0:NB],
                 in_=AR[0:NB, O_P + blk * 128:O_P + (blk + 1) * 128],
                 identity=idf[0:NB, 0:NB],
                 waits=[(ss, S.n), (sv, V.n)])
            V.op('tensor_copy', out=modT[:, blk, :], in_=ps8[:, blk % 2, 0:NB],
                 waits=[(st, T.n)])
        for v0 in (1, 4):
            V.op('tensor_scalar', out=modT[:, v0 * NKT:(v0 + 1) * NKT, :],
                 in0=modT[:, v0 * NKT:(v0 + 1) * NKT, :], scalar1=1.0, scalar2=None,
                 op0=Alu.add)
        for v0 in (2, 5):
            gsl = modT[:, v0 * NKT:(v0 + 1) * NKT, :]
            nn = NKT * NB
            V.op('tensor_scalar', out=big0[:, 0:nn], in0=gsl, scalar1=0.0, scalar2=None,
                 op0=Alu.is_ge)
            V.op('tensor_scalar', out=big0[:, 0:nn], in0=big0[:, 0:nn], scalar1=2.0,
                 scalar2=-1.0, op0=Alu.mult, op1=Alu.add)
            S.op('activation', out=big1[:, 0:nn], in_=gsl, func=Act.Abs, waits=[(sv, V.n)])
            V.op('tensor_scalar', out=big1[:, 0:nn], in0=big1[:, 0:nn], scalar1=0.25,
                 scalar2=None, op0=Alu.max, waits=[(ss, S.n)])
            V.op('tensor_tensor', out=gsl, in0=big0[:, 0:nn], in1=big1[:, 0:nn], op=Alu.mult)
        Y.dma(fw8[:], t_fw, dst0, c0_)
        Y.dma(fb8[:], t_fb, dst0, c0_)
        Y.dma(ffb8[:], t_ffb, dst0, c0_)
        Y.wait(dst0, c0_[0])
        bar()

        ln_sq_t = [0]

        def ln_chunk(xk, extra_wait=(), par=0):
            bs0, bs1 = (0, 1) if par == 0 else (4, 5)
            br0, br1 = (2, 3) if par == 0 else (6, 7)
            xkb = ARB[:, 2 * O_LNSQ:2 * O_LNSQ + NKT * CH].rearrange(
                "p (k c) -> p k c", k=NKT)
            sq = ARB[:, 2 * O_LNSQ + NKT * CH:2 * O_LNSQ + 2 * NKT * CH]
            sqk = sq.rearrange("p (k c) -> p k c", k=NKT)
            so = O_SROW + par * 1536
            mu = AR[0:1, so:so + CH]
            r_ = AR[0:1, so + CH:so + 2 * CH]
            tm = AR[0:1, so + 2 * CH:so + 3 * CH]
            S.op('copy', out=xkb.rearrange("p k c -> p (k c)"),
                 in_=xk.rearrange("p k c -> p (k c)"),
                 waits=list(extra_wait) + [(st, ln_sq_t[0])])
            S.op('activation', out=sq, in_=xk.rearrange("p k c -> p (k c)"),
                 func=Act.Square)
            for kt in range(NKT):
                T.op('matmul', out=ps8[0:1, bs0, :], lhsT=ones1b[:], rhs=xkb[:, kt, :],
                     start=(kt == 0), stop=(kt == NKT - 1),
                     waits=[(ss, S.n - 1)] if kt == 0 else [])
            for kt in range(NKT):
                T.op('matmul', out=ps8[0:1, bs1, :], lhsT=ones1b[:], rhs=sqk[:, kt, :],
                     start=(kt == 0), stop=(kt == NKT - 1),
                     waits=[(ss, S.n)] if kt == 0 else [])
            ln_sq_t[0] = T.n
            V.op('tensor_scalar', out=mu, in0=ps8[0:1, bs0, :], scalar1=1.0 / D, scalar2=None,
                 op0=Alu.mult, waits=[(st, T.n)])
            V.op('tensor_scalar', out=tm, in0=ps8[0:1, bs1, :], scalar1=1.0 / D, scalar2=None,
                 op0=Alu.mult)
            V.op('tensor_tensor', out=r_, in0=mu, in1=mu, op=Alu.mult)
            V.op('tensor_tensor', out=tm, in0=tm, in1=r_, op=Alu.subtract)
            S.op('activation', out=tm, in_=tm, func=Act.Ln, bias=cEPS[0:1, :],
                 waits=[(sv, V.n)])
            S.op('activation', out=r_, in_=tm, func=Act.Exp, scale=cNH[0:1, :])
            T.op('matmul', out=ps8[:, br0, :], lhsT=onesr[:], rhs=mu, start=True,
                 stop=True, waits=[(sv, V.n)])
            T.op('matmul', out=ps8[:, br1, :], lhsT=onesr[:], rhs=r_, start=True,
                 stop=True, waits=[(ss, S.n)])
            V.op('tensor_tensor', out=xk, in0=xk,
                 in1=ps8[:, br0, :].unsqueeze(1).broadcast_to([128, NKT, CH]),
                 op=Alu.subtract, waits=[(st, T.n)])
            V.op('tensor_tensor', out=xk, in0=xk,
                 in1=ps8[:, br1, :].unsqueeze(1).broadcast_to([128, NKT, CH]), op=Alu.mult)

        def r_chunk_ap(b, ch):
            return t_R[b].transpose([1, 0, 2])[:, :, ch * CH:(ch + 1) * CH]

        def hs_chunk_ap(b, ch):
            return t_hsT[b].rearrange("(k p) l -> p k l", p=128)[:, :, ch * CH:(ch + 1) * CH]

        _mark(nc, 'stage1')
        bigks = (big0k, big1k)
        st_sem = ((dst0, c0_), (dst1, c1_))
        vbuf = [0, 0]
        for idx, (b, ch) in enumerate([(b, ch) for b in range(NB) for ch in range(NCH)]):
            par = idx % 2
            xbk = bigks[par]
            psem, pcnt = st_sem[par]
            dv = Y.dma(xbk, hs_chunk_ap(b, ch), dr, cr_,
                       waits=[(sv, vbuf[par]), (psem, pcnt[0])])
            ln_chunk(xbk, extra_wait=[(dr, dv)], par=par)
            for kt in range(NKT):
                V.op('tensor_scalar', out=xbk[:, kt, :], in0=xbk[:, kt, :],
                     scalar1=modT[:, 1 * NKT + kt, b:b+1], scalar2=modT[:, 0 * NKT + kt, b:b+1],
                     op0=Alu.mult, op1=Alu.add)
            vbuf[par] = V.n
            Y.dma(r_chunk_ap(b, ch), xbk, psem, pcnt, waits=[(sv, V.n)])
        G.wait(dst0, c0_[0])
        G.wait(dst1, c1_[0])
        bar()

        _mark(nc, 'loop_head')
        loop_sems = (sv, ss, st, sg, sy, dw, dr, dsto, dsg, dst0, dst1, dst2, dst3, dwa, dwb)
        for sem_ in loop_sems:
            nc.sync.sem_clear(sem_)
        bar()
        with nc.Fori(0, NL_RUN) as li:
            for e in (V, S, T, G, Y):
                e.reset()
            for c in (cw_, cr_, cs_, c0_, c1_, c2_, c3_, ca_, cb_2):
                c[0] = 0
            ln_sq_t[0] = 0

            cg_ = [0]
            with nc.sync.register("lof") as lof:
                nc.sync.reg_mul(lof, li, PW)
                Y.dma(t_pstg[:], AP(t_lay.tensor, lof, [[1, PW]]), dsg, cg_)
                WQ = D * 2 * DIN // 4
                for j in range(4):
                    nc.sync.reg_mul(lof, li, D * 2 * DIN)
                    nc.sync.reg_add(lof, lof, j * WQ)
                    Y.dma(AP(t_wsin.tensor, j * WQ, [[1, WQ]]),
                          AP(t_winb.tensor, lof, [[1, WQ]]), dsg, cg_)
                WO = DIN * D // 2
                for j in range(2):
                    nc.sync.reg_mul(lof, li, DIN * D)
                    nc.sync.reg_add(lof, lof, j * WO)
                    Y.dma(AP(t_wsout.tensor, j * WO, [[1, WO]]),
                          AP(t_woutb.tensor, lof, [[1, WO]]), dsg, cg_)
            STAGE_DONE = cg_[0]
            Y.wait(dsg, STAGE_DONE)
            Y.dma(Abuf[:], AP(t_pstg.tensor, PO_A, [[256, 128], [1, 256]]), dw, cw_)
            Y.dma(cwb[:], AP(t_pstg.tensor, PO_CW, [[64, 128], [1, 64]]), dw, cw_)
            Y.dma(cbb[:], AP(t_pstg.tensor, PO_CB, [[16, 128], [1, 16]]), dw, cw_)
            Y.dma(dtbb[:], AP(t_pstg.tensor, PO_DTB, [[16, 128], [1, 16]]), dw, cw_)
            Y.dma(dpb[:], AP(t_pstg.tensor, PO_DP, [[16, 128], [1, 16]]), dw, cw_)
            Y.dma(nw8[:], AP(t_pstg.tensor, PO_NW, [[8, 128], [1, 8]]), dw, cw_)
            Y.dma(nb8[:], AP(t_pstg.tensor, PO_NB, [[8, 128], [1, 8]]), dw, cw_)
            S.op('activation', out=Abuf[:], in_=Abuf[:], func=Act.Exp, waits=[(dw, cw_[0])])
            V.op('tensor_scalar', out=Abuf[:], in0=Abuf[:], scalar1=-1.0, scalar2=None,
                 op0=Alu.mult, waits=[(ss, S.n)])
            bar()

            _mark(nc, 'a1')
            vbuf = [0, 0]
            for idx, (b, ch) in enumerate([(b, ch) for b in range(NB) for ch in range(NCH)]):
                par = idx % 2
                xbk = bigks[par]
                dv = Y.dma(xbk, r_chunk_ap(b, ch), dr, cr_,
                           waits=[(sv, vbuf[par])])
                ln_chunk(xbk, extra_wait=[(dr, dv)], par=par)
                for kt in range(NKT):
                    V.op('tensor_scalar', out=xhat[:, b, kt, ch * CH:(ch + 1) * CH],
                         in0=xbk[:, kt, :], scalar1=nw8[:, kt:kt + 1],
                         scalar2=nb8[:, kt:kt + 1], op0=Alu.mult, op1=Alu.add)
                vbuf[par] = V.n
            bar()

            V.region = "a2"
            _mark(nc, 'a2')
            V.op('memset', xpre[:, 0:3], 0.0)
            if True:
                last_store = [0, 0]
                t_after_mt = [0, 0]
                s_bank = [0] * 8

                def wsin_ap(m):
                    return AP(t_wsin.tensor, m * 128,
                              [[2 * DIN, 128], [128 * 2 * DIN, NKT], [1, 128]])

                dvw_slot = [0, 0]
                for m0 in range(2):
                    wsem, wcnt = ((dwa, ca_), (dwb, cb_2))[m0]
                    dvw_slot[m0] = Y.dma(wstr[:, m0, 0:NKT, :], wsin_ap(m0),
                                         wsem, wcnt, waits=[(dsg, STAGE_DONE)])
                for mt in range(2 * NDT):
                    wb = mt % 2
                    wsem, wcnt = ((dwa, ca_), (dwb, cb_2))[wb]
                    dvw = dvw_slot[wb]
                    for kt in range(NKT):
                        for b in range(NB):
                            for ch in range(NCH):
                                w0 = []
                                if kt == 0:
                                    w0 = [(ss, s_bank[b * NCH + ch])]
                                    if b == 0 and ch == 0:
                                        w0.append((wsem, dvw))
                                T.op('matmul', out=ps8[:, b * NCH + ch, :],
                                     lhsT=wstr[:, wb, kt, :],
                                     rhs=xhat[:, b, kt, ch * CH:(ch + 1) * CH],
                                     start=(kt == 0), stop=(kt == NKT - 1), waits=w0)
                    t_after_mt[wb] = T.n
                    if mt < NDT:
                        for b in range(NB):
                            ob = (ld0, ld1)[b]
                            bsem, bcnt = ((dst0, c0_), (dst1, c1_))[b]
                            for ch in range(NCH):
                                S.op('copy', out=xpre[:, 3 + ch * CH:3 + (ch + 1) * CH],
                                     in_=ps8[:, b * NCH + ch, :],
                                     waits=[(st, T.n), (sv, V.n)] if ch == 0 else [])
                                s_bank[b * NCH + ch] = S.n
                            t4 = mt * 4
                            V.op('tensor_scalar', out=cv[:], in0=xpre[:, 0:L],
                                 scalar1=cwb[:, t4:t4 + 1], scalar2=None, op0=Alu.mult,
                                 waits=[(ss, S.n)])
                            for k in range(1, 4):
                                V.op('scalar_tensor_tensor', out=cv[:], in0=xpre[:, k:L + k],
                                     scalar=cwb[:, t4 + k:t4 + k + 1], in1=cv[:],
                                     op0=Alu.mult, op1=Alu.add)
                            S.op('activation', out=ob, in_=cv[:], func=Act.Silu,
                                 bias=cbb[:, mt:mt + 1],
                                 waits=[(sv, V.n), (bsem, bcnt[0])])
                            S.dma(t_xc[b, mt], ob, bsem, bcnt, waits=[(ss, S.n)])
                    else:
                        for b in range(NB):
                            for ch in range(NCH):
                                S.op('copy', out=big1[:, ch * CH:(ch + 1) * CH],
                                     in_=ps8[:, b * NCH + ch, :],
                                     waits=([(st, T.n), (dst2, c2_[0])]
                                            if ch == 0 else []))
                                s_bank[b * NCH + ch] = S.n
                            S.dma(t_z[b, mt - NDT], big1[:, 0:L], dst2, c2_,
                                  waits=[(ss, S.n)])
                    if mt + 2 < 2 * NDT:
                        dvw_slot[wb] = S.dma(wstr[:, wb, 0:NKT, :], wsin_ap(mt + 2),
                                             wsem, wcnt)
            G.wait(dst0, c0_[0])
            G.wait(dst1, c1_[0])
            G.wait(dst2, c2_[0])
            bar()

            V.region = "a3"
            _mark(nc, 'a3')
            dvx1 = Y.dma(xpT, AP(t_pstg.tensor, PO_XP,
                                 [[96, 128], [128 * 96, NDT], [1, 96]]), dw, cw_)
            dvx2 = Y.dma(dtT, AP(t_pstg.tensor, PO_DTW, [[DIN, DTR], [1, DIN]]), dw, cw_)
            xpTb = bv(O_P + 3584, NDT * 96).rearrange("p (t m) -> p t m", t=NDT)
            dtTb = ARB[0:DTR, 2 * (O_P + 4352):2 * (O_P + 4352) + DIN]
            V.op('tensor_copy', out=xpTb, in_=xpT, waits=[(dw, dvx2)])
            V.op('tensor_copy', out=dtTb, in_=dtT)
            dtinB = ARB[0:DTR, 2 * (O_TMPB + CH):2 * (O_TMPB + CH) + CH]
            lbF = (ld0[:, 0:CH], ld1[:, 0:CH], tmpA[:, 0:CH], tmpA[:, CH:2 * CH])
            lbB4 = (ARB[:, 2 * (O_LD0 + CH):2 * (O_LD0 + CH) + CH],
                    ARB[:, 2 * (O_LD1 + CH):2 * (O_LD1 + CH) + CH],
                    ARB[:, 2 * (O_TMPA + 2 * CH):2 * (O_TMPA + 2 * CH) + CH],
                    ARB[:, 2 * (O_TMPA + 2 * CH) + CH:2 * (O_TMPA + 2 * CH) + 2 * CH])
            chunks = [(b, ch) for b in range(NB) for ch in range(NCH)]
            NQ = len(chunks) * NDT

            def xc_ap(q):
                qb, qch = chunks[q // NDT]
                return t_xc[qb, q % NDT, :, qch * CH:(qch + 1) * CH]

            dvxq = {}
            t_slot = [0, 0, 0, 0]
            for q in range(4):
                dvxq[q] = Y.dma(lbF[q], xc_ap(q), dr, cr_)
            dt_bank_s = [0, 0, 0, 0]
            dt_sems = ((dst0, c0_), (dst1, c1_), (dst2, c2_), (dst3, c3_))
            for ci, (b, ch) in enumerate(chunks):
                for kt in range(NDT):
                    q = ci * NDT + kt
                    sl = q % 4
                    S.op('copy', out=lbB4[sl], in_=lbF[sl], waits=[(dr, dvxq[q])])
                    T.op('matmul', out=ps8[0:DTR, 0, :], lhsT=xpTb[:, kt, 0:DTR],
                         rhs=lbB4[sl], start=(kt == 0), stop=(kt == NDT - 1),
                         waits=[(ss, S.n)] +
                               ([(sv, V.n)] if kt == 0 else []))
                    T.op('matmul', out=ps8[0:16, 1, :], lhsT=xpTb[:, kt, DTR:DTR + 16],
                         rhs=lbB4[sl], start=(kt == 0), stop=(kt == NDT - 1))
                    T.op('matmul', out=ps8[0:16, 2, :], lhsT=xpTb[:, kt, DTR + 16:96],
                         rhs=lbB4[sl], start=(kt == 0), stop=(kt == NDT - 1))
                    t_slot[sl] = T.n
                    if q + 4 < NQ:
                        dvxq[q + 4] = Y.dma(lbF[sl], xc_ap(q + 4), dr, cr_,
                                            waits=[(st, t_slot[sl])])
                S.op('copy', out=dtinB, in_=ps8[0:DTR, 0, :], waits=[(st, T.n)])
                s_dtin = S.n
                V.op('tensor_copy', out=BCrowb[:, b, 0, ch * CH:(ch + 1) * CH],
                     in_=ps8[0:16, 1, :], waits=[(st, T.n)])
                V.op('tensor_copy', out=BCrowb[:, b, 1, ch * CH:(ch + 1) * CH],
                     in_=ps8[0:16, 2, :])
                for mt in range(NDT):
                    jf = mt % 4
                    bk = 3 + jf
                    hsem, hcnt = dt_sems[jf]
                    T.op('matmul', out=ps8[:, bk, :],
                         lhsT=dtTb[:, mt * 128:(mt + 1) * 128], rhs=dtinB,
                         start=True, stop=True,
                         waits=[(ss, max(s_dtin, dt_bank_s[jf]))])
                    S.op('copy', out=big1[:, jf * CH:(jf + 1) * CH],
                         in_=ps8[:, bk, :],
                         waits=[(st, T.n), (hsem, hcnt[0])])
                    dt_bank_s[jf] = S.n
                    S.dma(t_dt[b, mt, :, ch * CH:(ch + 1) * CH],
                          big1[:, jf * CH:(jf + 1) * CH], hsem, hcnt,
                          waits=[(ss, S.n)])
            G.wait(dst0, c0_[0])
            G.wait(dst1, c1_[0])
            G.wait(dst2, c2_[0])
            G.wait(dst3, c3_[0])
            bar()

            V.region = "bprep"
            _mark(nc, 'b')
            BrepB = tmpA.bitcast(BF16)[:, 0:L]
            CrepB = tmpA.bitcast(BF16)[:, L:2 * L]
            for b in range(NB):
                for g0 in range(0, NDT, GRP):
                    V.region = "bprep"
                    xcls = []
                    for ti in range(GRP):
                        t = g0 + ti
                        Y.dma(dt_g[:, ti, :], t_dt[b, t], dr, cr_,
                              waits=[(sv, V.n), (ss, S.n), (dst2, c2_[0]),
                                     (dst3, c3_[0])] if ti == 0 else [])
                        xcl = (ld0, ld1, tmpB, tmpA)[ti]
                        xcls.append(xcl)
                        Y.dma(xcl, t_xc[b, t], dr, cr_)
                    GL = cr_[0]
                    for ti in range(GRP):
                        t = g0 + ti
                        xcl = xcls[ti]
                        V.op('tensor_scalar', out=dt_g[:, ti, :], in0=dt_g[:, ti, :],
                             scalar1=dtbb[:, t:t + 1], scalar2=None, op0=Alu.add,
                             waits=[(dr, GL)])
                        S.op('activation', out=dAb[:], in_=dt_g[:, ti, :], func=Act.Relu,
                             waits=[(sv, V.n)])
                        V.op('scalar_tensor_tensor', out=ub[:], in0=dAb[:], scalar=-2.0,
                             in1=dt_g[:, ti, :], op0=Alu.mult, op1=Alu.add,
                             waits=[(ss, S.n)])
                        S.op('activation', out=ub[:], in_=ub[:], func=Act.Exp,
                             waits=[(sv, V.n)])
                        S.op('activation', out=ub[:], in_=ub[:], func=Act.Ln, bias=ones1[:])
                        V.op('tensor_tensor', out=dt_g[:, ti, :], in0=dAb[:], in1=ub[:],
                             op=Alu.add, waits=[(ss, S.n)])
                        V.op('tensor_tensor', out=dtx_g[:, ti, :], in0=dt_g[:, ti, :],
                             in1=xcl, op=Alu.mult)
                    V.region = "bscan"
                    vread_rep = [V.n, V.n]
                    vread_dA = [V.n, V.n]
                    for n in range(16):
                        np_ = n % 2
                        repbuf = (tmpA, ld0)[np_]
                        Bp = repbuf.bitcast(BF16)[:, 0:L]
                        Cp = repbuf.bitcast(BF16)[:, L:2 * L]
                        for cc in range(4):
                            T.op('matmul', out=ps8[:, cc, :], lhsT=sel[:, n, :],
                                 rhs=BCrowb[:, b, 0, cc * CH:(cc + 1) * CH], start=True,
                                 stop=True,
                                 waits=[(sv, vread_rep[np_]), (ss, S.n)] if cc == 0 else [])
                            T.op('matmul', out=ps8[:, 4 + cc, :], lhsT=sel[:, n, :],
                                 rhs=BCrowb[:, b, 1, cc * CH:(cc + 1) * CH], start=True,
                                 stop=True)
                        for cc in range(4):
                            S.op('copy', out=Bp[:, cc * CH:(cc + 1) * CH],
                                 in_=ps8[:, cc, :],
                                 waits=[(st, T.n)] if cc == 0 else [])
                            S.op('copy', out=Cp[:, cc * CH:(cc + 1) * CH],
                                 in_=ps8[:, 4 + cc, :])
                        s_rep = S.n
                        for ti in range(GRP):
                            t = g0 + ti
                            tp = ti % 2
                            dab = (dAb, tmpB)[tp]
                            S.op('activation', out=dab[:], in_=dt_g[:, ti, :], func=Act.Exp,
                                 scale=Abuf[:, t * 16 + n:t * 16 + n + 1],
                                 waits=[(sv, vread_dA[tp])])
                            s_exp = S.n
                            V.op('tensor_tensor', out=ub[:], in0=dtx_g[:, ti, :], in1=Bp,
                                 op=Alu.mult, waits=[(ss, s_rep)])
                            V.op('tensor_tensor_scan', out=hbb[:], data0=dab[:], data1=ub[:],
                                 initial=0.0, op0=Alu.mult, op1=Alu.add,
                                 waits=[(ss, s_exp)])
                            vread_dA[tp] = V.n
                            if n == 0:
                                V.op('tensor_tensor', out=y_g[:, ti, :], in0=hbb[:],
                                     in1=Cp, op=Alu.mult)
                            else:
                                V.op('tensor_tensor', out=ub[:], in0=hbb[:], in1=Cp,
                                     op=Alu.mult)
                                V.op('tensor_tensor', out=y_g[:, ti, :], in0=y_g[:, ti, :],
                                     in1=ub[:], op=Alu.add)
                            vread_rep[np_] = V.n
                    V.region = "bgate"
                    for ti in range(GRP):
                        t = g0 + ti
                        dv1 = Y.dma(ld0, t_z[b, t], dr, cr_, waits=[(ss, S.n), (sv, V.n)])
                        dv2 = Y.dma(ld1, t_xc[b, t], dr, cr_)
                        gob = (dAb, hbb)[ti % 2]
                        gsem, gcnt = ((dst2, c2_), (dst3, c3_))[ti % 2]
                        S.op('activation', out=tmpB[:], in_=ld0, func=Act.Silu,
                             waits=[(dr, dv2), (sv, V.n)])
                        V.op('scalar_tensor_tensor', out=y_g[:, ti, :], in0=ld1,
                             scalar=dpb[:, t:t + 1], in1=y_g[:, ti, :], op0=Alu.mult,
                             op1=Alu.add, waits=[(dr, dv2), (ss, S.n)])
                        V.op('tensor_tensor', out=gob.bitcast(BF16)[:, 0:L],
                             in0=y_g[:, ti, :], in1=tmpB[:], op=Alu.mult,
                             waits=[(gsem, gcnt[0])])
                        Y.dma(t_g[b, t], gob.bitcast(BF16)[:, 0:L], gsem, gcnt,
                              waits=[(sv, V.n)])
            G.wait(dst2, c2_[0])
            G.wait(dst3, c3_[0])
            bar()

            V.region = "c"
            _mark(nc, 'c')
            if True:
                dvwo = Y.dma(woutres, AP(t_wsout.tensor, 0,
                                         [[D, 128], [128 * D, NDT], [1, D]]),
                             dwa, ca_, waits=[(dsg, STAGE_DONE), (st, T.n), (sv, V.n)])
                for b in range(NB):
                    for ch in range(NCH):
                        dvg = Y.dma(gst, t_g[b].transpose([1, 0, 2])[:, :, ch * CH:(ch + 1) * CH],
                                    dr, cr_, waits=[(st, T.n), (dsto, cs_[0])])
                        dvr = Y.dma(big0k, r_chunk_ap(b, ch), dr, cr_,
                                    waits=[(sv, V.n)])
                        dvg = dvr
                        v_mt = [0, 0]
                        for mt in range(NKT):
                            wb = mt % 2
                            for kt in range(NDT):
                                T.op('matmul', out=ps8[:, 6 + wb, :],
                                     lhsT=woutres[:, kt, mt * 128:(mt + 1) * 128],
                                     rhs=gst[:, kt, :], start=(kt == 0), stop=(kt == NDT - 1),
                                     waits=[(dwa, dvwo), (dr, dvg), (sv, v_mt[wb])]
                                     if kt == 0 else [])
                            V.op('tensor_tensor', out=big0k[:, mt, :], in0=big0k[:, mt, :],
                                 in1=ps8[:, 6 + wb, :], op=Alu.add,
                                 waits=[(st, T.n), (dr, dvr)])
                            v_mt[wb] = V.n
                        Y.dma(r_chunk_ap(b, ch), big0k, dsto, cs_, waits=[(sv, V.n)])
            G.wait(dsto, cs_[0])

            bar()
            for sem_ in loop_sems:
                nc.sync.sem_clear(sem_)
            bar()

        for e in (V, S, T, G, Y):
            e.reset()
        for c in (cw_, cr_, cs_, c0_, c1_, c2_, c3_, ca_, cb_2):
            c[0] = 0
        ln_sq_t[0] = 0

        V.region = "s2"
        _mark(nc, 'wquant')
        def quant_tile(x_ap, out_bf, width, extra_wait=()):
            nbk = width // 32
            xb = x_ap.rearrange("p (k c) -> p k c", c=32)
            V.op('tensor_reduce', out=q32[:, 0:nbk], in_=xb, axis=mybir.AxisListType.X,
                 op=Alu.max, apply_absolute_value=True, waits=list(extra_wait))
            V.op('tensor_scalar', out=q32[:, 0:nbk], in0=q32[:, 0:nbk], scalar1=1e-35,
                 scalar2=None, op0=Alu.max)
            V.op('reciprocal', out=q32b[:, 0:nbk], in_=q32[:, 0:nbk])
            V.op('tensor_scalar', out=q32b[:, 0:nbk], in0=q32b[:, 0:nbk], scalar1=7.5,
                 scalar2=None, op0=Alu.mult)
            V.op('reciprocal', out=q32[:, 0:nbk], in_=q32b[:, 0:nbk])
            V.op('tensor_scalar', out=q32[:, 0:nbk], in0=q32[:, 0:nbk], scalar1=1e-30,
                 scalar2=None, op0=Alu.max)
            V.op('reciprocal', out=q32b[:, 0:nbk], in_=q32[:, 0:nbk])
            V.op('tensor_tensor', out=qa[:, 0:width].rearrange("p (k c) -> p k c", c=32),
                 in0=xb, in1=q32b[:, 0:nbk].unsqueeze(2).broadcast_to([128, nbk, 32]),
                 op=Alu.mult)
            S.op('activation', out=qb[:, 0:width], in_=qa[:, 0:width],
                 func=Act.Square, waits=[(sv, V.n)])
            V.op('tensor_scalar', out=qc[:, 0:width], in0=qb[:, 0:width], scalar1=4.0,
                 scalar2=None, op0=Alu.is_ge, waits=[(ss, S.n)])
            V.op('tensor_scalar', out=qd[:, 0:width], in0=qb[:, 0:width], scalar1=16.0,
                 scalar2=None, op0=Alu.is_ge)
            V.op('tensor_scalar', out=qb[:, 0:width], in0=qc[:, 0:width], scalar1=0.125,
                 scalar2=0.125, op0=Alu.mult, op1=Alu.add)
            V.op('scalar_tensor_tensor', out=qb[:, 0:width], in0=qd[:, 0:width],
                 scalar=0.25, in1=qb[:, 0:width], op0=Alu.mult, op1=Alu.add)
            V.op('tensor_scalar', out=qc[:, 0:width], in0=qc[:, 0:width], scalar1=-4.0,
                 scalar2=8.0, op0=Alu.mult, op1=Alu.add)
            V.op('scalar_tensor_tensor', out=qd[:, 0:width], in0=qd[:, 0:width],
                 scalar=-2.0, in1=qc[:, 0:width], op0=Alu.mult, op1=Alu.add)
            V.op('tensor_tensor', out=qc[:, 0:width], in0=qa[:, 0:width],
                 in1=qd[:, 0:width], op=Alu.mult)
            V.op('tensor_scalar', out=qc[:, 0:width], in0=qc[:, 0:width], scalar1=MAGIC,
                 scalar2=-MAGIC, op0=Alu.add, op1=Alu.add)
            V.op('tensor_tensor', out=qb[:, 0:width], in0=qc[:, 0:width],
                 in1=qb[:, 0:width], op=Alu.mult)
            V.op('tensor_tensor', out=out_bf.rearrange("p (k c) -> p k c", c=32),
                 in0=qb[:, 0:width].rearrange("p (k c) -> p k c", c=32),
                 in1=q32[:, 0:nbk].unsqueeze(2).broadcast_to([128, nbk, 32]), op=Alu.mult)

        for wt in range(NKT):
            dvq = Y.dma(big0[:, 0:D], t_ffw[wt * 128:(wt + 1) * 128, :], dr, cr_,
                        waits=[(sv, V.n), (st, T.n)])
            V.wait(dr, dvq)
            quant_tile(big0[:, 0:D], qeb, D)
            for kt in range(NKT):
                T.op('transpose', out=ps8.bitcast(BF16)[:, kt % 2, 0:128],
                     in_=qeb[:, kt * 128:(kt + 1) * 128], identity=idb[:],
                     waits=[(sv, V.n), (ss, S.n)])
                S.op('copy', out=wdT[:, kt, wt * 128:(wt + 1) * 128],
                     in_=ps8.bitcast(BF16)[:, kt % 2, 0:128], waits=[(st, T.n)])
        bar()

        _mark(nc, 'stage2')
        for b in range(NB):
            for ch in range(NCH):
                dv1 = Y.dma(big0k, r_chunk_ap(b, ch), dr, cr_,
                            waits=[(sv, V.n), (ss, S.n), (dsto, cs_[0])])
                dv2 = Y.dma(x1buf, hs_chunk_ap(b, ch), dr, cr_)
                ln_chunk(big0k, extra_wait=[(dr, dv2)])
                for kt in range(NKT):
                    V.op('tensor_scalar', out=big0k[:, kt, :], in0=big0k[:, kt, :],
                         scalar1=fw8[:, kt:kt + 1], scalar2=fb8[:, kt:kt + 1],
                         op0=Alu.mult, op1=Alu.add)
                for kt in range(NKT):
                    V.op('scalar_tensor_tensor', out=x1buf[:, kt, :], in0=big0k[:, kt, :],
                         scalar=modT[:, 2 * NKT + kt, b:b+1], in1=x1buf[:, kt, :],
                         op0=Alu.mult, op1=Alu.add,
                         waits=[(dr, dv2)] if kt == 0 else [])
                V.op('tensor_copy', out=big0[:, 0:NKT * CH],
                     in_=x1buf.rearrange("p k c -> p (k c)"))
                ln_chunk(big0k, extra_wait=[(sv, V.n)])
                for kt in range(NKT):
                    V.op('tensor_scalar', out=big0k[:, kt, :], in0=big0k[:, kt, :],
                         scalar1=modT[:, 4 * NKT + kt, b:b+1], scalar2=modT[:, 3 * NKT + kt, b:b+1],
                         op0=Alu.mult, op1=Alu.add)
                for tb in range(4):
                    for half in range(2):
                        bk = 4 + half
                        for kq in range(4):
                            kt = half * 4 + kq
                            T.op('transpose', out=ps8[:, bk, kq * 128:(kq + 1) * 128],
                                 in_=big0k[:, kt, tb * 128:(tb + 1) * 128],
                                 identity=idf[:],
                                 waits=[(sv, V.n), (ss, S.n)] if kq == 0 else [])
                        S.op('copy', out=h2T[:, tb, half * CH:(half + 1) * CH],
                             in_=ps8[:, bk, :], waits=[(st, T.n)])
                for tb in range(4):
                    quant_tile(h2T[:, tb, :], qeb, D, extra_wait=[(ss, S.n)])
                    bk = 6 + (tb % 2)
                    pb = ps8.bitcast(BF16)[:, bk, 0:1024].rearrange(
                        "p (k c) -> p k c", k=NKT)
                    for kt in range(NKT):
                        T.op('transpose', out=pb[:, kt, :],
                             in_=qeb[:, kt * 128:(kt + 1) * 128], identity=idb[:],
                             waits=[(sv, V.n), (ss, S.n)] if kt == 0 else [])
                    S.op('copy', out=xdT[:, :, tb * 128:(tb + 1) * 128],
                         in_=pb, waits=[(st, T.n)])
                for mt in range(NKT):
                    bk = mt % 2
                    for kt in range(NKT):
                        T.op('matmul', out=ps8[:, bk, :],
                             lhsT=wdT[:, kt, mt * 128:(mt + 1) * 128], rhs=xdT[:, kt, :],
                             start=(kt == 0), stop=(kt == NKT - 1),
                             waits=[(sv, V.n), (ss, S.n)] if kt == 0 else [])
                    S.op('activation', out=big0k[:, mt, :], in_=ps8[:, bk, :],
                         func=Act.Identity, bias=ffb8[:, mt:mt + 1], waits=[(st, T.n)])
                    V.op('scalar_tensor_tensor', out=big0k[:, mt, :], in0=big0k[:, mt, :],
                         scalar=modT[:, 5 * NKT + mt, b:b+1], in1=x1buf[:, mt, :],
                         op0=Alu.mult, op1=Alu.add, waits=[(ss, S.n)])
                Y.dma(t_out[b].rearrange("(k p) l -> p k l", p=128)
                      [:, :, ch * CH:(ch + 1) * CH], big0k, dsto, cs_,
                      waits=[(sv, V.n)])
        G.wait(dsto, cs_[0])
    _mark(nc, 'end')
    return nc


_PROG = None


def _get_prog():
    global _PROG
    if _PROG is None:
        _PROG = build_program()
    return _PROG


def _layout(inputs):
    f = np.ascontiguousarray
    hs = np.asarray(inputs["L0_attn_hidden_states"])
    c = np.asarray(inputs["L0_c"])

    def vecN(x):
        n = x.shape[-1] // 128
        return f(np.asarray(x, np.float32).reshape(x.shape[0], n, 128).transpose(0, 2, 1))

    def vec1(x):
        n = x.shape[-1] // 128
        return f(np.asarray(x, np.float32).reshape(n, 128).T)

    A_lay = (np.asarray(inputs["L0_attn_A_log"], np.float32)
             .reshape(NL, NDT, 128, 16).transpose(0, 2, 1, 3).reshape(NL, 128, 256))
    conv_w = (np.asarray(inputs["L0_attn_conv_w"], np.float32)
              .reshape(NL, NDT, 128, 4).transpose(0, 2, 1, 3).reshape(NL, 128, NDT * 4))
    lay = np.concatenate([
        A_lay.reshape(NL, -1),
        conv_w.reshape(NL, -1),
        vecN(inputs["L0_attn_conv_b"]).reshape(NL, -1),
        vecN(inputs["L0_attn_dt_proj_b"]).reshape(NL, -1),
        vecN(inputs["L0_attn_D_param"]).reshape(NL, -1),
        vecN(inputs["L0_attn_norm_w"]).reshape(NL, -1),
        vecN(inputs["L0_attn_norm_b"]).reshape(NL, -1),
        np.asarray(inputs["L0_attn_x_proj_w"], np.float32).transpose(0, 2, 1).reshape(NL, -1),
        np.asarray(inputs["L0_attn_dt_proj_w"], np.float32).transpose(0, 2, 1).reshape(NL, -1),
    ], axis=1)
    assert lay.shape[1] == PW, (lay.shape, PW)
    com = dict(
        wmod=f(np.asarray(inputs["L0_wmod"], np.float32)),
        in_projT=f(np.asarray(inputs["L0_attn_in_proj_w"], np.float32).transpose(0, 2, 1)),
        out_projT=f(np.asarray(inputs["L0_attn_out_proj_w"], np.float32).transpose(0, 2, 1)),
        lay_params=f(lay),
        fnorm_w8=vec1(inputs["L0_attn_fnorm_w"]),
        fnorm_b8=vec1(inputs["L0_attn_fnorm_b"]),
        ffn_w=f(np.asarray(inputs["L0_ffn_weight"], np.float32)),
        ffn_b8=vec1(inputs["L0_ffn_bias"]),
    )
    maps = []
    for core in range(8):
        b0 = core * NB
        m = dict(com)
        m["hsT"] = f(hs[b0:b0 + NB].transpose(0, 2, 1).astype(np.float32))
        m["cT"] = f(c[b0:b0 + NB].T.astype(np.float32))
        maps.append(m)
    return maps


def kernel(**inputs):
    nc = _get_prog()
    in_maps = _layout(inputs)
    res = run_bass_kernel_spmd(nc, in_maps, core_ids=list(range(8)))
    outs = [np.asarray(r["outT"]).transpose(0, 2, 1) for r in res.results]
    return np.ascontiguousarray(np.concatenate(outs, axis=0)).astype(np.float32)


# version 13
# speedup vs baseline: 1.2688x; 1.0081x over previous
import numpy as np
import concourse.bass as bass
import concourse.mybir as mybir
from concourse.bass_utils import run_bass_kernel_spmd
from concourse.bass import AP

F32 = mybir.dt.float32
BF16 = mybir.dt.bfloat16
I32 = mybir.dt.int32
Alu = mybir.AluOpType
Act = mybir.ActivationFunctionType

NB = 2
L = 2048
D = 1024
DIN = 2048
NL = 4
NCH = 4
CH = 512
NKT = D // 128
NDT = DIN // 128
DTR = 64
GRP = 4
EPS = 1e-5
MAGIC = float(3 << 22)
NL_RUN = 4
SELF_SYNC = frozenset(["V"])
V_SYNC_REGIONS = frozenset(["prep", "c", "s2"])

PO_A = 0
PO_CW = PO_A + 128 * 256
PO_CB = PO_CW + 128 * 64
PO_DTB = PO_CB + 128 * 16
PO_DP = PO_DTB + 128 * 16
PO_NW = PO_DP + 128 * 16
PO_NB = PO_NW + 128 * 8
PO_XP = PO_NB + 128 * 8
PO_DTW = PO_XP + DIN * 96
PW = PO_DTW + DTR * DIN

O_LD0 = 0
O_LD1 = 2048
O_TMPA = 4096
O_TMPB = 6144
O_BIG0 = 8192
O_BIG1 = 12288
O_P = 16384
O_LNSQ = O_P + 20992
O_SROW = O_P + 25088
ARW = O_P + 30720


class Eng:

    def __init__(self, eng, sem, tag=""):
        self.e, self.sem, self.n = eng, sem, 0
        self.tag = tag
        self.region = "prep"

    @property
    def selfsync(self):
        if self.tag != "V":
            return self.tag in SELF_SYNC
        return self.region in V_SYNC_REGIONS

    def op(self, name, *a, waits=(), **kw):
        for s, v in waits:
            self.e.wait_ge(s, v)
        if self.selfsync and self.n > 0:
            self.e.wait_ge(self.sem, self.n)
        r = getattr(self.e, name)(*a, **kw)
        r.then_inc(self.sem, 1)
        self.n += 1
        return r

    def dma(self, out, in_, dsem, dcnt, waits=(), **kw):
        for s, v in waits:
            self.e.wait_ge(s, v)
        if self.selfsync and self.n > 0:
            self.e.wait_ge(self.sem, self.n)
        r = self.e.dma_start(out=out, in_=in_, **kw)
        r.then_inc(dsem, 16)
        dcnt[0] += 16
        return dcnt[0]

    def wait(self, sem, v):
        self.e.wait_ge(sem, v)

    def reset(self):
        self.n = 0


PHASE_MARKS = []


def _mark(nc, name):
    PHASE_MARKS.append((name, sum(1 for _ in nc.all_instructions())))


def build_program():
    nc = bass.Bass("TRN2", target_bir_lowering=False,
                   detect_race_conditions=False)

    t_hsT = nc.dram_tensor("hsT", [NB, D, L], F32, kind="ExternalInput").ap()
    t_cT = nc.dram_tensor("cT", [D, NB], F32, kind="ExternalInput").ap()
    t_wmod = nc.dram_tensor("wmod", [D, 6 * D], F32, kind="ExternalInput").ap()
    t_win = nc.dram_tensor("in_projT", [NL, D, 2 * DIN], F32, kind="ExternalInput").ap()
    t_wout = nc.dram_tensor("out_projT", [NL, DIN, D], F32, kind="ExternalInput").ap()
    t_lay = nc.dram_tensor("lay_params", [NL, PW], F32, kind="ExternalInput").ap()
    t_fw = nc.dram_tensor("fnorm_w8", [128, NKT], F32, kind="ExternalInput").ap()
    t_fb = nc.dram_tensor("fnorm_b8", [128, NKT], F32, kind="ExternalInput").ap()
    t_ffw = nc.dram_tensor("ffn_w", [D, D], F32, kind="ExternalInput").ap()
    t_ffb = nc.dram_tensor("ffn_b8", [128, NKT], F32, kind="ExternalInput").ap()

    t_out = nc.dram_tensor("outT", [NB, D, L], F32, kind="ExternalOutput").ap()

    t_winb = nc.dram_tensor("win_bf16", [NL, D, 2 * DIN], BF16, kind="Internal").ap()
    t_woutb = nc.dram_tensor("wout_bf16", [NL, DIN, D], BF16, kind="Internal").ap()
    t_R = nc.dram_tensor("R_s", [NB, NKT, 128, L], F32, kind="Internal").ap()
    t_xc = nc.dram_tensor("xc_s", [NB, NDT, 128, L], F32, kind="Internal").ap()
    t_z = nc.dram_tensor("z_s", [NB, NDT, 128, L], F32, kind="Internal").ap()
    t_dt = nc.dram_tensor("dt_s", [NB, NDT, 128, L], F32, kind="Internal").ap()
    t_g = nc.dram_tensor("g_s", [NB, NDT, 128, L], BF16, kind="Internal").ap()
    t_wsin = nc.dram_tensor("wsin", [D, 2 * DIN], BF16, kind="Internal").ap()
    t_wsout = nc.dram_tensor("wsout", [DIN, D], BF16, kind="Internal").ap()
    t_pstg = nc.dram_tensor("pstg", [PW], F32, kind="Internal").ap()

    import contextlib
    with contextlib.ExitStack() as _es:
        E = _es.enter_context
        AR = E(nc.sbuf_tensor("AR", [128, ARW], F32))
        ones1 = E(nc.sbuf_tensor("ones1", [128, 1], F32))
        ones1b = E(nc.sbuf_tensor("ones1b", [128, 1], BF16))
        cEPS = E(nc.sbuf_tensor("cEPS", [128, 1], F32))
        cNH = E(nc.sbuf_tensor("cNH", [128, 1], F32))
        onesr = E(nc.sbuf_tensor("onesr", [1, 128], F32))
        sel = E(nc.sbuf_tensor("sel", [16, 16, 128], BF16))
        modT = E(nc.sbuf_tensor("modT", [128, 6 * NKT, NB], F32))
        fw8 = E(nc.sbuf_tensor("fw8", [128, NKT], F32))
        fb8 = E(nc.sbuf_tensor("fb8", [128, NKT], F32))
        ffb8 = E(nc.sbuf_tensor("ffb8", [128, NKT], F32))
        idb = E(nc.sbuf_tensor("idb", [128, 128], BF16))
        idf = E(nc.sbuf_tensor("idf", [128, 128], F32))
        Abuf = E(nc.sbuf_tensor("Abuf", [128, 256], F32))
        cwb = E(nc.sbuf_tensor("cwb", [128, NDT * 4], F32))
        cbb = E(nc.sbuf_tensor("cbb", [128, NDT], F32))
        dtbb = E(nc.sbuf_tensor("dtbb", [128, NDT], F32))
        dpb = E(nc.sbuf_tensor("dpb", [128, NDT], F32))
        nw8 = E(nc.sbuf_tensor("nw8", [128, NKT], F32))
        nb8 = E(nc.sbuf_tensor("nb8", [128, NKT], F32))
        wstr = E(nc.sbuf_tensor("wstr", [128, 2, NDT, 128], BF16))
        q32 = E(nc.sbuf_tensor("q32", [128, 32], F32))
        q32b = E(nc.sbuf_tensor("q32b", [128, 32], F32))
        ps8 = E(nc.psum_tensor("ps8", [128, 8, 512], F32))
        sv = E(nc.semaphore("sv"))
        ss = E(nc.semaphore("ss"))
        st = E(nc.semaphore("st"))
        sg = E(nc.semaphore("sg"))
        sy = E(nc.semaphore("sy"))
        dw = E(nc.semaphore("dw"))
        dr = E(nc.semaphore("dr"))
        dsto = E(nc.semaphore("dsto"))
        dsg = E(nc.semaphore("dsg"))
        dst0 = E(nc.semaphore("dst0"))
        dst1 = E(nc.semaphore("dst1"))
        dst2 = E(nc.semaphore("dst2"))
        dst3 = E(nc.semaphore("dst3"))
        dwa = E(nc.semaphore("dwa"))
        dwb = E(nc.semaphore("dwb"))
        p1l = E(nc.semaphore("p1l"))
        p1s0 = E(nc.semaphore("p1s0"))
        p1s1 = E(nc.semaphore("p1s1"))
        p1s2 = E(nc.semaphore("p1s2"))

        V = Eng(nc.vector, sv, "V")
        S = Eng(nc.scalar, ss, "S")
        T = Eng(nc.tensor, st, "T")
        G = Eng(nc.gpsimd, sg, "G")
        Y = Eng(nc.sync, sy, "Y")
        cw_ = [0]
        cr_ = [0]
        cs_ = [0]
        c0_ = [0]
        c1_ = [0]
        c2_ = [0]
        c3_ = [0]
        ca_ = [0]
        cb_2 = [0]

        ARB = AR.bitcast(BF16)

        def fv(off, n):
            return AR[:, off:off + n]

        def bv(off, nbf):
            return ARB[:, 2 * off:2 * off + nbf]

        ld0 = fv(O_LD0, L)
        ld1 = fv(O_LD1, L)
        tmpA = fv(O_TMPA, L)
        tmpB = fv(O_TMPB, L)
        big0 = fv(O_BIG0, 4096)
        big1 = fv(O_BIG1, 4096)
        big0k = big0.rearrange("p (k c) -> p k c", k=NKT)
        big1k = big1.rearrange("p (k c) -> p k c", k=NKT)
        dtin = AR[0:DTR, O_TMPB:O_TMPB + CH]
        xhat = bv(O_P, 2 * 16384).rearrange("p (b k l) -> p b k l", b=NB, k=NKT)
        xpre = fv(O_P + 16384, L + 3)
        cv = fv(O_P + 18436, L)
        xpT = fv(O_P, NDT * 96).rearrange("p (t m) -> p t m", t=NDT)
        dtT = AR[0:DTR, O_P + 1536:O_P + 1536 + DIN]
        BCrowb = ARB[0:16, 2 * O_BIG0:2 * O_BIG0 + 2 * NB * L].rearrange(
            "p (b v l) -> p b v l", b=NB, v=2)
        dt_g = fv(O_P, GRP * L).rearrange("p (g l) -> p g l", g=GRP)
        dtx_g = fv(O_P + 8192, GRP * L).rearrange("p (g l) -> p g l", g=GRP)
        y_g = fv(O_P + 16384, GRP * L).rearrange("p (g l) -> p g l", g=GRP)
        dAb = fv(O_P + 24576, L)
        ub = fv(O_P + 26624, L)
        hbb = fv(O_P + 28672, L)
        gst = bv(O_P, NDT * CH).rearrange("p (t c) -> p t c", t=NDT)
        woutres = bv(O_P + 4096, NDT * D).rearrange("p (k m) -> p k m", k=NDT)
        wdT = bv(O_P, NKT * D).rearrange("p (k m) -> p k m", k=NKT)
        qa = fv(O_P + 4096, D)
        qb = fv(O_P + 5120, D)
        qc = fv(O_P + 6144, D)
        qd = fv(O_P + 7168, D)
        qeb = bv(O_P + 8192, D)
        x1buf = fv(O_P + 8704, NKT * CH).rearrange("p (k c) -> p k c", k=NKT)
        h2T = fv(O_P + 12800, 4 * D).rearrange("p (t d) -> p t d", t=4)
        xdT = bv(O_P + 16896, NKT * CH).rearrange("p (k c) -> p k c", k=NKT)

        def bar():
            nc.all_engine_barrier()

        _mark(nc, 'prep0')
        V.op('memset', ones1[:], 1.0)
        V.op('memset', ones1b[:], 1.0)
        V.op('memset', cEPS[:], EPS)
        V.op('memset', cNH[:], -0.5)
        V.op('memset', onesr[:], 1.0)
        io128 = E(nc.sbuf_tensor("io128", [128, 128], I32))
        G.op('iota', io128[:], [[-1, 128]], base=0, channel_multiplier=1)
        V.op('tensor_scalar', out=idb[:], in0=io128[:], scalar1=0, scalar2=None,
             op0=Alu.is_equal, waits=[(sg, G.n)])
        V.op('tensor_scalar', out=idf[:], in0=io128[:], scalar1=0, scalar2=None,
             op0=Alu.is_equal)
        G.op('iota', io128[0:16, :], [[0, 128]], base=0, channel_multiplier=1,
             waits=[(sv, V.n)])
        for n in range(16):
            V.op('tensor_scalar', out=sel[:, n, :], in0=io128[0:16, :], scalar1=n,
                 scalar2=None, op0=Alu.is_equal, waits=[(sg, G.n)] if n == 0 else [])
        bar()

        _mark(nc, 'prep1')
        jobs = []
        for li0 in range(NL):
            for kt in range(NKT):
                jobs.append((t_win, t_winb, (li0 * D + kt * 128) * 2 * DIN, 2 * DIN))
        for li0 in range(NL):
            for kt in range(NDT):
                jobs.append((t_wout, t_woutb, (li0 * DIN + kt * 128) * D, D))
        slabs = [fv(O_P + 6144 + i * 4096, 4096) for i in range(3)]
        pc0, pc1, pc2, pcl = [0], [0], [0], [0]
        p1_st = [(p1s0, pc0), (p1s1, pc1), (p1s2, pc2)]
        ldv = {}
        for j in range(3):
            src, dst, off, w = jobs[j]
            ldv[j] = G.dma(slabs[j][:, 0:w], AP(src.tensor, off, [[w, 128], [1, w]]),
                           p1l, pcl)
        for j, (src, dst, off, w) in enumerate(jobs):
            sb = j % 3
            stsem, stcnt = p1_st[sb]
            G.op('tensor_copy', out=slabs[sb].bitcast(BF16)[:, 0:w],
                 in_=slabs[sb][:, 0:w], waits=[(p1l, ldv[j])])
            G.dma(AP(dst.tensor, off, [[w, 128], [1, w]]),
                  slabs[sb].bitcast(BF16)[:, 0:w], stsem, stcnt,
                  waits=[(sg, G.n)])
            if j + 3 < len(jobs):
                src2, _, off2, w2 = jobs[j + 3]
                ldv[j + 3] = G.dma(slabs[sb][:, 0:w2],
                                   AP(src2.tensor, off2, [[w2, 128], [1, w2]]),
                                   p1l, pcl, waits=[(stsem, stcnt[0])])

        _mark(nc, 'prep2')
        dv = Y.dma(big0[:, 0:NKT * NB].rearrange("p (k b) -> p k b", k=NKT),
                   t_cT.rearrange("(k p) b -> p k b", p=128), dst0, c0_)
        S.op('activation', out=big0[:, 0:NKT * NB], in_=big0[:, 0:NKT * NB], func=Act.Silu,
             waits=[(dst0, c0_[0])])
        for jc in range(12):
            bank = jc % 2
            for kt in range(NKT):
                dv = Y.dma(tmpA[:, 0:512],
                           AP(t_wmod.tensor, kt * 128 * 6 * D + jc * 512,
                              [[6 * D, 128], [1, 512]]),
                           dr, cr_, waits=[(st, T.n)])
                T.op('matmul', out=ps8[0:NB, bank, :], lhsT=big0[:, 2 * kt:2 * kt + 2],
                     rhs=tmpA[:, 0:512], start=(kt == 0), stop=(kt == NKT - 1),
                     waits=[(dr, dv), (ss, S.n)])
            S.op('copy', out=AR[0:NB, O_P + jc * 512:O_P + (jc + 1) * 512],
                 in_=ps8[0:NB, bank, :], waits=[(st, T.n)])
        for blk in range(48):
            T.op('transpose', out=ps8[:, blk % 2, 0:NB],
                 in_=AR[0:NB, O_P + blk * 128:O_P + (blk + 1) * 128],
                 identity=idf[0:NB, 0:NB],
                 waits=[(ss, S.n), (sv, V.n)])
            V.op('tensor_copy', out=modT[:, blk, :], in_=ps8[:, blk % 2, 0:NB],
                 waits=[(st, T.n)])
        for v0 in (1, 4):
            V.op('tensor_scalar', out=modT[:, v0 * NKT:(v0 + 1) * NKT, :],
                 in0=modT[:, v0 * NKT:(v0 + 1) * NKT, :], scalar1=1.0, scalar2=None,
                 op0=Alu.add)
        for v0 in (2, 5):
            gsl = modT[:, v0 * NKT:(v0 + 1) * NKT, :]
            nn = NKT * NB
            V.op('tensor_scalar', out=big0[:, 0:nn], in0=gsl, scalar1=0.0, scalar2=None,
                 op0=Alu.is_ge)
            V.op('tensor_scalar', out=big0[:, 0:nn], in0=big0[:, 0:nn], scalar1=2.0,
                 scalar2=-1.0, op0=Alu.mult, op1=Alu.add)
            S.op('activation', out=big1[:, 0:nn], in_=gsl, func=Act.Abs, waits=[(sv, V.n)])
            V.op('tensor_scalar', out=big1[:, 0:nn], in0=big1[:, 0:nn], scalar1=0.25,
                 scalar2=None, op0=Alu.max, waits=[(ss, S.n)])
            V.op('tensor_tensor', out=gsl, in0=big0[:, 0:nn], in1=big1[:, 0:nn], op=Alu.mult)
        Y.dma(fw8[:], t_fw, dst0, c0_)
        Y.dma(fb8[:], t_fb, dst0, c0_)
        Y.dma(ffb8[:], t_ffb, dst0, c0_)
        Y.wait(dst0, c0_[0])
        bar()

        ln_sq_t = [0]

        def ln_chunk(xk, extra_wait=(), par=0):
            bs0, bs1 = (0, 1) if par == 0 else (4, 5)
            br0, br1 = (2, 3) if par == 0 else (6, 7)
            xkb = ARB[:, 2 * O_LNSQ:2 * O_LNSQ + NKT * CH].rearrange(
                "p (k c) -> p k c", k=NKT)
            sq = ARB[:, 2 * O_LNSQ + NKT * CH:2 * O_LNSQ + 2 * NKT * CH]
            sqk = sq.rearrange("p (k c) -> p k c", k=NKT)
            so = O_SROW + par * 1536
            mu = AR[0:1, so:so + CH]
            r_ = AR[0:1, so + CH:so + 2 * CH]
            tm = AR[0:1, so + 2 * CH:so + 3 * CH]
            S.op('copy', out=xkb.rearrange("p k c -> p (k c)"),
                 in_=xk.rearrange("p k c -> p (k c)"),
                 waits=list(extra_wait) + [(st, ln_sq_t[0])])
            S.op('activation', out=sq, in_=xk.rearrange("p k c -> p (k c)"),
                 func=Act.Square)
            for kt in range(NKT):
                T.op('matmul', out=ps8[0:1, bs0, :], lhsT=ones1b[:], rhs=xkb[:, kt, :],
                     start=(kt == 0), stop=(kt == NKT - 1),
                     waits=[(ss, S.n - 1)] if kt == 0 else [])
            for kt in range(NKT):
                T.op('matmul', out=ps8[0:1, bs1, :], lhsT=ones1b[:], rhs=sqk[:, kt, :],
                     start=(kt == 0), stop=(kt == NKT - 1),
                     waits=[(ss, S.n)] if kt == 0 else [])
            ln_sq_t[0] = T.n
            V.op('tensor_scalar', out=mu, in0=ps8[0:1, bs0, :], scalar1=1.0 / D, scalar2=None,
                 op0=Alu.mult, waits=[(st, T.n)])
            V.op('tensor_scalar', out=tm, in0=ps8[0:1, bs1, :], scalar1=1.0 / D, scalar2=None,
                 op0=Alu.mult)
            V.op('tensor_tensor', out=r_, in0=mu, in1=mu, op=Alu.mult)
            V.op('tensor_tensor', out=tm, in0=tm, in1=r_, op=Alu.subtract)
            S.op('activation', out=tm, in_=tm, func=Act.Ln, bias=cEPS[0:1, :],
                 waits=[(sv, V.n)])
            S.op('activation', out=r_, in_=tm, func=Act.Exp, scale=cNH[0:1, :])
            T.op('matmul', out=ps8[:, br0, :], lhsT=onesr[:], rhs=mu, start=True,
                 stop=True, waits=[(sv, V.n)])
            T.op('matmul', out=ps8[:, br1, :], lhsT=onesr[:], rhs=r_, start=True,
                 stop=True, waits=[(ss, S.n)])
            V.op('tensor_tensor', out=xk, in0=xk,
                 in1=ps8[:, br0, :].unsqueeze(1).broadcast_to([128, NKT, CH]),
                 op=Alu.subtract, waits=[(st, T.n)])
            V.op('tensor_tensor', out=xk, in0=xk,
                 in1=ps8[:, br1, :].unsqueeze(1).broadcast_to([128, NKT, CH]), op=Alu.mult)

        def r_chunk_ap(b, ch):
            return t_R[b].transpose([1, 0, 2])[:, :, ch * CH:(ch + 1) * CH]

        def hs_chunk_ap(b, ch):
            return t_hsT[b].rearrange("(k p) l -> p k l", p=128)[:, :, ch * CH:(ch + 1) * CH]

        _mark(nc, 'stage1')
        bigks = (big0k, big1k)
        st_sem = ((dst0, c0_), (dst1, c1_))
        vbuf = [0, 0]
        for idx, (b, ch) in enumerate([(b, ch) for b in range(NB) for ch in range(NCH)]):
            par = idx % 2
            xbk = bigks[par]
            psem, pcnt = st_sem[par]
            dv = Y.dma(xbk, hs_chunk_ap(b, ch), dr, cr_,
                       waits=[(sv, vbuf[par]), (psem, pcnt[0])])
            ln_chunk(xbk, extra_wait=[(dr, dv)], par=par)
            for kt in range(NKT):
                V.op('tensor_scalar', out=xbk[:, kt, :], in0=xbk[:, kt, :],
                     scalar1=modT[:, 1 * NKT + kt, b:b+1], scalar2=modT[:, 0 * NKT + kt, b:b+1],
                     op0=Alu.mult, op1=Alu.add)
            vbuf[par] = V.n
            Y.dma(r_chunk_ap(b, ch), xbk, psem, pcnt, waits=[(sv, V.n)])
        G.wait(dst0, c0_[0])
        G.wait(dst1, c1_[0])
        G.wait(p1s0, pc0[0])
        G.wait(p1s1, pc1[0])
        G.wait(p1s2, pc2[0])
        bar()

        _mark(nc, 'loop_head')
        loop_sems = (sv, ss, st, sg, sy, dw, dr, dsto, dsg, dst0, dst1, dst2, dst3, dwa, dwb)
        for sem_ in loop_sems:
            nc.sync.sem_clear(sem_)
        bar()
        with nc.Fori(0, NL_RUN) as li:
            for e in (V, S, T, G, Y):
                e.reset()
            for c in (cw_, cr_, cs_, c0_, c1_, c2_, c3_, ca_, cb_2):
                c[0] = 0
            ln_sq_t[0] = 0

            cg_ = [0]
            with nc.sync.register("lof") as lof:
                nc.sync.reg_mul(lof, li, PW)
                Y.dma(t_pstg[:], AP(t_lay.tensor, lof, [[1, PW]]), dsg, cg_)
                WQ = D * 2 * DIN // 4
                for j in range(4):
                    nc.sync.reg_mul(lof, li, D * 2 * DIN)
                    nc.sync.reg_add(lof, lof, j * WQ)
                    Y.dma(AP(t_wsin.tensor, j * WQ, [[1, WQ]]),
                          AP(t_winb.tensor, lof, [[1, WQ]]), dsg, cg_)
                WO = DIN * D // 2
                for j in range(2):
                    nc.sync.reg_mul(lof, li, DIN * D)
                    nc.sync.reg_add(lof, lof, j * WO)
                    Y.dma(AP(t_wsout.tensor, j * WO, [[1, WO]]),
                          AP(t_woutb.tensor, lof, [[1, WO]]), dsg, cg_)
            STAGE_DONE = cg_[0]
            Y.wait(dsg, STAGE_DONE)
            Y.dma(Abuf[:], AP(t_pstg.tensor, PO_A, [[256, 128], [1, 256]]), dw, cw_)
            Y.dma(cwb[:], AP(t_pstg.tensor, PO_CW, [[64, 128], [1, 64]]), dw, cw_)
            Y.dma(cbb[:], AP(t_pstg.tensor, PO_CB, [[16, 128], [1, 16]]), dw, cw_)
            Y.dma(dtbb[:], AP(t_pstg.tensor, PO_DTB, [[16, 128], [1, 16]]), dw, cw_)
            Y.dma(dpb[:], AP(t_pstg.tensor, PO_DP, [[16, 128], [1, 16]]), dw, cw_)
            Y.dma(nw8[:], AP(t_pstg.tensor, PO_NW, [[8, 128], [1, 8]]), dw, cw_)
            Y.dma(nb8[:], AP(t_pstg.tensor, PO_NB, [[8, 128], [1, 8]]), dw, cw_)
            S.op('activation', out=Abuf[:], in_=Abuf[:], func=Act.Exp, waits=[(dw, cw_[0])])
            V.op('tensor_scalar', out=Abuf[:], in0=Abuf[:], scalar1=-1.0, scalar2=None,
                 op0=Alu.mult, waits=[(ss, S.n)])
            bar()

            _mark(nc, 'a1')
            vbuf = [0, 0]
            for idx, (b, ch) in enumerate([(b, ch) for b in range(NB) for ch in range(NCH)]):
                par = idx % 2
                xbk = bigks[par]
                dv = Y.dma(xbk, r_chunk_ap(b, ch), dr, cr_,
                           waits=[(sv, vbuf[par])])
                ln_chunk(xbk, extra_wait=[(dr, dv)], par=par)
                for kt in range(NKT):
                    V.op('tensor_scalar', out=xhat[:, b, kt, ch * CH:(ch + 1) * CH],
                         in0=xbk[:, kt, :], scalar1=nw8[:, kt:kt + 1],
                         scalar2=nb8[:, kt:kt + 1], op0=Alu.mult, op1=Alu.add)
                vbuf[par] = V.n
            bar()

            V.region = "a2"
            _mark(nc, 'a2')
            V.op('memset', xpre[:, 0:3], 0.0)
            if True:
                last_store = [0, 0]
                t_after_mt = [0, 0]
                s_bank = [0] * 8

                def wsin_ap(m):
                    return AP(t_wsin.tensor, m * 128,
                              [[2 * DIN, 128], [128 * 2 * DIN, NKT], [1, 128]])

                dvw_slot = [0, 0]
                for m0 in range(2):
                    wsem, wcnt = ((dwa, ca_), (dwb, cb_2))[m0]
                    dvw_slot[m0] = Y.dma(wstr[:, m0, 0:NKT, :], wsin_ap(m0),
                                         wsem, wcnt, waits=[(dsg, STAGE_DONE)])
                for mt in range(2 * NDT):
                    wb = mt % 2
                    wsem, wcnt = ((dwa, ca_), (dwb, cb_2))[wb]
                    dvw = dvw_slot[wb]
                    for kt in range(NKT):
                        for b in range(NB):
                            for ch in range(NCH):
                                w0 = []
                                if kt == 0:
                                    w0 = [(ss, s_bank[b * NCH + ch])]
                                    if b == 0 and ch == 0:
                                        w0.append((wsem, dvw))
                                T.op('matmul', out=ps8[:, b * NCH + ch, :],
                                     lhsT=wstr[:, wb, kt, :],
                                     rhs=xhat[:, b, kt, ch * CH:(ch + 1) * CH],
                                     start=(kt == 0), stop=(kt == NKT - 1), waits=w0)
                    t_after_mt[wb] = T.n
                    if mt < NDT:
                        for b in range(NB):
                            ob = (ld0, ld1)[b]
                            bsem, bcnt = ((dst0, c0_), (dst1, c1_))[b]
                            for ch in range(NCH):
                                S.op('copy', out=xpre[:, 3 + ch * CH:3 + (ch + 1) * CH],
                                     in_=ps8[:, b * NCH + ch, :],
                                     waits=[(st, T.n), (sv, V.n)] if ch == 0 else [])
                                s_bank[b * NCH + ch] = S.n
                            t4 = mt * 4
                            V.op('tensor_scalar', out=cv[:], in0=xpre[:, 0:L],
                                 scalar1=cwb[:, t4:t4 + 1], scalar2=None, op0=Alu.mult,
                                 waits=[(ss, S.n)])
                            for k in range(1, 4):
                                V.op('scalar_tensor_tensor', out=cv[:], in0=xpre[:, k:L + k],
                                     scalar=cwb[:, t4 + k:t4 + k + 1], in1=cv[:],
                                     op0=Alu.mult, op1=Alu.add)
                            S.op('activation', out=ob, in_=cv[:], func=Act.Silu,
                                 bias=cbb[:, mt:mt + 1],
                                 waits=[(sv, V.n), (bsem, bcnt[0])])
                            S.dma(t_xc[b, mt], ob, bsem, bcnt, waits=[(ss, S.n)])
                    else:
                        for b in range(NB):
                            for ch in range(NCH):
                                S.op('copy', out=big1[:, ch * CH:(ch + 1) * CH],
                                     in_=ps8[:, b * NCH + ch, :],
                                     waits=([(st, T.n), (dst2, c2_[0])]
                                            if ch == 0 else []))
                                s_bank[b * NCH + ch] = S.n
                            S.dma(t_z[b, mt - NDT], big1[:, 0:L], dst2, c2_,
                                  waits=[(ss, S.n)])
                    if mt + 2 < 2 * NDT:
                        dvw_slot[wb] = S.dma(wstr[:, wb, 0:NKT, :], wsin_ap(mt + 2),
                                             wsem, wcnt)
            G.wait(dst0, c0_[0])
            G.wait(dst1, c1_[0])
            G.wait(dst2, c2_[0])
            bar()

            V.region = "a3"
            _mark(nc, 'a3')
            dvx1 = Y.dma(xpT, AP(t_pstg.tensor, PO_XP,
                                 [[96, 128], [128 * 96, NDT], [1, 96]]), dw, cw_)
            dvx2 = Y.dma(dtT, AP(t_pstg.tensor, PO_DTW, [[DIN, DTR], [1, DIN]]), dw, cw_)
            xpTb = bv(O_P + 3584, NDT * 96).rearrange("p (t m) -> p t m", t=NDT)
            dtTb = ARB[0:DTR, 2 * (O_P + 4352):2 * (O_P + 4352) + DIN]
            V.op('tensor_copy', out=xpTb, in_=xpT, waits=[(dw, dvx2)])
            V.op('tensor_copy', out=dtTb, in_=dtT)
            dtinB = ARB[0:DTR, 2 * (O_TMPB + CH):2 * (O_TMPB + CH) + CH]
            lbF = (ld0[:, 0:CH], ld1[:, 0:CH], tmpA[:, 0:CH], tmpA[:, CH:2 * CH])
            lbB4 = (ARB[:, 2 * (O_LD0 + CH):2 * (O_LD0 + CH) + CH],
                    ARB[:, 2 * (O_LD1 + CH):2 * (O_LD1 + CH) + CH],
                    ARB[:, 2 * (O_TMPA + 2 * CH):2 * (O_TMPA + 2 * CH) + CH],
                    ARB[:, 2 * (O_TMPA + 2 * CH) + CH:2 * (O_TMPA + 2 * CH) + 2 * CH])
            chunks = [(b, ch) for b in range(NB) for ch in range(NCH)]
            NQ = len(chunks) * NDT

            def xc_ap(q):
                qb, qch = chunks[q // NDT]
                return t_xc[qb, q % NDT, :, qch * CH:(qch + 1) * CH]

            dvxq = {}
            t_slot = [0, 0, 0, 0]
            for q in range(4):
                dvxq[q] = Y.dma(lbF[q], xc_ap(q), dr, cr_)
            dt_bank_s = [0, 0, 0, 0]
            dt_sems = ((dst0, c0_), (dst1, c1_), (dst2, c2_), (dst3, c3_))
            for ci, (b, ch) in enumerate(chunks):
                for kt in range(NDT):
                    q = ci * NDT + kt
                    sl = q % 4
                    S.op('copy', out=lbB4[sl], in_=lbF[sl], waits=[(dr, dvxq[q])])
                    T.op('matmul', out=ps8[0:DTR, 0, :], lhsT=xpTb[:, kt, 0:DTR],
                         rhs=lbB4[sl], start=(kt == 0), stop=(kt == NDT - 1),
                         waits=[(ss, S.n)] +
                               ([(sv, V.n)] if kt == 0 else []))
                    T.op('matmul', out=ps8[0:16, 1, :], lhsT=xpTb[:, kt, DTR:DTR + 16],
                         rhs=lbB4[sl], start=(kt == 0), stop=(kt == NDT - 1))
                    T.op('matmul', out=ps8[0:16, 2, :], lhsT=xpTb[:, kt, DTR + 16:96],
                         rhs=lbB4[sl], start=(kt == 0), stop=(kt == NDT - 1))
                    t_slot[sl] = T.n
                    if q + 4 < NQ:
                        dvxq[q + 4] = Y.dma(lbF[sl], xc_ap(q + 4), dr, cr_,
                                            waits=[(st, t_slot[sl])])
                S.op('copy', out=dtinB, in_=ps8[0:DTR, 0, :], waits=[(st, T.n)])
                s_dtin = S.n
                V.op('tensor_copy', out=BCrowb[:, b, 0, ch * CH:(ch + 1) * CH],
                     in_=ps8[0:16, 1, :], waits=[(st, T.n)])
                V.op('tensor_copy', out=BCrowb[:, b, 1, ch * CH:(ch + 1) * CH],
                     in_=ps8[0:16, 2, :])
                for mt in range(NDT):
                    jf = mt % 4
                    bk = 3 + jf
                    hsem, hcnt = dt_sems[jf]
                    T.op('matmul', out=ps8[:, bk, :],
                         lhsT=dtTb[:, mt * 128:(mt + 1) * 128], rhs=dtinB,
                         start=True, stop=True,
                         waits=[(ss, max(s_dtin, dt_bank_s[jf]))])
                    S.op('copy', out=big1[:, jf * CH:(jf + 1) * CH],
                         in_=ps8[:, bk, :],
                         waits=[(st, T.n), (hsem, hcnt[0])])
                    dt_bank_s[jf] = S.n
                    S.dma(t_dt[b, mt, :, ch * CH:(ch + 1) * CH],
                          big1[:, jf * CH:(jf + 1) * CH], hsem, hcnt,
                          waits=[(ss, S.n)])
            G.wait(dst0, c0_[0])
            G.wait(dst1, c1_[0])
            G.wait(dst2, c2_[0])
            G.wait(dst3, c3_[0])
            bar()

            V.region = "bprep"
            _mark(nc, 'b')
            BrepB = tmpA.bitcast(BF16)[:, 0:L]
            CrepB = tmpA.bitcast(BF16)[:, L:2 * L]
            for b in range(NB):
                for g0 in range(0, NDT, GRP):
                    V.region = "bprep"
                    xcls = []
                    for ti in range(GRP):
                        t = g0 + ti
                        Y.dma(dt_g[:, ti, :], t_dt[b, t], dr, cr_,
                              waits=[(sv, V.n), (ss, S.n), (dst2, c2_[0]),
                                     (dst3, c3_[0])] if ti == 0 else [])
                        xcl = (ld0, ld1, tmpB, tmpA)[ti]
                        xcls.append(xcl)
                        Y.dma(xcl, t_xc[b, t], dr, cr_)
                    GL = cr_[0]
                    for ti in range(GRP):
                        t = g0 + ti
                        xcl = xcls[ti]
                        V.op('tensor_scalar', out=dt_g[:, ti, :], in0=dt_g[:, ti, :],
                             scalar1=dtbb[:, t:t + 1], scalar2=None, op0=Alu.add,
                             waits=[(dr, GL)])
                        S.op('activation', out=dAb[:], in_=dt_g[:, ti, :], func=Act.Relu,
                             waits=[(sv, V.n)])
                        V.op('scalar_tensor_tensor', out=ub[:], in0=dAb[:], scalar=-2.0,
                             in1=dt_g[:, ti, :], op0=Alu.mult, op1=Alu.add,
                             waits=[(ss, S.n)])
                        S.op('activation', out=ub[:], in_=ub[:], func=Act.Exp,
                             waits=[(sv, V.n)])
                        S.op('activation', out=ub[:], in_=ub[:], func=Act.Ln, bias=ones1[:])
                        V.op('tensor_tensor', out=dt_g[:, ti, :], in0=dAb[:], in1=ub[:],
                             op=Alu.add, waits=[(ss, S.n)])
                        V.op('tensor_tensor', out=dtx_g[:, ti, :], in0=dt_g[:, ti, :],
                             in1=xcl, op=Alu.mult)
                    V.region = "bscan"
                    vread_rep = [V.n, V.n]
                    vread_dA = [V.n, V.n]
                    for n in range(16):
                        np_ = n % 2
                        repbuf = (tmpA, ld0)[np_]
                        Bp = repbuf.bitcast(BF16)[:, 0:L]
                        Cp = repbuf.bitcast(BF16)[:, L:2 * L]
                        for cc in range(4):
                            T.op('matmul', out=ps8[:, cc, :], lhsT=sel[:, n, :],
                                 rhs=BCrowb[:, b, 0, cc * CH:(cc + 1) * CH], start=True,
                                 stop=True,
                                 waits=[(sv, vread_rep[np_]), (ss, S.n)] if cc == 0 else [])
                            T.op('matmul', out=ps8[:, 4 + cc, :], lhsT=sel[:, n, :],
                                 rhs=BCrowb[:, b, 1, cc * CH:(cc + 1) * CH], start=True,
                                 stop=True)
                        for cc in range(4):
                            S.op('copy', out=Bp[:, cc * CH:(cc + 1) * CH],
                                 in_=ps8[:, cc, :],
                                 waits=[(st, T.n)] if cc == 0 else [])
                            S.op('copy', out=Cp[:, cc * CH:(cc + 1) * CH],
                                 in_=ps8[:, 4 + cc, :])
                        s_rep = S.n
                        for ti in range(GRP):
                            t = g0 + ti
                            tp = ti % 2
                            dab = (dAb, tmpB)[tp]
                            S.op('activation', out=dab[:], in_=dt_g[:, ti, :], func=Act.Exp,
                                 scale=Abuf[:, t * 16 + n:t * 16 + n + 1],
                                 waits=[(sv, vread_dA[tp])])
                            s_exp = S.n
                            V.op('tensor_tensor', out=ub[:], in0=dtx_g[:, ti, :], in1=Bp,
                                 op=Alu.mult, waits=[(ss, s_rep)])
                            V.op('tensor_tensor_scan', out=hbb[:], data0=dab[:], data1=ub[:],
                                 initial=0.0, op0=Alu.mult, op1=Alu.add,
                                 waits=[(ss, s_exp)])
                            vread_dA[tp] = V.n
                            if n == 0:
                                V.op('tensor_tensor', out=y_g[:, ti, :], in0=hbb[:],
                                     in1=Cp, op=Alu.mult)
                            else:
                                V.op('tensor_tensor', out=ub[:], in0=hbb[:], in1=Cp,
                                     op=Alu.mult)
                                V.op('tensor_tensor', out=y_g[:, ti, :], in0=y_g[:, ti, :],
                                     in1=ub[:], op=Alu.add)
                            vread_rep[np_] = V.n
                    V.region = "bgate"
                    for ti in range(GRP):
                        t = g0 + ti
                        dv1 = Y.dma(ld0, t_z[b, t], dr, cr_, waits=[(ss, S.n), (sv, V.n)])
                        dv2 = Y.dma(ld1, t_xc[b, t], dr, cr_)
                        gob = (dAb, hbb)[ti % 2]
                        gsem, gcnt = ((dst2, c2_), (dst3, c3_))[ti % 2]
                        S.op('activation', out=tmpB[:], in_=ld0, func=Act.Silu,
                             waits=[(dr, dv2), (sv, V.n)])
                        V.op('scalar_tensor_tensor', out=y_g[:, ti, :], in0=ld1,
                             scalar=dpb[:, t:t + 1], in1=y_g[:, ti, :], op0=Alu.mult,
                             op1=Alu.add, waits=[(dr, dv2), (ss, S.n)])
                        V.op('tensor_tensor', out=gob.bitcast(BF16)[:, 0:L],
                             in0=y_g[:, ti, :], in1=tmpB[:], op=Alu.mult,
                             waits=[(gsem, gcnt[0])])
                        Y.dma(t_g[b, t], gob.bitcast(BF16)[:, 0:L], gsem, gcnt,
                              waits=[(sv, V.n)])
            G.wait(dst2, c2_[0])
            G.wait(dst3, c3_[0])
            bar()

            V.region = "c"
            _mark(nc, 'c')
            if True:
                dvwo = Y.dma(woutres, AP(t_wsout.tensor, 0,
                                         [[D, 128], [128 * D, NDT], [1, D]]),
                             dwa, ca_, waits=[(dsg, STAGE_DONE), (st, T.n), (sv, V.n)])
                for b in range(NB):
                    for ch in range(NCH):
                        dvg = Y.dma(gst, t_g[b].transpose([1, 0, 2])[:, :, ch * CH:(ch + 1) * CH],
                                    dr, cr_, waits=[(st, T.n), (dsto, cs_[0])])
                        dvr = Y.dma(big0k, r_chunk_ap(b, ch), dr, cr_,
                                    waits=[(sv, V.n)])
                        dvg = dvr
                        v_mt = [0, 0]
                        for mt in range(NKT):
                            wb = mt % 2
                            for kt in range(NDT):
                                T.op('matmul', out=ps8[:, 6 + wb, :],
                                     lhsT=woutres[:, kt, mt * 128:(mt + 1) * 128],
                                     rhs=gst[:, kt, :], start=(kt == 0), stop=(kt == NDT - 1),
                                     waits=[(dwa, dvwo), (dr, dvg), (sv, v_mt[wb])]
                                     if kt == 0 else [])
                            V.op('tensor_tensor', out=big0k[:, mt, :], in0=big0k[:, mt, :],
                                 in1=ps8[:, 6 + wb, :], op=Alu.add,
                                 waits=[(st, T.n), (dr, dvr)])
                            v_mt[wb] = V.n
                        Y.dma(r_chunk_ap(b, ch), big0k, dsto, cs_, waits=[(sv, V.n)])
            G.wait(dsto, cs_[0])

            bar()
            for sem_ in loop_sems:
                nc.sync.sem_clear(sem_)
            bar()

        for e in (V, S, T, G, Y):
            e.reset()
        for c in (cw_, cr_, cs_, c0_, c1_, c2_, c3_, ca_, cb_2):
            c[0] = 0
        ln_sq_t[0] = 0

        V.region = "s2"
        _mark(nc, 'wquant')
        def quant_tile(x_ap, out_bf, width, extra_wait=()):
            nbk = width // 32
            xb = x_ap.rearrange("p (k c) -> p k c", c=32)
            V.op('tensor_reduce', out=q32[:, 0:nbk], in_=xb, axis=mybir.AxisListType.X,
                 op=Alu.max, apply_absolute_value=True, waits=list(extra_wait))
            V.op('tensor_scalar', out=q32[:, 0:nbk], in0=q32[:, 0:nbk], scalar1=1e-35,
                 scalar2=None, op0=Alu.max)
            V.op('reciprocal', out=q32b[:, 0:nbk], in_=q32[:, 0:nbk])
            V.op('tensor_scalar', out=q32b[:, 0:nbk], in0=q32b[:, 0:nbk], scalar1=7.5,
                 scalar2=None, op0=Alu.mult)
            V.op('reciprocal', out=q32[:, 0:nbk], in_=q32b[:, 0:nbk])
            V.op('tensor_scalar', out=q32[:, 0:nbk], in0=q32[:, 0:nbk], scalar1=1e-30,
                 scalar2=None, op0=Alu.max)
            V.op('reciprocal', out=q32b[:, 0:nbk], in_=q32[:, 0:nbk])
            V.op('tensor_tensor', out=qa[:, 0:width].rearrange("p (k c) -> p k c", c=32),
                 in0=xb, in1=q32b[:, 0:nbk].unsqueeze(2).broadcast_to([128, nbk, 32]),
                 op=Alu.mult)
            S.op('activation', out=qb[:, 0:width], in_=qa[:, 0:width],
                 func=Act.Square, waits=[(sv, V.n)])
            V.op('tensor_scalar', out=qc[:, 0:width], in0=qb[:, 0:width], scalar1=4.0,
                 scalar2=None, op0=Alu.is_ge, waits=[(ss, S.n)])
            V.op('tensor_scalar', out=qd[:, 0:width], in0=qb[:, 0:width], scalar1=16.0,
                 scalar2=None, op0=Alu.is_ge)
            V.op('tensor_scalar', out=qb[:, 0:width], in0=qc[:, 0:width], scalar1=0.125,
                 scalar2=0.125, op0=Alu.mult, op1=Alu.add)
            V.op('scalar_tensor_tensor', out=qb[:, 0:width], in0=qd[:, 0:width],
                 scalar=0.25, in1=qb[:, 0:width], op0=Alu.mult, op1=Alu.add)
            V.op('tensor_scalar', out=qc[:, 0:width], in0=qc[:, 0:width], scalar1=-4.0,
                 scalar2=8.0, op0=Alu.mult, op1=Alu.add)
            V.op('scalar_tensor_tensor', out=qd[:, 0:width], in0=qd[:, 0:width],
                 scalar=-2.0, in1=qc[:, 0:width], op0=Alu.mult, op1=Alu.add)
            V.op('tensor_tensor', out=qc[:, 0:width], in0=qa[:, 0:width],
                 in1=qd[:, 0:width], op=Alu.mult)
            V.op('tensor_scalar', out=qc[:, 0:width], in0=qc[:, 0:width], scalar1=MAGIC,
                 scalar2=-MAGIC, op0=Alu.add, op1=Alu.add)
            V.op('tensor_tensor', out=qb[:, 0:width], in0=qc[:, 0:width],
                 in1=qb[:, 0:width], op=Alu.mult)
            V.op('tensor_tensor', out=out_bf.rearrange("p (k c) -> p k c", c=32),
                 in0=qb[:, 0:width].rearrange("p (k c) -> p k c", c=32),
                 in1=q32[:, 0:nbk].unsqueeze(2).broadcast_to([128, nbk, 32]), op=Alu.mult)

        for wt in range(NKT):
            dvq = Y.dma(big0[:, 0:D], t_ffw[wt * 128:(wt + 1) * 128, :], dr, cr_,
                        waits=[(sv, V.n), (st, T.n)])
            V.wait(dr, dvq)
            quant_tile(big0[:, 0:D], qeb, D)
            for kt in range(NKT):
                T.op('transpose', out=ps8.bitcast(BF16)[:, kt % 2, 0:128],
                     in_=qeb[:, kt * 128:(kt + 1) * 128], identity=idb[:],
                     waits=[(sv, V.n), (ss, S.n)])
                S.op('copy', out=wdT[:, kt, wt * 128:(wt + 1) * 128],
                     in_=ps8.bitcast(BF16)[:, kt % 2, 0:128], waits=[(st, T.n)])
        bar()

        _mark(nc, 'stage2')
        for b in range(NB):
            for ch in range(NCH):
                dv1 = Y.dma(big0k, r_chunk_ap(b, ch), dr, cr_,
                            waits=[(sv, V.n), (ss, S.n), (dsto, cs_[0])])
                dv2 = Y.dma(x1buf, hs_chunk_ap(b, ch), dr, cr_)
                ln_chunk(big0k, extra_wait=[(dr, dv2)])
                for kt in range(NKT):
                    V.op('tensor_scalar', out=big0k[:, kt, :], in0=big0k[:, kt, :],
                         scalar1=fw8[:, kt:kt + 1], scalar2=fb8[:, kt:kt + 1],
                         op0=Alu.mult, op1=Alu.add)
                for kt in range(NKT):
                    V.op('scalar_tensor_tensor', out=x1buf[:, kt, :], in0=big0k[:, kt, :],
                         scalar=modT[:, 2 * NKT + kt, b:b+1], in1=x1buf[:, kt, :],
                         op0=Alu.mult, op1=Alu.add,
                         waits=[(dr, dv2)] if kt == 0 else [])
                V.op('tensor_copy', out=big0[:, 0:NKT * CH],
                     in_=x1buf.rearrange("p k c -> p (k c)"))
                ln_chunk(big0k, extra_wait=[(sv, V.n)])
                for kt in range(NKT):
                    V.op('tensor_scalar', out=big0k[:, kt, :], in0=big0k[:, kt, :],
                         scalar1=modT[:, 4 * NKT + kt, b:b+1], scalar2=modT[:, 3 * NKT + kt, b:b+1],
                         op0=Alu.mult, op1=Alu.add)
                for tb in range(4):
                    for half in range(2):
                        bk = 4 + half
                        for kq in range(4):
                            kt = half * 4 + kq
                            T.op('transpose', out=ps8[:, bk, kq * 128:(kq + 1) * 128],
                                 in_=big0k[:, kt, tb * 128:(tb + 1) * 128],
                                 identity=idf[:],
                                 waits=[(sv, V.n), (ss, S.n)] if kq == 0 else [])
                        S.op('copy', out=h2T[:, tb, half * CH:(half + 1) * CH],
                             in_=ps8[:, bk, :], waits=[(st, T.n)])
                for tb in range(4):
                    quant_tile(h2T[:, tb, :], qeb, D, extra_wait=[(ss, S.n)])
                    bk = 6 + (tb % 2)
                    pb = ps8.bitcast(BF16)[:, bk, 0:1024].rearrange(
                        "p (k c) -> p k c", k=NKT)
                    for kt in range(NKT):
                        T.op('transpose', out=pb[:, kt, :],
                             in_=qeb[:, kt * 128:(kt + 1) * 128], identity=idb[:],
                             waits=[(sv, V.n), (ss, S.n)] if kt == 0 else [])
                    S.op('copy', out=xdT[:, :, tb * 128:(tb + 1) * 128],
                         in_=pb, waits=[(st, T.n)])
                for mt in range(NKT):
                    bk = mt % 2
                    for kt in range(NKT):
                        T.op('matmul', out=ps8[:, bk, :],
                             lhsT=wdT[:, kt, mt * 128:(mt + 1) * 128], rhs=xdT[:, kt, :],
                             start=(kt == 0), stop=(kt == NKT - 1),
                             waits=[(sv, V.n), (ss, S.n)] if kt == 0 else [])
                    S.op('activation', out=big0k[:, mt, :], in_=ps8[:, bk, :],
                         func=Act.Identity, bias=ffb8[:, mt:mt + 1], waits=[(st, T.n)])
                    V.op('scalar_tensor_tensor', out=big0k[:, mt, :], in0=big0k[:, mt, :],
                         scalar=modT[:, 5 * NKT + mt, b:b+1], in1=x1buf[:, mt, :],
                         op0=Alu.mult, op1=Alu.add, waits=[(ss, S.n)])
                Y.dma(t_out[b].rearrange("(k p) l -> p k l", p=128)
                      [:, :, ch * CH:(ch + 1) * CH], big0k, dsto, cs_,
                      waits=[(sv, V.n)])
        G.wait(dsto, cs_[0])
    _mark(nc, 'end')
    return nc


_PROG = None


def _get_prog():
    global _PROG
    if _PROG is None:
        _PROG = build_program()
    return _PROG


def _layout(inputs):
    f = np.ascontiguousarray
    hs = np.asarray(inputs["L0_attn_hidden_states"])
    c = np.asarray(inputs["L0_c"])

    def vecN(x):
        n = x.shape[-1] // 128
        return f(np.asarray(x, np.float32).reshape(x.shape[0], n, 128).transpose(0, 2, 1))

    def vec1(x):
        n = x.shape[-1] // 128
        return f(np.asarray(x, np.float32).reshape(n, 128).T)

    A_lay = (np.asarray(inputs["L0_attn_A_log"], np.float32)
             .reshape(NL, NDT, 128, 16).transpose(0, 2, 1, 3).reshape(NL, 128, 256))
    conv_w = (np.asarray(inputs["L0_attn_conv_w"], np.float32)
              .reshape(NL, NDT, 128, 4).transpose(0, 2, 1, 3).reshape(NL, 128, NDT * 4))
    lay = np.concatenate([
        A_lay.reshape(NL, -1),
        conv_w.reshape(NL, -1),
        vecN(inputs["L0_attn_conv_b"]).reshape(NL, -1),
        vecN(inputs["L0_attn_dt_proj_b"]).reshape(NL, -1),
        vecN(inputs["L0_attn_D_param"]).reshape(NL, -1),
        vecN(inputs["L0_attn_norm_w"]).reshape(NL, -1),
        vecN(inputs["L0_attn_norm_b"]).reshape(NL, -1),
        np.asarray(inputs["L0_attn_x_proj_w"], np.float32).transpose(0, 2, 1).reshape(NL, -1),
        np.asarray(inputs["L0_attn_dt_proj_w"], np.float32).transpose(0, 2, 1).reshape(NL, -1),
    ], axis=1)
    assert lay.shape[1] == PW, (lay.shape, PW)
    com = dict(
        wmod=f(np.asarray(inputs["L0_wmod"], np.float32)),
        in_projT=f(np.asarray(inputs["L0_attn_in_proj_w"], np.float32).transpose(0, 2, 1)),
        out_projT=f(np.asarray(inputs["L0_attn_out_proj_w"], np.float32).transpose(0, 2, 1)),
        lay_params=f(lay),
        fnorm_w8=vec1(inputs["L0_attn_fnorm_w"]),
        fnorm_b8=vec1(inputs["L0_attn_fnorm_b"]),
        ffn_w=f(np.asarray(inputs["L0_ffn_weight"], np.float32)),
        ffn_b8=vec1(inputs["L0_ffn_bias"]),
    )
    maps = []
    for core in range(8):
        b0 = core * NB
        m = dict(com)
        m["hsT"] = f(hs[b0:b0 + NB].transpose(0, 2, 1).astype(np.float32))
        m["cT"] = f(c[b0:b0 + NB].T.astype(np.float32))
        maps.append(m)
    return maps


def kernel(**inputs):
    nc = _get_prog()
    in_maps = _layout(inputs)
    res = run_bass_kernel_spmd(nc, in_maps, core_ids=list(range(8)))
    outs = [np.asarray(r["outT"]).transpose(0, 2, 1) for r in res.results]
    return np.ascontiguousarray(np.concatenate(outs, axis=0)).astype(np.float32)


# version 14 (final)
# speedup vs baseline: 1.2702x; 1.0011x over previous
import numpy as np
import concourse.bass as bass
import concourse.mybir as mybir
from concourse.bass_utils import run_bass_kernel_spmd
from concourse.bass import AP

F32 = mybir.dt.float32
BF16 = mybir.dt.bfloat16
I32 = mybir.dt.int32
Alu = mybir.AluOpType
Act = mybir.ActivationFunctionType

NB = 2
L = 2048
D = 1024
DIN = 2048
NL = 4
NCH = 4
CH = 512
NKT = D // 128
NDT = DIN // 128
DTR = 64
GRP = 4
EPS = 1e-5
MAGIC = float(3 << 22)
NL_RUN = 4
SELF_SYNC = frozenset(["V"])
V_SYNC_REGIONS = frozenset(["prep", "c", "s2"])

PO_A = 0
PO_CW = PO_A + 128 * 256
PO_CB = PO_CW + 128 * 64
PO_DTB = PO_CB + 128 * 16
PO_DP = PO_DTB + 128 * 16
PO_NW = PO_DP + 128 * 16
PO_NB = PO_NW + 128 * 8
PO_XP = PO_NB + 128 * 8
PO_DTW = PO_XP + DIN * 96
PW = PO_DTW + DTR * DIN

O_LD0 = 0
O_LD1 = 2048
O_TMPA = 4096
O_TMPB = 6144
O_BIG0 = 8192
O_BIG1 = 12288
O_P = 16384
O_LNSQ = O_P + 20992
O_SROW = O_P + 25088
ARW = O_P + 30720


class Eng:

    def __init__(self, eng, sem, tag=""):
        self.e, self.sem, self.n = eng, sem, 0
        self.tag = tag
        self.region = "prep"

    @property
    def selfsync(self):
        if self.tag != "V":
            return self.tag in SELF_SYNC
        return self.region in V_SYNC_REGIONS

    def op(self, name, *a, waits=(), **kw):
        for s, v in waits:
            self.e.wait_ge(s, v)
        if self.selfsync and self.n > 0:
            self.e.wait_ge(self.sem, self.n)
        r = getattr(self.e, name)(*a, **kw)
        r.then_inc(self.sem, 1)
        self.n += 1
        return r

    def dma(self, out, in_, dsem, dcnt, waits=(), **kw):
        for s, v in waits:
            self.e.wait_ge(s, v)
        if self.selfsync and self.n > 0:
            self.e.wait_ge(self.sem, self.n)
        r = self.e.dma_start(out=out, in_=in_, **kw)
        r.then_inc(dsem, 16)
        dcnt[0] += 16
        return dcnt[0]

    def wait(self, sem, v):
        self.e.wait_ge(sem, v)

    def reset(self):
        self.n = 0


PHASE_MARKS = []


def _mark(nc, name):
    PHASE_MARKS.append((name, sum(1 for _ in nc.all_instructions())))


def build_program():
    nc = bass.Bass("TRN2", target_bir_lowering=False,
                   detect_race_conditions=False)

    t_hsT = nc.dram_tensor("hsT", [NB, D, L], F32, kind="ExternalInput").ap()
    t_cT = nc.dram_tensor("cT", [D, NB], F32, kind="ExternalInput").ap()
    t_wmod = nc.dram_tensor("wmod", [D, 6 * D], F32, kind="ExternalInput").ap()
    t_win = nc.dram_tensor("in_projT", [NL, D, 2 * DIN], F32, kind="ExternalInput").ap()
    t_wout = nc.dram_tensor("out_projT", [NL, DIN, D], F32, kind="ExternalInput").ap()
    t_lay = nc.dram_tensor("lay_params", [NL, PW], F32, kind="ExternalInput").ap()
    t_fw = nc.dram_tensor("fnorm_w8", [128, NKT], F32, kind="ExternalInput").ap()
    t_fb = nc.dram_tensor("fnorm_b8", [128, NKT], F32, kind="ExternalInput").ap()
    t_ffw = nc.dram_tensor("ffn_w", [D, D], F32, kind="ExternalInput").ap()
    t_ffb = nc.dram_tensor("ffn_b8", [128, NKT], F32, kind="ExternalInput").ap()

    t_out = nc.dram_tensor("outT", [NB, D, L], F32, kind="ExternalOutput").ap()

    t_winb = nc.dram_tensor("win_bf16", [NL, D, 2 * DIN], BF16, kind="Internal").ap()
    t_woutb = nc.dram_tensor("wout_bf16", [NL, DIN, D], BF16, kind="Internal").ap()
    t_R = nc.dram_tensor("R_s", [NB, NKT, 128, L], F32, kind="Internal").ap()
    t_xc = nc.dram_tensor("xc_s", [NB, NDT, 128, L], F32, kind="Internal").ap()
    t_z = nc.dram_tensor("z_s", [NB, NDT, 128, L], F32, kind="Internal").ap()
    t_dt = nc.dram_tensor("dt_s", [NB, NDT, 128, L], F32, kind="Internal").ap()
    t_g = nc.dram_tensor("g_s", [NB, NDT, 128, L], BF16, kind="Internal").ap()
    t_wsin = nc.dram_tensor("wsin", [D, 2 * DIN], BF16, kind="Internal").ap()
    t_wsout = nc.dram_tensor("wsout", [DIN, D], BF16, kind="Internal").ap()
    t_pstg = nc.dram_tensor("pstg", [PW], F32, kind="Internal").ap()

    import contextlib
    with contextlib.ExitStack() as _es:
        E = _es.enter_context
        AR = E(nc.sbuf_tensor("AR", [128, ARW], F32))
        ones1 = E(nc.sbuf_tensor("ones1", [128, 1], F32))
        ones1b = E(nc.sbuf_tensor("ones1b", [128, 1], BF16))
        cEPS = E(nc.sbuf_tensor("cEPS", [128, 1], F32))
        cNH = E(nc.sbuf_tensor("cNH", [128, 1], F32))
        onesr = E(nc.sbuf_tensor("onesr", [1, 128], F32))
        sel = E(nc.sbuf_tensor("sel", [16, 16, 128], BF16))
        modT = E(nc.sbuf_tensor("modT", [128, 6 * NKT, NB], F32))
        fw8 = E(nc.sbuf_tensor("fw8", [128, NKT], F32))
        fb8 = E(nc.sbuf_tensor("fb8", [128, NKT], F32))
        ffb8 = E(nc.sbuf_tensor("ffb8", [128, NKT], F32))
        idb = E(nc.sbuf_tensor("idb", [128, 128], BF16))
        idf = E(nc.sbuf_tensor("idf", [128, 128], F32))
        Abuf = E(nc.sbuf_tensor("Abuf", [128, 256], F32))
        cwb = E(nc.sbuf_tensor("cwb", [128, NDT * 4], F32))
        cbb = E(nc.sbuf_tensor("cbb", [128, NDT], F32))
        dtbb = E(nc.sbuf_tensor("dtbb", [128, NDT], F32))
        dpb = E(nc.sbuf_tensor("dpb", [128, NDT], F32))
        nw8 = E(nc.sbuf_tensor("nw8", [128, NKT], F32))
        nb8 = E(nc.sbuf_tensor("nb8", [128, NKT], F32))
        wstr = E(nc.sbuf_tensor("wstr", [128, 2, NDT, 128], BF16))
        q32 = E(nc.sbuf_tensor("q32", [128, 32], F32))
        q32b = E(nc.sbuf_tensor("q32b", [128, 32], F32))
        ps8 = E(nc.psum_tensor("ps8", [128, 8, 512], F32))
        sv = E(nc.semaphore("sv"))
        ss = E(nc.semaphore("ss"))
        st = E(nc.semaphore("st"))
        sg = E(nc.semaphore("sg"))
        sy = E(nc.semaphore("sy"))
        dw = E(nc.semaphore("dw"))
        dr = E(nc.semaphore("dr"))
        dsto = E(nc.semaphore("dsto"))
        dsg = E(nc.semaphore("dsg"))
        dst0 = E(nc.semaphore("dst0"))
        dst1 = E(nc.semaphore("dst1"))
        dst2 = E(nc.semaphore("dst2"))
        dst3 = E(nc.semaphore("dst3"))
        dwa = E(nc.semaphore("dwa"))
        dwb = E(nc.semaphore("dwb"))
        p1l = E(nc.semaphore("p1l"))
        p1s0 = E(nc.semaphore("p1s0"))
        p1s1 = E(nc.semaphore("p1s1"))
        p1s2 = E(nc.semaphore("p1s2"))

        V = Eng(nc.vector, sv, "V")
        S = Eng(nc.scalar, ss, "S")
        T = Eng(nc.tensor, st, "T")
        G = Eng(nc.gpsimd, sg, "G")
        Y = Eng(nc.sync, sy, "Y")
        cw_ = [0]
        cr_ = [0]
        cs_ = [0]
        c0_ = [0]
        c1_ = [0]
        c2_ = [0]
        c3_ = [0]
        ca_ = [0]
        cb_2 = [0]

        ARB = AR.bitcast(BF16)

        def fv(off, n):
            return AR[:, off:off + n]

        def bv(off, nbf):
            return ARB[:, 2 * off:2 * off + nbf]

        ld0 = fv(O_LD0, L)
        ld1 = fv(O_LD1, L)
        tmpA = fv(O_TMPA, L)
        tmpB = fv(O_TMPB, L)
        big0 = fv(O_BIG0, 4096)
        big1 = fv(O_BIG1, 4096)
        big0k = big0.rearrange("p (k c) -> p k c", k=NKT)
        big1k = big1.rearrange("p (k c) -> p k c", k=NKT)
        dtin = AR[0:DTR, O_TMPB:O_TMPB + CH]
        xhat = bv(O_P, 2 * 16384).rearrange("p (b k l) -> p b k l", b=NB, k=NKT)
        xpre = fv(O_P + 16384, L + 3)
        cv = fv(O_P + 18436, L)
        xpT = fv(O_P, NDT * 96).rearrange("p (t m) -> p t m", t=NDT)
        dtT = AR[0:DTR, O_P + 1536:O_P + 1536 + DIN]
        BCrowb = ARB[0:16, 2 * O_BIG0:2 * O_BIG0 + 2 * NB * L].rearrange(
            "p (b v l) -> p b v l", b=NB, v=2)
        dt_g = fv(O_P, GRP * L).rearrange("p (g l) -> p g l", g=GRP)
        dtx_g = fv(O_P + 8192, GRP * L).rearrange("p (g l) -> p g l", g=GRP)
        y_g = fv(O_P + 16384, GRP * L).rearrange("p (g l) -> p g l", g=GRP)
        dAb = fv(O_P + 24576, L)
        ub = fv(O_P + 26624, L)
        hbb = fv(O_P + 28672, L)
        gst = bv(O_P, NDT * CH).rearrange("p (t c) -> p t c", t=NDT)
        woutres = bv(O_P + 4096, NDT * D).rearrange("p (k m) -> p k m", k=NDT)
        wdT = bv(O_P, NKT * D).rearrange("p (k m) -> p k m", k=NKT)
        qa = fv(O_P + 4096, D)
        qb = fv(O_P + 5120, D)
        qc = fv(O_P + 6144, D)
        qd = fv(O_P + 7168, D)
        qeb = bv(O_P + 8192, D)
        x1buf = fv(O_P + 8704, NKT * CH).rearrange("p (k c) -> p k c", k=NKT)
        h2T = fv(O_P + 12800, 4 * D).rearrange("p (t d) -> p t d", t=4)
        xdT = bv(O_P + 16896, NKT * CH).rearrange("p (k c) -> p k c", k=NKT)

        def bar():
            nc.all_engine_barrier()

        _mark(nc, 'prep0')
        V.op('memset', ones1[:], 1.0)
        V.op('memset', ones1b[:], 1.0)
        V.op('memset', cEPS[:], EPS)
        V.op('memset', cNH[:], -0.5)
        V.op('memset', onesr[:], 1.0)
        io128 = E(nc.sbuf_tensor("io128", [128, 128], I32))
        G.op('iota', io128[:], [[-1, 128]], base=0, channel_multiplier=1)
        V.op('tensor_scalar', out=idb[:], in0=io128[:], scalar1=0, scalar2=None,
             op0=Alu.is_equal, waits=[(sg, G.n)])
        V.op('tensor_scalar', out=idf[:], in0=io128[:], scalar1=0, scalar2=None,
             op0=Alu.is_equal)
        G.op('iota', io128[0:16, :], [[0, 128]], base=0, channel_multiplier=1,
             waits=[(sv, V.n)])
        for n in range(16):
            V.op('tensor_scalar', out=sel[:, n, :], in0=io128[0:16, :], scalar1=n,
                 scalar2=None, op0=Alu.is_equal, waits=[(sg, G.n)] if n == 0 else [])
        bar()

        _mark(nc, 'prep1')
        jobs = []
        for li0 in range(NL):
            for kt in range(NKT):
                jobs.append((t_win, t_winb, (li0 * D + kt * 128) * 2 * DIN, 2 * DIN))
        for li0 in range(NL):
            for kt in range(NDT):
                jobs.append((t_wout, t_woutb, (li0 * DIN + kt * 128) * D, D))
        slabs = [fv(O_P + 6144 + i * 4096, 4096) for i in range(3)]
        pc0, pc1, pc2, pcl = [0], [0], [0], [0]
        p1_st = [(p1s0, pc0), (p1s1, pc1), (p1s2, pc2)]
        ldv = {}
        for j in range(3):
            src, dst, off, w = jobs[j]
            ldv[j] = G.dma(slabs[j][:, 0:w], AP(src.tensor, off, [[w, 128], [1, w]]),
                           p1l, pcl)
        for j, (src, dst, off, w) in enumerate(jobs):
            sb = j % 3
            stsem, stcnt = p1_st[sb]
            G.op('tensor_copy', out=slabs[sb].bitcast(BF16)[:, 0:w],
                 in_=slabs[sb][:, 0:w], waits=[(p1l, ldv[j])])
            G.dma(AP(dst.tensor, off, [[w, 128], [1, w]]),
                  slabs[sb].bitcast(BF16)[:, 0:w], stsem, stcnt,
                  waits=[(sg, G.n)])
            if j + 3 < len(jobs):
                src2, _, off2, w2 = jobs[j + 3]
                ldv[j + 3] = G.dma(slabs[sb][:, 0:w2],
                                   AP(src2.tensor, off2, [[w2, 128], [1, w2]]),
                                   p1l, pcl, waits=[(stsem, stcnt[0])])

        _mark(nc, 'prep2')
        dv = Y.dma(big0[:, 0:NKT * NB].rearrange("p (k b) -> p k b", k=NKT),
                   t_cT.rearrange("(k p) b -> p k b", p=128), dst0, c0_)
        S.op('activation', out=big0[:, 0:NKT * NB], in_=big0[:, 0:NKT * NB], func=Act.Silu,
             waits=[(dst0, c0_[0])])
        for jc in range(12):
            bank = jc % 2
            for kt in range(NKT):
                dv = Y.dma(tmpA[:, 0:512],
                           AP(t_wmod.tensor, kt * 128 * 6 * D + jc * 512,
                              [[6 * D, 128], [1, 512]]),
                           dr, cr_, waits=[(st, T.n)])
                T.op('matmul', out=ps8[0:NB, bank, :], lhsT=big0[:, 2 * kt:2 * kt + 2],
                     rhs=tmpA[:, 0:512], start=(kt == 0), stop=(kt == NKT - 1),
                     waits=[(dr, dv), (ss, S.n)])
            S.op('copy', out=AR[0:NB, O_P + jc * 512:O_P + (jc + 1) * 512],
                 in_=ps8[0:NB, bank, :], waits=[(st, T.n)])
        for blk in range(48):
            T.op('transpose', out=ps8[:, blk % 2, 0:NB],
                 in_=AR[0:NB, O_P + blk * 128:O_P + (blk + 1) * 128],
                 identity=idf[0:NB, 0:NB],
                 waits=[(ss, S.n), (sv, V.n)])
            V.op('tensor_copy', out=modT[:, blk, :], in_=ps8[:, blk % 2, 0:NB],
                 waits=[(st, T.n)])
        for v0 in (1, 4):
            V.op('tensor_scalar', out=modT[:, v0 * NKT:(v0 + 1) * NKT, :],
                 in0=modT[:, v0 * NKT:(v0 + 1) * NKT, :], scalar1=1.0, scalar2=None,
                 op0=Alu.add)
        for v0 in (2, 5):
            gsl = modT[:, v0 * NKT:(v0 + 1) * NKT, :]
            nn = NKT * NB
            V.op('tensor_scalar', out=big0[:, 0:nn], in0=gsl, scalar1=0.0, scalar2=None,
                 op0=Alu.is_ge)
            V.op('tensor_scalar', out=big0[:, 0:nn], in0=big0[:, 0:nn], scalar1=2.0,
                 scalar2=-1.0, op0=Alu.mult, op1=Alu.add)
            S.op('activation', out=big1[:, 0:nn], in_=gsl, func=Act.Abs, waits=[(sv, V.n)])
            V.op('tensor_scalar', out=big1[:, 0:nn], in0=big1[:, 0:nn], scalar1=0.25,
                 scalar2=None, op0=Alu.max, waits=[(ss, S.n)])
            V.op('tensor_tensor', out=gsl, in0=big0[:, 0:nn], in1=big1[:, 0:nn], op=Alu.mult)
        Y.dma(fw8[:], t_fw, dst0, c0_)
        Y.dma(fb8[:], t_fb, dst0, c0_)
        Y.dma(ffb8[:], t_ffb, dst0, c0_)
        Y.wait(dst0, c0_[0])
        bar()

        ln_sq_t = [0]

        def ln_chunk(xk, extra_wait=(), par=0):
            bs0, bs1 = (0, 1) if par == 0 else (4, 5)
            br0, br1 = (2, 3) if par == 0 else (6, 7)
            xkb = ARB[:, 2 * O_LNSQ:2 * O_LNSQ + NKT * CH].rearrange(
                "p (k c) -> p k c", k=NKT)
            sq = ARB[:, 2 * O_LNSQ + NKT * CH:2 * O_LNSQ + 2 * NKT * CH]
            sqk = sq.rearrange("p (k c) -> p k c", k=NKT)
            so = O_SROW + par * 1536
            mu = AR[0:1, so:so + CH]
            r_ = AR[0:1, so + CH:so + 2 * CH]
            tm = AR[0:1, so + 2 * CH:so + 3 * CH]
            S.op('copy', out=xkb.rearrange("p k c -> p (k c)"),
                 in_=xk.rearrange("p k c -> p (k c)"),
                 waits=list(extra_wait) + [(st, ln_sq_t[0])])
            S.op('activation', out=sq, in_=xk.rearrange("p k c -> p (k c)"),
                 func=Act.Square)
            for kt in range(NKT):
                T.op('matmul', out=ps8[0:1, bs0, :], lhsT=ones1b[:], rhs=xkb[:, kt, :],
                     start=(kt == 0), stop=(kt == NKT - 1),
                     waits=[(ss, S.n - 1)] if kt == 0 else [])
            for kt in range(NKT):
                T.op('matmul', out=ps8[0:1, bs1, :], lhsT=ones1b[:], rhs=sqk[:, kt, :],
                     start=(kt == 0), stop=(kt == NKT - 1),
                     waits=[(ss, S.n)] if kt == 0 else [])
            ln_sq_t[0] = T.n
            V.op('tensor_scalar', out=mu, in0=ps8[0:1, bs0, :], scalar1=1.0 / D, scalar2=None,
                 op0=Alu.mult, waits=[(st, T.n)])
            V.op('tensor_scalar', out=tm, in0=ps8[0:1, bs1, :], scalar1=1.0 / D, scalar2=None,
                 op0=Alu.mult)
            V.op('tensor_tensor', out=r_, in0=mu, in1=mu, op=Alu.mult)
            V.op('tensor_tensor', out=tm, in0=tm, in1=r_, op=Alu.subtract)
            S.op('activation', out=tm, in_=tm, func=Act.Ln, bias=cEPS[0:1, :],
                 waits=[(sv, V.n)])
            S.op('activation', out=r_, in_=tm, func=Act.Exp, scale=cNH[0:1, :])
            T.op('matmul', out=ps8[:, br0, :], lhsT=onesr[:], rhs=mu, start=True,
                 stop=True, waits=[(sv, V.n)])
            T.op('matmul', out=ps8[:, br1, :], lhsT=onesr[:], rhs=r_, start=True,
                 stop=True, waits=[(ss, S.n)])
            V.op('tensor_tensor', out=xk, in0=xk,
                 in1=ps8[:, br0, :].unsqueeze(1).broadcast_to([128, NKT, CH]),
                 op=Alu.subtract, waits=[(st, T.n)])
            V.op('tensor_tensor', out=xk, in0=xk,
                 in1=ps8[:, br1, :].unsqueeze(1).broadcast_to([128, NKT, CH]), op=Alu.mult)

        def r_chunk_ap(b, ch):
            return t_R[b].transpose([1, 0, 2])[:, :, ch * CH:(ch + 1) * CH]

        def hs_chunk_ap(b, ch):
            return t_hsT[b].rearrange("(k p) l -> p k l", p=128)[:, :, ch * CH:(ch + 1) * CH]

        _mark(nc, 'stage1')
        bigks = (big0k, big1k)
        st_sem = ((dst0, c0_), (dst1, c1_))
        vbuf = [0, 0]
        for idx, (b, ch) in enumerate([(b, ch) for b in range(NB) for ch in range(NCH)]):
            par = idx % 2
            xbk = bigks[par]
            psem, pcnt = st_sem[par]
            dv = Y.dma(xbk, hs_chunk_ap(b, ch), dr, cr_,
                       waits=[(sv, vbuf[par]), (psem, pcnt[0])])
            ln_chunk(xbk, extra_wait=[(dr, dv)], par=par)
            for kt in range(NKT):
                V.op('tensor_scalar', out=xbk[:, kt, :], in0=xbk[:, kt, :],
                     scalar1=modT[:, 1 * NKT + kt, b:b+1], scalar2=modT[:, 0 * NKT + kt, b:b+1],
                     op0=Alu.mult, op1=Alu.add)
            vbuf[par] = V.n
            Y.dma(r_chunk_ap(b, ch), xbk, psem, pcnt, waits=[(sv, V.n)])
        G.wait(dst0, c0_[0])
        G.wait(dst1, c1_[0])
        G.wait(p1s0, pc0[0])
        G.wait(p1s1, pc1[0])
        G.wait(p1s2, pc2[0])
        bar()

        _mark(nc, 'loop_head')
        loop_sems = (sv, ss, st, sg, sy, dw, dr, dsto, dsg, dst0, dst1, dst2, dst3, dwa, dwb)
        for sem_ in loop_sems:
            nc.sync.sem_clear(sem_)
        bar()
        with nc.Fori(0, NL_RUN) as li:
            for e in (V, S, T, G, Y):
                e.reset()
            for c in (cw_, cr_, cs_, c0_, c1_, c2_, c3_, ca_, cb_2):
                c[0] = 0
            ln_sq_t[0] = 0

            cg_ = [0]
            with nc.sync.register("lof") as lof:
                nc.sync.reg_mul(lof, li, PW)
                Y.dma(t_pstg[:], AP(t_lay.tensor, lof, [[1, PW]]), dsg, cg_)
                WQ = D * 2 * DIN // 4
                for j in range(4):
                    nc.sync.reg_mul(lof, li, D * 2 * DIN)
                    nc.sync.reg_add(lof, lof, j * WQ)
                    Y.dma(AP(t_wsin.tensor, j * WQ, [[1, WQ]]),
                          AP(t_winb.tensor, lof, [[1, WQ]]), dsg, cg_)
                WO = DIN * D // 2
                for j in range(2):
                    nc.sync.reg_mul(lof, li, DIN * D)
                    nc.sync.reg_add(lof, lof, j * WO)
                    Y.dma(AP(t_wsout.tensor, j * WO, [[1, WO]]),
                          AP(t_woutb.tensor, lof, [[1, WO]]), dsg, cg_)
            STAGE_DONE = cg_[0]
            Y.wait(dsg, STAGE_DONE)
            Y.dma(Abuf[:], AP(t_pstg.tensor, PO_A, [[256, 128], [1, 256]]), dw, cw_)
            Y.dma(cwb[:], AP(t_pstg.tensor, PO_CW, [[64, 128], [1, 64]]), dw, cw_)
            Y.dma(cbb[:], AP(t_pstg.tensor, PO_CB, [[16, 128], [1, 16]]), dw, cw_)
            Y.dma(dtbb[:], AP(t_pstg.tensor, PO_DTB, [[16, 128], [1, 16]]), dw, cw_)
            Y.dma(dpb[:], AP(t_pstg.tensor, PO_DP, [[16, 128], [1, 16]]), dw, cw_)
            Y.dma(nw8[:], AP(t_pstg.tensor, PO_NW, [[8, 128], [1, 8]]), dw, cw_)
            Y.dma(nb8[:], AP(t_pstg.tensor, PO_NB, [[8, 128], [1, 8]]), dw, cw_)
            S.op('activation', out=Abuf[:], in_=Abuf[:], func=Act.Exp, waits=[(dw, cw_[0])])
            V.op('tensor_scalar', out=Abuf[:], in0=Abuf[:], scalar1=-1.0, scalar2=None,
                 op0=Alu.mult, waits=[(ss, S.n)])
            bar()

            _mark(nc, 'a1')
            vbuf = [0, 0]
            for idx, (b, ch) in enumerate([(b, ch) for b in range(NB) for ch in range(NCH)]):
                par = idx % 2
                xbk = bigks[par]
                dv = Y.dma(xbk, r_chunk_ap(b, ch), dr, cr_,
                           waits=[(sv, vbuf[par])])
                ln_chunk(xbk, extra_wait=[(dr, dv)], par=par)
                for kt in range(NKT):
                    V.op('tensor_scalar', out=xhat[:, b, kt, ch * CH:(ch + 1) * CH],
                         in0=xbk[:, kt, :], scalar1=nw8[:, kt:kt + 1],
                         scalar2=nb8[:, kt:kt + 1], op0=Alu.mult, op1=Alu.add)
                vbuf[par] = V.n
            bar()

            V.region = "a2"
            _mark(nc, 'a2')
            V.op('memset', xpre[:, 0:3], 0.0)
            if True:
                last_store = [0, 0]
                t_after_mt = [0, 0]
                s_bank = [0] * 8

                def wsin_ap(m):
                    return AP(t_wsin.tensor, m * 128,
                              [[2 * DIN, 128], [128 * 2 * DIN, NKT], [1, 128]])

                dvw_slot = [0, 0]
                for m0 in range(2):
                    wsem, wcnt = ((dwa, ca_), (dwb, cb_2))[m0]
                    dvw_slot[m0] = Y.dma(wstr[:, m0, 0:NKT, :], wsin_ap(m0),
                                         wsem, wcnt, waits=[(dsg, STAGE_DONE)])
                for mt in range(2 * NDT):
                    wb = mt % 2
                    wsem, wcnt = ((dwa, ca_), (dwb, cb_2))[wb]
                    dvw = dvw_slot[wb]
                    for kt in range(NKT):
                        for b in range(NB):
                            for ch in range(NCH):
                                w0 = []
                                if kt == 0:
                                    w0 = [(ss, s_bank[b * NCH + ch])]
                                    if b == 0 and ch == 0:
                                        w0.append((wsem, dvw))
                                T.op('matmul', out=ps8[:, b * NCH + ch, :],
                                     lhsT=wstr[:, wb, kt, :],
                                     rhs=xhat[:, b, kt, ch * CH:(ch + 1) * CH],
                                     start=(kt == 0), stop=(kt == NKT - 1), waits=w0)
                    t_after_mt[wb] = T.n
                    if mt < NDT:
                        for b in range(NB):
                            ob = (ld0, ld1)[b]
                            bsem, bcnt = ((dst0, c0_), (dst1, c1_))[b]
                            for ch in range(NCH):
                                S.op('copy', out=xpre[:, 3 + ch * CH:3 + (ch + 1) * CH],
                                     in_=ps8[:, b * NCH + ch, :],
                                     waits=[(st, T.n), (sv, V.n)] if ch == 0 else [])
                                s_bank[b * NCH + ch] = S.n
                            t4 = mt * 4
                            V.op('tensor_scalar', out=cv[:], in0=xpre[:, 0:L],
                                 scalar1=cwb[:, t4:t4 + 1], scalar2=None, op0=Alu.mult,
                                 waits=[(ss, S.n)])
                            for k in range(1, 4):
                                V.op('scalar_tensor_tensor', out=cv[:], in0=xpre[:, k:L + k],
                                     scalar=cwb[:, t4 + k:t4 + k + 1], in1=cv[:],
                                     op0=Alu.mult, op1=Alu.add)
                            S.op('activation', out=ob, in_=cv[:], func=Act.Silu,
                                 bias=cbb[:, mt:mt + 1],
                                 waits=[(sv, V.n), (bsem, bcnt[0])])
                            S.dma(t_xc[b, mt], ob, bsem, bcnt, waits=[(ss, S.n)])
                    else:
                        for b in range(NB):
                            for ch in range(NCH):
                                S.op('copy', out=big1[:, ch * CH:(ch + 1) * CH],
                                     in_=ps8[:, b * NCH + ch, :],
                                     waits=([(st, T.n), (dst2, c2_[0])]
                                            if ch == 0 else []))
                                s_bank[b * NCH + ch] = S.n
                            S.dma(t_z[b, mt - NDT], big1[:, 0:L], dst2, c2_,
                                  waits=[(ss, S.n)])
                    if mt + 2 < 2 * NDT:
                        dvw_slot[wb] = S.dma(wstr[:, wb, 0:NKT, :], wsin_ap(mt + 2),
                                             wsem, wcnt)
            G.wait(dst0, c0_[0])
            G.wait(dst1, c1_[0])
            G.wait(dst2, c2_[0])
            bar()

            V.region = "a3"
            _mark(nc, 'a3')
            dvx1 = Y.dma(xpT, AP(t_pstg.tensor, PO_XP,
                                 [[96, 128], [128 * 96, NDT], [1, 96]]), dw, cw_)
            dvx2 = Y.dma(dtT, AP(t_pstg.tensor, PO_DTW, [[DIN, DTR], [1, DIN]]), dw, cw_)
            xpTb = bv(O_P + 3584, NDT * 96).rearrange("p (t m) -> p t m", t=NDT)
            dtTb = ARB[0:DTR, 2 * (O_P + 4352):2 * (O_P + 4352) + DIN]
            V.op('tensor_copy', out=xpTb, in_=xpT, waits=[(dw, dvx2)])
            V.op('tensor_copy', out=dtTb, in_=dtT)
            dtinB = ARB[0:DTR, 2 * (O_TMPB + CH):2 * (O_TMPB + CH) + CH]
            lbF = (ld0[:, 0:CH], ld1[:, 0:CH], tmpA[:, 0:CH], tmpA[:, CH:2 * CH])
            lbB4 = (ARB[:, 2 * (O_LD0 + CH):2 * (O_LD0 + CH) + CH],
                    ARB[:, 2 * (O_LD1 + CH):2 * (O_LD1 + CH) + CH],
                    ARB[:, 2 * (O_TMPA + 2 * CH):2 * (O_TMPA + 2 * CH) + CH],
                    ARB[:, 2 * (O_TMPA + 2 * CH) + CH:2 * (O_TMPA + 2 * CH) + 2 * CH])
            chunks = [(b, ch) for b in range(NB) for ch in range(NCH)]
            NQ = len(chunks) * NDT

            def xc_ap(q):
                qb, qch = chunks[q // NDT]
                return t_xc[qb, q % NDT, :, qch * CH:(qch + 1) * CH]

            dvxq = {}
            t_slot = [0, 0, 0, 0]
            for q in range(4):
                dvxq[q] = Y.dma(lbF[q], xc_ap(q), dr, cr_)
            dt_bank_s = [0, 0, 0, 0]
            dt_sems = ((dst0, c0_), (dst1, c1_), (dst2, c2_), (dst3, c3_))
            for ci, (b, ch) in enumerate(chunks):
                for kt in range(NDT):
                    q = ci * NDT + kt
                    sl = q % 4
                    S.op('copy', out=lbB4[sl], in_=lbF[sl], waits=[(dr, dvxq[q])])
                    T.op('matmul', out=ps8[0:DTR, 0, :], lhsT=xpTb[:, kt, 0:DTR],
                         rhs=lbB4[sl], start=(kt == 0), stop=(kt == NDT - 1),
                         waits=[(ss, S.n)] +
                               ([(sv, V.n)] if kt == 0 else []))
                    T.op('matmul', out=ps8[0:16, 1, :], lhsT=xpTb[:, kt, DTR:DTR + 16],
                         rhs=lbB4[sl], start=(kt == 0), stop=(kt == NDT - 1))
                    T.op('matmul', out=ps8[0:16, 2, :], lhsT=xpTb[:, kt, DTR + 16:96],
                         rhs=lbB4[sl], start=(kt == 0), stop=(kt == NDT - 1))
                    t_slot[sl] = T.n
                    if q + 4 < NQ:
                        dvxq[q + 4] = Y.dma(lbF[sl], xc_ap(q + 4), dr, cr_,
                                            waits=[(st, t_slot[sl])])
                S.op('copy', out=dtinB, in_=ps8[0:DTR, 0, :], waits=[(st, T.n)])
                s_dtin = S.n
                V.op('tensor_copy', out=BCrowb[:, b, 0, ch * CH:(ch + 1) * CH],
                     in_=ps8[0:16, 1, :], waits=[(st, T.n)])
                V.op('tensor_copy', out=BCrowb[:, b, 1, ch * CH:(ch + 1) * CH],
                     in_=ps8[0:16, 2, :])
                for mt in range(NDT):
                    jf = mt % 4
                    bk = 3 + jf
                    hsem, hcnt = dt_sems[jf]
                    T.op('matmul', out=ps8[:, bk, :],
                         lhsT=dtTb[:, mt * 128:(mt + 1) * 128], rhs=dtinB,
                         start=True, stop=True,
                         waits=[(ss, max(s_dtin, dt_bank_s[jf]))])
                    S.op('copy', out=big1[:, jf * CH:(jf + 1) * CH],
                         in_=ps8[:, bk, :],
                         waits=[(st, T.n), (hsem, hcnt[0])])
                    dt_bank_s[jf] = S.n
                    S.dma(t_dt[b, mt, :, ch * CH:(ch + 1) * CH],
                          big1[:, jf * CH:(jf + 1) * CH], hsem, hcnt,
                          waits=[(ss, S.n)])
            G.wait(dst0, c0_[0])
            G.wait(dst1, c1_[0])
            G.wait(dst2, c2_[0])
            G.wait(dst3, c3_[0])
            bar()

            V.region = "bprep"
            _mark(nc, 'b')
            BrepB = tmpA.bitcast(BF16)[:, 0:L]
            CrepB = tmpA.bitcast(BF16)[:, L:2 * L]
            groups = [(b, g0) for b in range(NB) for g0 in range(0, NDT, GRP)]
            pfh = (big1[:, 0:L], big1[:, L:2 * L])
            for gi, (b, g0) in enumerate(groups):
                if True:
                    first = gi == 0
                    V.region = "bprep"
                    xcls = []
                    for ti in range(GRP):
                        t = g0 + ti
                        w0 = [(sv, V.n), (ss, S.n), (dst2, c2_[0]),
                              (dst3, c3_[0])] if ti == 0 else []
                        if ti >= 2 or first:
                            Y.dma(dt_g[:, ti, :], t_dt[b, t], dr, cr_, waits=w0)
                            w0 = []
                        xcl = (ld0, ld1, tmpB, tmpA)[ti]
                        xcls.append(xcl)
                        Y.dma(xcl, t_xc[b, t], dr, cr_, waits=w0)
                    GL = cr_[0]
                    for ti in range(GRP):
                        t = g0 + ti
                        xcl = xcls[ti]
                        dtsrc = dt_g[:, ti, :] if (ti >= 2 or first) else pfh[ti]
                        V.op('tensor_scalar', out=dt_g[:, ti, :], in0=dtsrc,
                             scalar1=dtbb[:, t:t + 1], scalar2=None, op0=Alu.add,
                             waits=[(dr, GL)])
                        S.op('activation', out=dAb[:], in_=dt_g[:, ti, :], func=Act.Relu,
                             waits=[(sv, V.n)])
                        V.op('scalar_tensor_tensor', out=ub[:], in0=dAb[:], scalar=-2.0,
                             in1=dt_g[:, ti, :], op0=Alu.mult, op1=Alu.add,
                             waits=[(ss, S.n)])
                        S.op('activation', out=ub[:], in_=ub[:], func=Act.Exp,
                             waits=[(sv, V.n)])
                        S.op('activation', out=ub[:], in_=ub[:], func=Act.Ln, bias=ones1[:])
                        V.op('tensor_tensor', out=dt_g[:, ti, :], in0=dAb[:], in1=ub[:],
                             op=Alu.add, waits=[(ss, S.n)])
                        V.op('tensor_tensor', out=dtx_g[:, ti, :], in0=dt_g[:, ti, :],
                             in1=xcl, op=Alu.mult)
                    if gi + 1 < len(groups):
                        nb_, ng0 = groups[gi + 1]
                        Y.dma(pfh[0], t_dt[nb_, ng0 + 0], dr, cr_,
                              waits=[(sv, V.n)])
                        Y.dma(pfh[1], t_dt[nb_, ng0 + 1], dr, cr_)
                    V.region = "bscan"
                    vread_rep = [V.n, V.n]
                    vread_dA = [V.n, V.n]
                    for n in range(16):
                        np_ = n % 2
                        repbuf = (tmpA, ld0)[np_]
                        Bp = repbuf.bitcast(BF16)[:, 0:L]
                        Cp = repbuf.bitcast(BF16)[:, L:2 * L]
                        for cc in range(4):
                            T.op('matmul', out=ps8[:, cc, :], lhsT=sel[:, n, :],
                                 rhs=BCrowb[:, b, 0, cc * CH:(cc + 1) * CH], start=True,
                                 stop=True,
                                 waits=[(sv, vread_rep[np_]), (ss, S.n)] if cc == 0 else [])
                            T.op('matmul', out=ps8[:, 4 + cc, :], lhsT=sel[:, n, :],
                                 rhs=BCrowb[:, b, 1, cc * CH:(cc + 1) * CH], start=True,
                                 stop=True)
                        for cc in range(4):
                            S.op('copy', out=Bp[:, cc * CH:(cc + 1) * CH],
                                 in_=ps8[:, cc, :],
                                 waits=[(st, T.n)] if cc == 0 else [])
                            S.op('copy', out=Cp[:, cc * CH:(cc + 1) * CH],
                                 in_=ps8[:, 4 + cc, :])
                        s_rep = S.n
                        for ti in range(GRP):
                            t = g0 + ti
                            tp = ti % 2
                            dab = (dAb, tmpB)[tp]
                            S.op('activation', out=dab[:], in_=dt_g[:, ti, :], func=Act.Exp,
                                 scale=Abuf[:, t * 16 + n:t * 16 + n + 1],
                                 waits=[(sv, vread_dA[tp])])
                            s_exp = S.n
                            V.op('tensor_tensor', out=ub[:], in0=dtx_g[:, ti, :], in1=Bp,
                                 op=Alu.mult, waits=[(ss, s_rep)])
                            V.op('tensor_tensor_scan', out=hbb[:], data0=dab[:], data1=ub[:],
                                 initial=0.0, op0=Alu.mult, op1=Alu.add,
                                 waits=[(ss, s_exp)])
                            vread_dA[tp] = V.n
                            if n == 0:
                                V.op('tensor_tensor', out=y_g[:, ti, :], in0=hbb[:],
                                     in1=Cp, op=Alu.mult)
                            else:
                                V.op('tensor_tensor', out=ub[:], in0=hbb[:], in1=Cp,
                                     op=Alu.mult)
                                V.op('tensor_tensor', out=y_g[:, ti, :], in0=y_g[:, ti, :],
                                     in1=ub[:], op=Alu.add)
                            vread_rep[np_] = V.n
                    V.region = "bgate"
                    for ti in range(GRP):
                        t = g0 + ti
                        dv1 = Y.dma(ld0, t_z[b, t], dr, cr_, waits=[(ss, S.n), (sv, V.n)])
                        dv2 = Y.dma(ld1, t_xc[b, t], dr, cr_)
                        gob = (dAb, hbb)[ti % 2]
                        gsem, gcnt = ((dst2, c2_), (dst3, c3_))[ti % 2]
                        S.op('activation', out=tmpB[:], in_=ld0, func=Act.Silu,
                             waits=[(dr, dv2), (sv, V.n)])
                        V.op('scalar_tensor_tensor', out=y_g[:, ti, :], in0=ld1,
                             scalar=dpb[:, t:t + 1], in1=y_g[:, ti, :], op0=Alu.mult,
                             op1=Alu.add, waits=[(dr, dv2), (ss, S.n)])
                        V.op('tensor_tensor', out=gob.bitcast(BF16)[:, 0:L],
                             in0=y_g[:, ti, :], in1=tmpB[:], op=Alu.mult,
                             waits=[(gsem, gcnt[0])])
                        Y.dma(t_g[b, t], gob.bitcast(BF16)[:, 0:L], gsem, gcnt,
                              waits=[(sv, V.n)])
            G.wait(dst2, c2_[0])
            G.wait(dst3, c3_[0])
            bar()

            V.region = "c"
            _mark(nc, 'c')
            if True:
                dvwo = Y.dma(woutres, AP(t_wsout.tensor, 0,
                                         [[D, 128], [128 * D, NDT], [1, D]]),
                             dwa, ca_, waits=[(dsg, STAGE_DONE), (st, T.n), (sv, V.n)])
                for b in range(NB):
                    for ch in range(NCH):
                        dvg = Y.dma(gst, t_g[b].transpose([1, 0, 2])[:, :, ch * CH:(ch + 1) * CH],
                                    dr, cr_, waits=[(st, T.n), (dsto, cs_[0])])
                        dvr = Y.dma(big0k, r_chunk_ap(b, ch), dr, cr_,
                                    waits=[(sv, V.n)])
                        dvg = dvr
                        v_mt = [0, 0]
                        for mt in range(NKT):
                            wb = mt % 2
                            for kt in range(NDT):
                                T.op('matmul', out=ps8[:, 6 + wb, :],
                                     lhsT=woutres[:, kt, mt * 128:(mt + 1) * 128],
                                     rhs=gst[:, kt, :], start=(kt == 0), stop=(kt == NDT - 1),
                                     waits=[(dwa, dvwo), (dr, dvg), (sv, v_mt[wb])]
                                     if kt == 0 else [])
                            V.op('tensor_tensor', out=big0k[:, mt, :], in0=big0k[:, mt, :],
                                 in1=ps8[:, 6 + wb, :], op=Alu.add,
                                 waits=[(st, T.n), (dr, dvr)])
                            v_mt[wb] = V.n
                        Y.dma(r_chunk_ap(b, ch), big0k, dsto, cs_, waits=[(sv, V.n)])
            G.wait(dsto, cs_[0])

            bar()
            for sem_ in loop_sems:
                nc.sync.sem_clear(sem_)
            bar()

        for e in (V, S, T, G, Y):
            e.reset()
        for c in (cw_, cr_, cs_, c0_, c1_, c2_, c3_, ca_, cb_2):
            c[0] = 0
        ln_sq_t[0] = 0

        V.region = "s2"
        _mark(nc, 'wquant')
        def quant_tile(x_ap, out_bf, width, extra_wait=()):
            nbk = width // 32
            xb = x_ap.rearrange("p (k c) -> p k c", c=32)
            V.op('tensor_reduce', out=q32[:, 0:nbk], in_=xb, axis=mybir.AxisListType.X,
                 op=Alu.max, apply_absolute_value=True, waits=list(extra_wait))
            V.op('tensor_scalar', out=q32[:, 0:nbk], in0=q32[:, 0:nbk], scalar1=1e-35,
                 scalar2=None, op0=Alu.max)
            V.op('reciprocal', out=q32b[:, 0:nbk], in_=q32[:, 0:nbk])
            V.op('tensor_scalar', out=q32b[:, 0:nbk], in0=q32b[:, 0:nbk], scalar1=7.5,
                 scalar2=None, op0=Alu.mult)
            V.op('reciprocal', out=q32[:, 0:nbk], in_=q32b[:, 0:nbk])
            V.op('tensor_scalar', out=q32[:, 0:nbk], in0=q32[:, 0:nbk], scalar1=1e-30,
                 scalar2=None, op0=Alu.max)
            V.op('reciprocal', out=q32b[:, 0:nbk], in_=q32[:, 0:nbk])
            V.op('tensor_tensor', out=qa[:, 0:width].rearrange("p (k c) -> p k c", c=32),
                 in0=xb, in1=q32b[:, 0:nbk].unsqueeze(2).broadcast_to([128, nbk, 32]),
                 op=Alu.mult)
            S.op('activation', out=qb[:, 0:width], in_=qa[:, 0:width],
                 func=Act.Square, waits=[(sv, V.n)])
            V.op('tensor_scalar', out=qc[:, 0:width], in0=qb[:, 0:width], scalar1=4.0,
                 scalar2=None, op0=Alu.is_ge, waits=[(ss, S.n)])
            V.op('tensor_scalar', out=qd[:, 0:width], in0=qb[:, 0:width], scalar1=16.0,
                 scalar2=None, op0=Alu.is_ge)
            V.op('tensor_scalar', out=qb[:, 0:width], in0=qc[:, 0:width], scalar1=0.125,
                 scalar2=0.125, op0=Alu.mult, op1=Alu.add)
            V.op('scalar_tensor_tensor', out=qb[:, 0:width], in0=qd[:, 0:width],
                 scalar=0.25, in1=qb[:, 0:width], op0=Alu.mult, op1=Alu.add)
            V.op('tensor_scalar', out=qc[:, 0:width], in0=qc[:, 0:width], scalar1=-4.0,
                 scalar2=8.0, op0=Alu.mult, op1=Alu.add)
            V.op('scalar_tensor_tensor', out=qd[:, 0:width], in0=qd[:, 0:width],
                 scalar=-2.0, in1=qc[:, 0:width], op0=Alu.mult, op1=Alu.add)
            V.op('tensor_tensor', out=qc[:, 0:width], in0=qa[:, 0:width],
                 in1=qd[:, 0:width], op=Alu.mult)
            V.op('tensor_scalar', out=qc[:, 0:width], in0=qc[:, 0:width], scalar1=MAGIC,
                 scalar2=-MAGIC, op0=Alu.add, op1=Alu.add)
            V.op('tensor_tensor', out=qb[:, 0:width], in0=qc[:, 0:width],
                 in1=qb[:, 0:width], op=Alu.mult)
            V.op('tensor_tensor', out=out_bf.rearrange("p (k c) -> p k c", c=32),
                 in0=qb[:, 0:width].rearrange("p (k c) -> p k c", c=32),
                 in1=q32[:, 0:nbk].unsqueeze(2).broadcast_to([128, nbk, 32]), op=Alu.mult)

        for wt in range(NKT):
            dvq = Y.dma(big0[:, 0:D], t_ffw[wt * 128:(wt + 1) * 128, :], dr, cr_,
                        waits=[(sv, V.n), (st, T.n)])
            V.wait(dr, dvq)
            quant_tile(big0[:, 0:D], qeb, D)
            for kt in range(NKT):
                T.op('transpose', out=ps8.bitcast(BF16)[:, kt % 2, 0:128],
                     in_=qeb[:, kt * 128:(kt + 1) * 128], identity=idb[:],
                     waits=[(sv, V.n), (ss, S.n)])
                S.op('copy', out=wdT[:, kt, wt * 128:(wt + 1) * 128],
                     in_=ps8.bitcast(BF16)[:, kt % 2, 0:128], waits=[(st, T.n)])
        bar()

        _mark(nc, 'stage2')
        for b in range(NB):
            for ch in range(NCH):
                dv1 = Y.dma(big0k, r_chunk_ap(b, ch), dr, cr_,
                            waits=[(sv, V.n), (ss, S.n), (dsto, cs_[0])])
                dv2 = Y.dma(x1buf, hs_chunk_ap(b, ch), dr, cr_)
                ln_chunk(big0k, extra_wait=[(dr, dv2)])
                for kt in range(NKT):
                    V.op('tensor_scalar', out=big0k[:, kt, :], in0=big0k[:, kt, :],
                         scalar1=fw8[:, kt:kt + 1], scalar2=fb8[:, kt:kt + 1],
                         op0=Alu.mult, op1=Alu.add)
                for kt in range(NKT):
                    V.op('scalar_tensor_tensor', out=x1buf[:, kt, :], in0=big0k[:, kt, :],
                         scalar=modT[:, 2 * NKT + kt, b:b+1], in1=x1buf[:, kt, :],
                         op0=Alu.mult, op1=Alu.add,
                         waits=[(dr, dv2)] if kt == 0 else [])
                V.op('tensor_copy', out=big0[:, 0:NKT * CH],
                     in_=x1buf.rearrange("p k c -> p (k c)"))
                ln_chunk(big0k, extra_wait=[(sv, V.n)])
                for kt in range(NKT):
                    V.op('tensor_scalar', out=big0k[:, kt, :], in0=big0k[:, kt, :],
                         scalar1=modT[:, 4 * NKT + kt, b:b+1], scalar2=modT[:, 3 * NKT + kt, b:b+1],
                         op0=Alu.mult, op1=Alu.add)
                for tb in range(4):
                    for half in range(2):
                        bk = 4 + half
                        for kq in range(4):
                            kt = half * 4 + kq
                            T.op('transpose', out=ps8[:, bk, kq * 128:(kq + 1) * 128],
                                 in_=big0k[:, kt, tb * 128:(tb + 1) * 128],
                                 identity=idf[:],
                                 waits=[(sv, V.n), (ss, S.n)] if kq == 0 else [])
                        S.op('copy', out=h2T[:, tb, half * CH:(half + 1) * CH],
                             in_=ps8[:, bk, :], waits=[(st, T.n)])
                for tb in range(4):
                    quant_tile(h2T[:, tb, :], qeb, D, extra_wait=[(ss, S.n)])
                    bk = 6 + (tb % 2)
                    pb = ps8.bitcast(BF16)[:, bk, 0:1024].rearrange(
                        "p (k c) -> p k c", k=NKT)
                    for kt in range(NKT):
                        T.op('transpose', out=pb[:, kt, :],
                             in_=qeb[:, kt * 128:(kt + 1) * 128], identity=idb[:],
                             waits=[(sv, V.n), (ss, S.n)] if kt == 0 else [])
                    S.op('copy', out=xdT[:, :, tb * 128:(tb + 1) * 128],
                         in_=pb, waits=[(st, T.n)])
                for mt in range(NKT):
                    bk = mt % 2
                    for kt in range(NKT):
                        T.op('matmul', out=ps8[:, bk, :],
                             lhsT=wdT[:, kt, mt * 128:(mt + 1) * 128], rhs=xdT[:, kt, :],
                             start=(kt == 0), stop=(kt == NKT - 1),
                             waits=[(sv, V.n), (ss, S.n)] if kt == 0 else [])
                    S.op('activation', out=big0k[:, mt, :], in_=ps8[:, bk, :],
                         func=Act.Identity, bias=ffb8[:, mt:mt + 1], waits=[(st, T.n)])
                    V.op('scalar_tensor_tensor', out=big0k[:, mt, :], in0=big0k[:, mt, :],
                         scalar=modT[:, 5 * NKT + mt, b:b+1], in1=x1buf[:, mt, :],
                         op0=Alu.mult, op1=Alu.add, waits=[(ss, S.n)])
                Y.dma(t_out[b].rearrange("(k p) l -> p k l", p=128)
                      [:, :, ch * CH:(ch + 1) * CH], big0k, dsto, cs_,
                      waits=[(sv, V.n)])
        G.wait(dsto, cs_[0])
    _mark(nc, 'end')
    return nc


_PROG = None


def _get_prog():
    global _PROG
    if _PROG is None:
        _PROG = build_program()
    return _PROG


def _layout(inputs):
    f = np.ascontiguousarray
    hs = np.asarray(inputs["L0_attn_hidden_states"])
    c = np.asarray(inputs["L0_c"])

    def vecN(x):
        n = x.shape[-1] // 128
        return f(np.asarray(x, np.float32).reshape(x.shape[0], n, 128).transpose(0, 2, 1))

    def vec1(x):
        n = x.shape[-1] // 128
        return f(np.asarray(x, np.float32).reshape(n, 128).T)

    A_lay = (np.asarray(inputs["L0_attn_A_log"], np.float32)
             .reshape(NL, NDT, 128, 16).transpose(0, 2, 1, 3).reshape(NL, 128, 256))
    conv_w = (np.asarray(inputs["L0_attn_conv_w"], np.float32)
              .reshape(NL, NDT, 128, 4).transpose(0, 2, 1, 3).reshape(NL, 128, NDT * 4))
    lay = np.concatenate([
        A_lay.reshape(NL, -1),
        conv_w.reshape(NL, -1),
        vecN(inputs["L0_attn_conv_b"]).reshape(NL, -1),
        vecN(inputs["L0_attn_dt_proj_b"]).reshape(NL, -1),
        vecN(inputs["L0_attn_D_param"]).reshape(NL, -1),
        vecN(inputs["L0_attn_norm_w"]).reshape(NL, -1),
        vecN(inputs["L0_attn_norm_b"]).reshape(NL, -1),
        np.asarray(inputs["L0_attn_x_proj_w"], np.float32).transpose(0, 2, 1).reshape(NL, -1),
        np.asarray(inputs["L0_attn_dt_proj_w"], np.float32).transpose(0, 2, 1).reshape(NL, -1),
    ], axis=1)
    assert lay.shape[1] == PW, (lay.shape, PW)
    com = dict(
        wmod=f(np.asarray(inputs["L0_wmod"], np.float32)),
        in_projT=f(np.asarray(inputs["L0_attn_in_proj_w"], np.float32).transpose(0, 2, 1)),
        out_projT=f(np.asarray(inputs["L0_attn_out_proj_w"], np.float32).transpose(0, 2, 1)),
        lay_params=f(lay),
        fnorm_w8=vec1(inputs["L0_attn_fnorm_w"]),
        fnorm_b8=vec1(inputs["L0_attn_fnorm_b"]),
        ffn_w=f(np.asarray(inputs["L0_ffn_weight"], np.float32)),
        ffn_b8=vec1(inputs["L0_ffn_bias"]),
    )
    maps = []
    for core in range(8):
        b0 = core * NB
        m = dict(com)
        m["hsT"] = f(hs[b0:b0 + NB].transpose(0, 2, 1).astype(np.float32))
        m["cT"] = f(c[b0:b0 + NB].T.astype(np.float32))
        maps.append(m)
    return maps


def kernel(**inputs):
    nc = _get_prog()
    in_maps = _layout(inputs)
    res = run_bass_kernel_spmd(nc, in_maps, core_ids=list(range(8)))
    outs = [np.asarray(r["outT"]).transpose(0, 2, 1) for r in res.results]
    return np.ascontiguousarray(np.concatenate(outs, axis=0)).astype(np.float32)
